# version 02
# baseline (speedup 1.0000x reference)
import numpy as np
import ml_dtypes
from contextlib import ExitStack

import concourse.bass as bass
import concourse.mybir as mybir
from concourse.bass_utils import run_bass_kernel_spmd

F32 = mybir.dt.float32
BF16 = mybir.dt.bfloat16
AF = mybir.ActivationFunctionType
ALU = mybir.AluOpType
AX = mybir.AxisListType.X
BF = ml_dtypes.bfloat16

DM = 4096
PROJ = 8512
CONVD = 4352
NH, HD, SS = 64, 64, 128
TR = 2048
TH = 2176
NCH = 16
EPS = 1e-5
P5LIM = 16
P7PART = 3

DBG = set()
STOP = None
NCORES = 8


class Sem:
    def __init__(self, ctx, nc, name):
        self.h = ctx.enter_context(nc.semaphore(name))
        self.n = 0

    def inc(self, instr, k=1):
        instr.then_inc(self.h, k)
        self.n += k
        return self.n


class K:
    pass


def _barrier(k):
    nc = k.nc
    sems = [k.ld, k.lg, k.ls, k.pe, k.dv, k.ac, k.po, k.cc, k.out] + list(k.q)
    vals = [(s.h, s.n) for s in sems if s.n]
    with nc.Block() as blk:
        for eng_name in ("sync", "tensor", "vector", "scalar", "gpsimd"):
            def mk(eng, _v=vals):
                for h, n in _v:
                    eng.wait_ge(h, n)
            getattr(blk, eng_name)(mk)


def _ln_phase(k, src_dram, dst_view, nchunks, with_rms, x1_mode=False, tag=""):
    nc = k.nc
    npc = 15 if x1_mode else 14
    ldpc = 32 if x1_mode else 16
    ld0, dv0, ac0, pe0, ls0 = k.ld.n, k.dv.n, k.ac.n, k.pe.n, k.ls.n
    q0_base = [k.q[0].n, k.q[1].n]
    q2_base = k.q[2].n
    off = 1 if x1_mode else 0

    with ExitStack() as ctx:
        e = ctx.enter_context
        xt = [e(nc.sbuf_tensor(f"xt{tag}{i}", [128, DM], F32)) for i in range(2)]
        xb = [e(nc.sbuf_tensor(f"xb{tag}{i}", [128, DM], BF16)) for i in range(2)]
        at = e(nc.sbuf_tensor(f"at{tag}", [128, DM], F32)) if x1_mode else None
        stats = e(nc.sbuf_tensor(f"lnst{tag}", [128, 48], F32))
        mv = e(nc.sbuf_tensor(f"lnmv{tag}", [128, 8], F32))

        with nc.Block() as blk:
            @blk.sync
            def _(sync):
                for i in range(nchunks):
                    if i >= 2:
                        sync.wait_ge(k.dv.h, dv0 + npc * (i - 2) + off + 10)
                        if x1_mode:
                            sync.wait_ge(k.ls.h, ls0 + 16 * (i - 1))
                    qs = k.q[i % 2]
                    sync.dma_start(out=xt[i % 2][:], in_=src_dram[i * 128:(i + 1) * 128, :]).then_inc(qs.h, 16)
                    qs.n += 16
                    if x1_mode:
                        if i >= 1:
                            sync.wait_ge(k.dv.h, dv0 + npc * (i - 1) + 1)
                        sync.dma_start(out=at[:], in_=k.attn_scr[i * 128:(i + 1) * 128, :]).then_inc(k.q[2].h, 16)
                        k.q[2].n += 16

            @blk.vector
            def _(vector):
                for i in range(nchunks):
                    sl = i % 2
                    base = dv0 + npc * i
                    vector.wait_ge(k.q[i % 2].h, q0_base[i % 2] + 16 * (i // 2 + 1))
                    if x1_mode:
                        vector.wait_ge(k.q[2].h, q2_base + 16 * (i + 1))
                        vector.tensor_add(out=xt[sl][:], in0=xt[sl][:], in1=at[:]).then_inc(k.dv.h, 1)
                        k.dv.n += 1
                    for q in range(8):
                        ins_ = vector.bn_stats(out=stats[:, q * 6:(q + 1) * 6],
                                               in_=xt[sl][:, q * 512:(q + 1) * 512])
                    ins_.then_inc(k.dv.h, 1)
                    k.dv.n += 1
                    vector.wait_ge(k.dv.h, base + off + 1)
                    def _step(ins_):
                        ins_.then_inc(k.dv.h, 1)
                        k.dv.n += 1
                        k._tmpc += 1
                        vector.wait_ge(k.dv.h, k.dv.n)
                    k._tmpc = 0
                    _step(vector.bn_aggr(out=mv[:, 0:2], in_=stats[:, 0:48]))
                    _step(vector.tensor_scalar(out=mv[:, 2:3], in0=mv[:, 1:2], scalar1=EPS, scalar2=None, op0=ALU.add))
                    _step(vector.reciprocal(out=mv[:, 3:4], in_=mv[:, 2:3]))
                    if with_rms:
                        _step(vector.tensor_tensor(out=mv[:, 4:5], in0=mv[:, 1:2], in1=mv[:, 3:4], op=ALU.mult))
                        _step(vector.tensor_scalar(out=mv[:, 4:5], in0=mv[:, 4:5], scalar1=EPS, scalar2=None, op0=ALU.add))
                        _step(vector.reciprocal(out=mv[:, 5:6], in_=mv[:, 4:5]))
                    else:
                        for _ in range(3):
                            _step(vector.engine_nop())
                    vector.engine_nop().then_inc(k.dv.h, 1)
                    k.dv.n += 1
                    vector.wait_ge(k.ac.h, ac0 + (i + 1))
                    if with_rms:
                        vector.tensor_tensor(out=mv[:, 6:7], in0=mv[:, 6:7], in1=mv[:, 7:8], op=ALU.mult).then_inc(k.dv.h, 1)
                        k.dv.n += 1
                        vector.wait_ge(k.dv.h, base + off + 9)
                    else:
                        vector.engine_nop().then_inc(k.dv.h, 1)
                        k.dv.n += 1
                    vector.tensor_scalar(out=xb[sl][:], in0=xt[sl][:], scalar1=mv[:, 0:1], scalar2=mv[:, 6:7],
                                         op0=ALU.subtract, op1=ALU.mult).then_inc(k.dv.h, 1)
                    k.dv.n += 1
                    for g in range(4):
                        vector.wait_ge(k.pe.h, pe0 + 32 * i + 8 * (g + 1))
                        pb = k.pbf[g % 2]
                        last = None
                        for j in range(8):
                            kk = g * 8 + j
                            last = vector.tensor_copy(out=dst_view(kk, i), in_=pb[:, j * 128:(j + 1) * 128])
                        last.then_inc(k.dv.h, 1)
                        k.dv.n += 1

            @blk.scalar
            def _(scalar):
                for i in range(nchunks):
                    base = dv0 + npc * i
                    scalar.wait_ge(k.dv.h, base + off + 8)
                    if with_rms:
                        scalar.activation(out=mv[:, 6:7], in_=mv[:, 3:4], func=AF.Sqrt)
                        scalar.activation(out=mv[:, 7:8], in_=mv[:, 5:6], func=AF.Sqrt).then_inc(k.ac.h, 1)
                    else:
                        scalar.activation(out=mv[:, 6:7], in_=mv[:, 3:4], func=AF.Sqrt).then_inc(k.ac.h, 1)
                    k.ac.n += 1
                    if x1_mode:
                        scalar.wait_ge(k.dv.h, base + 1)
                        scalar.dma_start(out=k.x1_scr[i * 128:(i + 1) * 128, :], in_=xt[i % 2][:]).then_inc(k.ls.h, 16)
                        k.ls.n += 16

            @blk.tensor
            def _(tensor):
                for i in range(nchunks):
                    base = dv0 + npc * i
                    tensor.wait_ge(k.dv.h, base + off + 10)
                    for g in range(4):
                        gg = 4 * i + g
                        if gg >= 2:
                            i2, g2 = (gg - 2) // 4, (gg - 2) % 4
                            tensor.wait_ge(k.dv.h, dv0 + npc * i2 + off + 10 + g2 + 1)
                        pb = k.pbf[g % 2]
                        last = None
                        for j in range(8):
                            kk = g * 8 + j
                            last = tensor.transpose(pb[:, j * 128:(j + 1) * 128],
                                                    xb[i % 2][:, kk * 128:(kk + 1) * 128], k.ident_bf[:])
                        last.then_inc(k.pe.h, 8)
                        k.pe.n += 8


def _p2_inproj(k):
    nc = k.nc
    ZT = list(range(1, 17))
    TWS = [(0, 512), (512, 512), (1024, 512), (1536, 512), (2048, 128)]
    CW = 384
    FM0 = 11
    slices = [(s * CW, CW, "z") for s in range(10)] + [(3840, 256, "z")] + \
             [(4096 + s * CW, CW, "fm") for s in range(11)] + [(8320, 192, "fm")]

    def nevac(sl):
        c0, cw, kind = sl
        return 16 if kind == "z" else ((cw + 127) // 128) * 5

    def ntile(sl):
        return nevac(sl)

    dcum = [0]
    for sl in slices:
        dcum.append(dcum[-1] + 32 + nevac(sl))
    pcum = [0]
    for sl in slices:
        pcum.append(pcum[-1] + ntile(sl))

    def cast_dv(g):
        si, kk = g // 32, g % 32
        return dcum[si] + kk + 1

    def evac_dv(si, e):
        return dcum[si] + 32 + e + 1

    def tile_pe(si, j):
        return pcum[si] + j + 1

    ld0, dv0, pe0, ls0 = k.ld.n, k.dv.n, k.pe.n, k.ls.n
    qb2 = [k.q[0].n, k.q[1].n]
    qb3 = [k.q[2].n, k.q[3].n]
    qb4 = [k.q[4].n, k.q[5].n]
    k.fm_evac_ctr = 0
    k.fm_store_ctr = 0

    with ExitStack() as ctx:
        e = ctx.enter_context
        wst = [e(nc.sbuf_tensor(f"wst{i}", [128, 384], F32)) for i in range(2)]
        wsb = [e(nc.sbuf_tensor(f"wsb{i}", [128, 32 * 384], BF16)) for i in range(2)]
        zst = [e(nc.sbuf_tensor(f"zst{i}", [128, 384], F32)) for i in range(2)]
        fst = [e(nc.sbuf_tensor(f"fst{i}", [128, 512], F32)) for i in range(2)]

        with nc.Block() as blk:
            @blk.sync
            def _(sync):
                for si, (c0, cw, kind) in enumerate(slices):
                    for kk in range(32):
                        g = si * 32 + kk
                        if g >= 2:
                            sync.wait_ge(k.dv.h, dv0 + cast_dv(g - 2))
                        sync.dma_start(out=wst[g % 2][:, 0:cw],
                                       in_=k.w_in[kk * 128:(kk + 1) * 128, c0:c0 + cw]).then_inc(k.q[g % 2].h, 16)
                        k.q[g % 2].n += 16

            @blk.vector
            def _(vector):
                for si, (c0, cw, kind) in enumerate(slices):
                    for kk in range(32):
                        g = si * 32 + kk
                        vector.wait_ge(k.q[g % 2].h, qb2[g % 2] + 16 * (g // 2 + 1))
                        if si >= 2 and kk == 0:
                            vector.wait_ge(k.pe.h, pe0 + pcum[si - 1])
                        vector.tensor_scalar(out=wsb[si % 2][:, kk * CW:kk * CW + cw],
                                             in0=wst[g % 2][:, 0:cw],
                                             scalar1=k.lnw[:, kk:kk + 1], scalar2=None,
                                             op0=ALU.mult).then_inc(k.dv.h, 1)
                        k.dv.n += 1
                    if kind == "z":
                        for j in range(16):
                            ze = si * 16 + j
                            vector.wait_ge(k.pe.h, pe0 + tile_pe(si, j))
                            if ze >= 2:
                                vector.wait_ge(k.q[2 + ze % 2].h, qb3[ze % 2] + 16 * (ze // 2))
                            pb = k.pf[j % 2]
                            vector.tensor_copy(out=zst[j % 2][:, 0:cw],
                                                 in_=pb[:, 0:cw]).then_inc(k.dv.h, 1)
                            k.dv.n += 1
                    else:
                        nmc = (cw + 127) // 128
                        for m in range(nmc):
                            mw = min(128, cw - m * 128)
                            ch = (c0 + m * 128 - 4096) // 128
                            for j, (t0, tw) in enumerate(TWS):
                                idx = m * 5 + j
                                fe = k.fm_evac_ctr; k.fm_evac_ctr += 1
                                vector.wait_ge(k.pe.h, pe0 + tile_pe(si, idx))
                                if fe >= 2:
                                    vector.wait_ge(k.q[4 + fe % 2].h, qb4[fe % 2] + 16 * (fe // 2))
                                pb = k.pf[2 + idx % 2]
                                vector.tensor_scalar(out=fst[idx % 2][0:mw, 0:tw], in0=pb[0:mw, 0:tw],
                                                     scalar1=k.mupf[0:mw, ch:ch + 1],
                                                     scalar2=None, op0=ALU.mult).then_inc(k.dv.h, 1)
                                k.dv.n += 1

            @blk.tensor
            def _(tensor):
                for si, (c0, cw, kind) in enumerate(slices):
                    tensor.wait_ge(k.dv.h, dv0 + dcum[si] + 32)
                    if kind == "z":
                        for j, tc in enumerate(ZT):
                            if si * 16 + j >= 2:
                                jj = si * 16 + j - 2
                                si2, j2 = jj // 16, jj % 16
                                tensor.wait_ge(k.dv.h, dv0 + evac_dv(si2, j2))
                            pb = k.pf[j % 2]
                            last = None
                            for kk in range(32):
                                last = tensor.matmul(pb[:, 0:cw], k.xn(kk)[:, tc * 128:(tc + 1) * 128],
                                                     wsb[si % 2][:, kk * CW:kk * CW + cw],
                                                     start=(kk == 0), stop=(kk == 31))
                            last.then_inc(k.pe.h, 1)
                            k.pe.n += 1
                    else:
                        nmc = (cw + 127) // 128
                        for m in range(nmc):
                            mw = min(128, cw - m * 128)
                            for j, (t0, tw) in enumerate(TWS):
                                idx = m * 5 + j
                                gidx = sum(ntile(slices[s2]) for s2 in range(FM0, si)) + idx
                                if gidx >= 2:
                                    pj = gidx - 2
                                    acc = 0
                                    for s2 in range(FM0, len(slices)):
                                        nt = ntile(slices[s2])
                                        if pj < acc + nt:
                                            tensor.wait_ge(k.dv.h, dv0 + evac_dv(s2, pj - acc))
                                            break
                                        acc += nt
                                pb = k.pf[2 + idx % 2]
                                last = None
                                for kk in range(32):
                                    last = tensor.matmul(pb[0:mw, 0:tw],
                                                         wsb[si % 2][:, kk * CW + m * 128: kk * CW + m * 128 + mw],
                                                         k.xn(kk)[:, t0:t0 + tw],
                                                         start=(kk == 0), stop=(kk == 31))
                                last.then_inc(k.pe.h, 1)
                                k.pe.n += 1

            @blk.scalar
            def _(scalar):
                for si, (c0, cw, kind) in enumerate(slices):
                    if kind == "z":
                        for j, tc in enumerate(ZT):
                            scalar.wait_ge(k.dv.h, dv0 + evac_dv(si, j))
                            ze2 = si * 16 + j
                            scalar.dma_start(out=k.z_scr[(tc - 1) * 128:tc * 128, c0:c0 + cw],
                                             in_=zst[j % 2][:, 0:cw]).then_inc(k.q[2 + ze2 % 2].h, 16)
                            k.q[2 + ze2 % 2].n += 16
                    else:
                        nmc = (cw + 127) // 128
                        for m in range(nmc):
                            mw = min(128, cw - m * 128)
                            ch0 = c0 + m * 128 - 4096
                            for j, (t0, tw) in enumerate(TWS):
                                idx = m * 5 + j
                                scalar.wait_ge(k.dv.h, dv0 + evac_dv(si, idx))
                                fe2 = k.fm_store_ctr; k.fm_store_ctr += 1
                                if ch0 < CONVD:
                                    scalar.dma_start(out=k.xbc_scr[ch0:ch0 + mw, t0:t0 + tw],
                                                     in_=fst[idx % 2][0:mw, 0:tw]).then_inc(k.q[4 + fe2 % 2].h, 16)
                                else:
                                    scalar.dma_start(out=k.dt_scr[0:64, t0:t0 + tw],
                                                     in_=fst[idx % 2][0:64, 0:tw]).then_inc(k.q[4 + fe2 % 2].h, 16)
                                k.q[4 + fe2 % 2].n += 16




def _p3_dt(k):
    nc = k.nc
    with ExitStack() as ctx:
        e = ctx.enter_context
        dtf = e(nc.sbuf_tensor("dtf", [64, TH], F32))
        tA = e(nc.sbuf_tensor("tA", [64, TR], F32))
        tB = e(nc.sbuf_tensor("tB", [64, TR], F32))
        sp = e(nc.sbuf_tensor("sp", [64, TR], BF16))
        ldt = dtf[:, 0:TR]
        spf = dtf[:, 0:TR]
        ld0, dv0, ac0, lg0 = k.ld.n, k.dv.n, k.ac.n, k.lg.n
        t64 = k.t64g
        k._sp_pos = []

        with nc.Block() as blk:
            @blk.sync
            def _(sync):
                sync.dma_start(out=dtf[:], in_=k.dt_scr).then_inc(k.ld.h, 16)
                k.ld.n += 16

            @blk.vector
            def _(vector):
                def dstep(ins_):
                    ins_.then_inc(k.dv.h, 1)
                    k.dv.n += 1
                    vector.wait_ge(k.dv.h, k.dv.n)
                vector.wait_ge(k.ld.h, ld0 + 16)
                dstep(vector.tensor_scalar(out=tA[:], in0=dtf[:, 128:TH], scalar1=k.dbias[:, 0:1],
                                           scalar2=None, op0=ALU.add))
                dstep(vector.tensor_scalar(out=tB[:], in0=tA[:], scalar1=0.0, scalar2=None, op0=ALU.max))
                dstep(vector.engine_nop())
                vector.wait_ge(k.ac.h, ac0 + 3)
                dstep(vector.tensor_add(out=k.dt_t[:], in0=tB[:], in1=ldt[:]))
                dstep(vector.tensor_scalar(out=tB[:], in0=k.dt_t[:], scalar1=1e-30, scalar2=None, op0=ALU.max))
                dstep(vector.tensor_scalar(out=tA[:], in0=k.dt_t[:], scalar1=k.aA[:, 0:1],
                                           scalar2=None, op0=ALU.mult))
                for c in range(NCH):
                    dstep(vector.tensor_tensor_scan(out=k.cl_t[:, c * 128:(c + 1) * 128],
                                                    data0=k.ones64[:, 0:128],
                                                    data1=tA[:, c * 128:(c + 1) * 128],
                                                    initial=0.0, op0=ALU.mult, op1=ALU.add))
                dstep(vector.memset(t64[:, 16:17], 0.0))
                for c in range(NCH):
                    dstep(vector.tensor_copy(out=t64[:, c:c + 1], in_=k.cl_t[:, c * 128 + 127:c * 128 + 128]))
                for c in range(1, NCH):
                    dstep(vector.tensor_add(out=t64[:, 16 + c:17 + c], in0=t64[:, 15 + c:16 + c],
                                            in1=t64[:, c - 1:c]))
                vector.wait_ge(k.ac.h, ac0 + 4)
                for c in range(NCH):
                    dstep(vector.tensor_scalar(out=tB[:, c * 128:(c + 1) * 128],
                                               in0=k.cl_t[:, c * 128:(c + 1) * 128],
                                               scalar1=t64[:, 16 + c:17 + c], scalar2=None, op0=ALU.add))
                vector.wait_ge(k.ac.h, ac0 + 5)
                dstep(vector.tensor_copy(out=k.egl_bf[:], in_=tA[:]))
                dstep(vector.tensor_sub(out=tB[:], in0=ldt[:], in1=k.cl_t[:]))
                nd = 0
                for which, val in enumerate((tB, k.cl_t)):
                    cur = val
                    for part in range(3):
                        if nd:
                            vector.wait_ge(k.lg.h, lg0 + 16 * nd)
                        dstep(vector.tensor_copy(out=sp[:], in_=cur[:]))
                        k._sp_pos.append(k.dv.n)
                        nd += 1
                        if part < 2:
                            dstep(vector.tensor_copy(out=spf[:], in_=sp[:]))
                            dstep(vector.tensor_sub(out=tA[:], in0=cur[:], in1=spf[:]))
                            cur = tA
                dstep(vector.tensor_copy(out=t64[:, 32:48], in_=t64[:, 0:16]))
                vector.wait_ge(k.ac.h, ac0 + 6)
                vector.wait_ge(k.lg.h, lg0 + 16 * nd)

            @blk.scalar
            def _(scalar):
                def astep(ins_):
                    ins_.then_inc(k.ac.h, 1)
                    k.ac.n += 1
                    scalar.wait_ge(k.ac.h, k.ac.n)
                scalar.wait_ge(k.dv.h, dv0 + 3)
                astep(scalar.activation(out=ldt[:], in_=tA[:], func=AF.Abs))
                astep(scalar.activation(out=tA[:], in_=ldt[:], func=AF.Exp, scale=-1.0))
                astep(scalar.activation(out=ldt[:], in_=tA[:], func=AF.Ln, bias=1.0))
                scalar.wait_ge(k.dv.h, dv0 + 5)
                astep(scalar.activation(out=ldt[:], in_=tB[:], func=AF.Ln))
                scalar.wait_ge(k.dv.h, dv0 + 70)
                astep(scalar.activation(out=tA[:], in_=tB[:], func=AF.Exp))
                scalar.wait_ge(k.dv.h, dv0 + 87)
                astep(scalar.activation(out=t64[:, 48:64], in_=t64[:, 32:48], func=AF.Exp))

            @blk.gpsimd
            def _(g):
                for di, dvpos in enumerate(k._sp_pos):
                    which, part = di // 3, di % 3
                    scr = k.vsp_scr if which == 0 else k.csp_scr
                    g.wait_ge(k.dv.h, dvpos)
                    g.dma_start(out=scr[:, part * TR:(part + 1) * TR], in_=sp[:]).then_inc(k.lg.h, 16)
                    k.lg.n += 16
                g.wait_ge(k.lg.h, k.lg.n)
    _barrier(k)


def _p3_conv(k):
    nc = k.nc
    with ExitStack() as ctx:
        e = ctx.enter_context
        cbuf = [e(nc.sbuf_tensor(f"cbuf{i}", [128, TH - 124], F32)) for i in range(2)]
        co = [e(nc.sbuf_tensor(f"co{i}", [128, TR], F32)) for i in range(2)]
        xsb = e(nc.sbuf_tensor("xsb", [128, TR], BF16))
        xts = [e(nc.sbuf_tensor(f"xts{i}", [128, TR], BF16)) for i in range(2)]
        ld0, dv0, pe0, ac0, ls0 = k.ld.n, k.dv.n, k.pe.n, k.ac.n, k.ls.n
        qb0 = [k.q[0].n, k.q[1].n]
        qb2 = [k.q[2].n, k.q[3].n]
        DPQ = 9

        with nc.Block() as blk:
            @blk.sync
            def _(sync):
                for q in range(34):
                    if q >= 2:
                        sync.wait_ge(k.dv.h, dv0 + DPQ * (q - 2) + 4)
                    sync.dma_start(out=cbuf[q % 2][:], in_=k.xbc_scr[q * 128:(q + 1) * 128, 124:TH]).then_inc(k.q[q % 2].h, 16)
                    k.q[q % 2].n += 16

            @blk.vector
            def _(vector):
                for q in range(34):
                    vector.wait_ge(k.q[q % 2].h, qb0[q % 2] + 16 * (q // 2 + 1))
                    cb, c0_, c1_ = cbuf[q % 2], co[0], co[1]
                    vector.tensor_scalar(out=c0_[:], in0=cb[:, 1:1 + TR], scalar1=k.convw[:, 4 * q:4 * q + 1],
                                         scalar2=None, op0=ALU.mult).then_inc(k.dv.h, 1)
                    k.dv.n += 1
                    vector.wait_ge(k.dv.h, k.dv.n)
                    vector.scalar_tensor_tensor(out=c1_[:], in0=cb[:, 2:2 + TR], scalar=k.convw[:, 4 * q + 1:4 * q + 2],
                                                in1=c0_[:], op0=ALU.mult, op1=ALU.add).then_inc(k.dv.h, 1)
                    k.dv.n += 1
                    vector.wait_ge(k.dv.h, k.dv.n)
                    vector.scalar_tensor_tensor(out=c0_[:], in0=cb[:, 3:3 + TR], scalar=k.convw[:, 4 * q + 2:4 * q + 3],
                                                in1=c1_[:], op0=ALU.mult, op1=ALU.add).then_inc(k.dv.h, 1)
                    k.dv.n += 1
                    vector.wait_ge(k.dv.h, k.dv.n)
                    vector.scalar_tensor_tensor(out=c1_[:], in0=cb[:, 4:4 + TR], scalar=k.convw[:, 4 * q + 3:4 * q + 4],
                                                in1=c0_[:], op0=ALU.mult, op1=ALU.add).then_inc(k.dv.h, 1)
                    k.dv.n += 1
                    vector.wait_ge(k.ac.h, ac0 + q + 1)
                    if q < 32:
                        vector.tensor_copy(out=xsb[:], in_=c0_[:]).then_inc(k.dv.h, 1)
                        k.dv.n += 1
                        for g in range(2):
                            vector.wait_ge(k.pe.h, pe0 + 16 * q + 8 * (g + 1))
                            if q >= 2:
                                vector.wait_ge(k.q[2 + q % 2].h, qb2[q % 2] + 256 * (q // 2))
                            pb = k.pbf[g % 2]
                            last = None
                            for j in range(8):
                                tc = g * 8 + j
                                last = vector.tensor_copy(out=xts[q % 2][:, tc * 128:(tc + 1) * 128],
                                                          in_=pb[:, j * 128:(j + 1) * 128])
                            last.then_inc(k.dv.h, 1)
                            k.dv.n += 1
                    elif q == 32:
                        vector.tensor_copy(out=k.bB[:], in_=c0_[:]).then_inc(k.dv.h, 1)
                        k.dv.n += 1
                        for g in range(2):
                            vector.wait_ge(k.pe.h, pe0 + 16 * 32 + 8 * (g + 1))
                            pb = k.pbf[g % 2]
                            last = None
                            for j in range(8):
                                tc = g * 8 + j
                                last = vector.tensor_copy(out=k.bBt[:, tc * 128:(tc + 1) * 128],
                                                          in_=pb[:, j * 128:(j + 1) * 128])
                            last.then_inc(k.dv.h, 1)
                            k.dv.n += 1
                    else:
                        vector.tensor_copy(out=k.bC[:], in_=c0_[:]).then_inc(k.dv.h, 1)
                        k.dv.n += 1
                        for _ in range(2):
                            vector.engine_nop().then_inc(k.dv.h, 1)
                            k.dv.n += 1
                    for _ in range(2):
                        vector.engine_nop().then_inc(k.dv.h, 1)
                        k.dv.n += 1

            @blk.scalar
            def _(scalar):
                for q in range(34):
                    scalar.wait_ge(k.dv.h, dv0 + DPQ * q + 4)
                    scalar.activation(out=co[0][:], in_=co[1][:], func=AF.Silu).then_inc(k.ac.h, 1)
                    k.ac.n += 1
                    if q < 32:
                        for g in range(2):
                            scalar.wait_ge(k.dv.h, dv0 + DPQ * q + 5 + g + 1)
                            for j in range(8):
                                tc = g * 8 + j
                                scalar.dma_start(out=k.xtm_scr[tc * 128:(tc + 1) * 128, q * 128:(q + 1) * 128],
                                                 in_=xts[q % 2][:, tc * 128:(tc + 1) * 128]).then_inc(k.q[2 + q % 2].h, 16)
                                k.q[2 + q % 2].n += 16

            @blk.tensor
            def _(tensor):
                for q in range(33):
                    src2 = xsb if q < 32 else k.bB
                    tensor.wait_ge(k.dv.h, dv0 + DPQ * q + 5)
                    for g in range(2):
                        gg = 2 * q + g
                        if gg >= 2:
                            q2, g2 = (gg - 2) // 2, (gg - 2) % 2
                            tensor.wait_ge(k.dv.h, dv0 + DPQ * q2 + 5 + g2 + 1)
                        pb = k.pbf[g % 2]
                        last = None
                        for j in range(8):
                            tc = g * 8 + j
                            last = tensor.transpose(pb[:, j * 128:(j + 1) * 128],
                                                    src2[:, tc * 128:(tc + 1) * 128], k.ident_bf[:])
                        last.then_inc(k.pe.h, 8)
                        k.pe.n += 8
    _barrier(k)




def _p4_ssd(k):
    nc = k.nc
    with ExitStack() as ctx:
        e = ctx.enter_context
        xtt = [e(nc.sbuf_tensor(f"xtt{i}", [128, DM], BF16)) for i in range(2)]
        m4b = e(nc.sbuf_tensor("m4b", [128, 512], BF16))
        combL = e(nc.sbuf_tensor("combL", [6, 4096], BF16))
        combR = e(nc.sbuf_tensor("combR", [6, 4096], BF16))
        G = e(nc.sbuf_tensor("G", [128, 8192], BF16))
        estg = e(nc.sbuf_tensor("estg", [128, 1024], F32))
        e2 = e(nc.sbuf_tensor("e2", [128, 1024], F32))
        CBr = e(nc.sbuf_tensor("CBr", [128, 512], F32))
        eclT = e(nc.sbuf_tensor("eclT", [64, 128], BF16))
        wfT = e(nc.sbuf_tensor("wfT", [64, 128], BF16))
        gmT = e(nc.sbuf_tensor("gmT", [64, 128], BF16))
        tmpF = e(nc.sbuf_tensor("tmpF", [64, 128], F32))
        tmpG = e(nc.sbuf_tensor("tmpG", [64, 128], F32))
        tmpG2 = e(nc.sbuf_tensor("tmpG2", [64, 128], F32))
        tmpF2 = e(nc.sbuf_tensor("tmpF2", [64, 128], F32))
        eclb = e(nc.sbuf_tensor("eclb", [128, 1024], BF16))
        hbf = e(nc.sbuf_tensor("hbf", [128, 4096], BF16))
        wx = e(nc.sbuf_tensor("wx", [128, 1024], BF16))
        yst = e(nc.sbuf_tensor("yst", [128, 512], F32))
        yst2 = e(nc.sbuf_tensor("yst2", [128, 1024], F32))
        hmt = e(nc.sbuf_tensor("hmt", [128, 512], F32))
        k.H = e(nc.sbuf_tensor("H", [128, DM], F32))

        with nc.Block() as blk0:
            @blk0.vector
            def _(vector):
                vector.memset(k.H[:], 0.0)
                vector.memset(combL[:], 1.0)
                vector.memset(combR[:], 1.0).then_inc(k.dv.h, 1)
                k.dv.n += 1

            @blk0.gpsimd
            def _(g):
                g.dma_start(out=m4b[:], in_=k.m4_d).then_inc(k.lg.h, 16)
                k.lg.n += 16
        _barrier(k)

        ld0, lg0, ls0, dv0, pe0, ac0 = k.ld.n, k.lg.n, k.ls.n, k.dv.n, k.pe.n, k.ac.n
        qb0 = [k.q[0].n, k.q[1].n]
        qb2 = [k.q[6].n, k.q[7].n]
        qb4 = [k.q[4].n, k.q[5].n]
        DV, PE_, AC = 112, 65, 18

        def dvC(c, p):
            return dv0 + DV * c + p

        def peC(c, p):
            return pe0 + PE_ * c + p

        def acC(c, p):
            return ac0 + AC * c + p

        pe_cb = 1
        pe_dg = lambda i: 2 + i
        pe_eclB = lambda j: 18 + 3 * j
        pe_intra = lambda j: 19 + 3 * j
        pe_raw = lambda j: 20 + 3 * j
        pe_wrep = lambda s: 42 + 3 * s
        pe_smm = lambda s: 43 + 3 * s
        pe_gmm = lambda s: 44 + 3 * s
        dv_eclT, dv_tsG, dv_wmul, dv_wfT, dv_gmT = 1, 2, 3, 4, 5
        dv_hbf = lambda j: 6 + j
        dv_cbr = 14
        dv_em = lambda i: 17 + 3 * i
        dv_g = lambda i: 18 + 3 * i
        dv_diag = lambda i: 19 + 3 * i
        dv_eclb = lambda j: 65 + 3 * j
        dv_t1 = lambda j: 66 + 3 * j
        dv_y = lambda j: 67 + 3 * j
        dv_wx = lambda s: 89 + 3 * s
        dv_hm = lambda s: 90 + 3 * s
        dv_hadd = lambda s: 91 + 3 * s
        ac_e1, ac_e3 = 1, 2
        ac_exp = lambda i: 3 + i

        with nc.Block() as blk:
            @blk.sync
            def _(sync):
                for c in range(NCH):
                    if c >= 2:
                        sync.wait_ge(k.dv.h, dvC(c - 2, dv_wx(7)))
                        sync.wait_ge(k.pe.h, peC(c - 2, pe_intra(7)))
                    sync.dma_start(out=xtt[c % 2][:], in_=k.xtm_scr[c * 128:(c + 1) * 128, :]).then_inc(k.q[c % 2].h, 16)
                    k.q[c % 2].n += 16

            @blk.gpsimd
            def _(g):
                for c in range(NCH):
                    for r in range(4):
                        gr = 4 * c + r
                        if gr >= 2:
                            c2, r2 = (gr - 2) // 4, (gr - 2) % 4
                            g.wait_ge(k.pe.h, peC(c2, pe_dg(4 * r2 + 3)))
                        sl = r % 2
                        hb = 16 * r
                        g.dma_start(
                            out=combL[0:3, sl * 2048:(sl + 1) * 2048].rearrange("s (r t) -> s r t", r=16),
                            in_=k.vsp_scr[hb:hb + 16, :].rearrange("r (s ch t) -> s ch r t", s=3, ch=NCH)[:, c],
                        ).then_inc(k.q[6 + gr % 2].h, 16)
                        k.q[6 + gr % 2].n += 16
                        g.dma_start(
                            out=combR[3:6, sl * 2048:(sl + 1) * 2048].rearrange("s (r t) -> s r t", r=16),
                            in_=k.csp_scr[hb:hb + 16, :].rearrange("r (s ch t) -> s ch r t", s=3, ch=NCH)[:, c],
                        ).then_inc(k.q[6 + gr % 2].h, 16)
                        k.q[6 + gr % 2].n += 16
                if "G" in DBG:
                    g.wait_ge(k.dv.h, dvC(NCH - 1, dv_diag(15)))
                    g.dma_start(out=k.dbgG, in_=G[:]).then_inc(k.po.h, 16)
                    k.po.n += 16
                    g.wait_ge(k.dv.h, dvC(NCH - 1, dv_cbr + 2))
                    g.dma_start(out=k.dbgCB, in_=CBr[:]).then_inc(k.po.h, 16)
                    k.po.n += 16
                    g.wait_ge(k.po.h, k.po.n)

            @blk.vector
            def _(vector):
                for c in range(NCH):
                    base = dv0 + DV * c

                    def dput(ins_, pos):
                        ins_.then_inc(k.dv.h, 1)
                        k.dv.n += 1
                        assert k.dv.n == base + pos, (c, pos, k.dv.n - base)

                    vector.wait_ge(k.q[c % 2].h, qb0[c % 2] + 16 * (c // 2 + 1))
                    vector.wait_ge(k.ac.h, acC(c, ac_e1))
                    dput(vector.tensor_copy(out=eclT[:], in_=tmpF[:]), dv_eclT)
                    dput(vector.tensor_scalar(out=tmpG[:], in0=k.cl_t[:, c * 128:(c + 1) * 128],
                                              scalar1=-1.0, scalar2=k.t64g[:, c:c + 1],
                                              op0=ALU.mult, op1=ALU.add), dv_tsG)
                    vector.wait_ge(k.dv.h, base + dv_tsG)
                    vector.wait_ge(k.ac.h, acC(c, ac_e3))
                    dput(vector.tensor_tensor(out=tmpF2[:], in0=tmpG2[:],
                                              in1=k.dt_t[:, c * 128:(c + 1) * 128], op=ALU.mult), dv_wmul)
                    vector.wait_ge(k.dv.h, base + dv_wmul)
                    dput(vector.tensor_copy(out=wfT[:], in_=tmpF2[:]), dv_wfT)
                    dput(vector.tensor_scalar(out=gmT[:], in0=k.ones64[:, 0:128],
                                              scalar1=k.t64g[:, 48 + c:49 + c], scalar2=None,
                                              op0=ALU.mult), dv_gmT)
                    for j in range(8):
                        dput(vector.tensor_copy(out=hbf[:, j * 512:(j + 1) * 512],
                                                in_=k.H[:, j * 512:(j + 1) * 512]), dv_hbf(j))
                    vector.wait_ge(k.pe.h, peC(c, pe_cb))
                    dput(vector.tensor_copy(out=CBr[:, 0:128], in_=k.pf[4][:, 0:128]), dv_cbr)
                    vector.wait_ge(k.dv.h, base + dv_cbr)
                    dput(vector.tensor_copy(out=CBr[:, 128:256], in_=CBr[:, 0:128]), dv_cbr + 1)
                    vector.wait_ge(k.dv.h, base + dv_cbr + 1)
                    dput(vector.tensor_copy(out=CBr[:, 256:512], in_=CBr[:, 0:256]), dv_cbr + 2)
                    for i in range(16):
                        if c > 0 or i >= 2:
                            tgt = acC(c, ac_exp(i - 2)) if i >= 2 else acC(c - 1, ac_exp(14 + i))
                            vector.wait_ge(k.ac.h, tgt)
                        vector.wait_ge(k.pe.h, peC(c, pe_dg(i)))
                        pfD = k.pf[i % 2]
                        dput(vector.tensor_tensor(out=estg[:, (i % 2) * 512:(i % 2 + 1) * 512],
                                                  in0=pfD[:], in1=m4b[:], op=ALU.add), dv_em(i))
                        vector.wait_ge(k.ac.h, acC(c, ac_exp(i)))
                        if c > 0:
                            vector.wait_ge(k.pe.h, peC(c - 1, pe_intra(i // 2)))
                        dput(vector.tensor_tensor(out=G[:, i * 512:(i + 1) * 512],
                                                  in0=e2[:, (i % 2) * 512:(i % 2 + 1) * 512],
                                                  in1=CBr[:], op=ALU.mult), dv_g(i))
                        vector.wait_ge(k.dv.h, base + dv_g(i))
                        last = None
                        for hh in range(4):
                            head = 4 * i + hh
                            col = head * 128
                            last = vector.scalar_tensor_tensor(out=G[:, col:col + 128], in0=k.ident_bf[:],
                                                               scalar=k.dbc[:, head:head + 1],
                                                               in1=G[:, col:col + 128],
                                                               op0=ALU.mult, op1=ALU.add)
                        dput(last, dv_diag(i))
                        vector.wait_ge(k.dv.h, base + dv_diag(i))
                    for j in range(8):
                        vector.wait_ge(k.pe.h, peC(c, pe_eclB(j)))
                        pfe = k.pf[4 + (j % 2)]
                        dput(vector.tensor_copy(out=eclb[:, (j % 2) * 512:(j % 2 + 1) * 512],
                                                in_=pfe[:]), dv_eclb(j))
                        vector.wait_ge(k.dv.h, base + dv_eclb(j))
                        vector.wait_ge(k.pe.h, peC(c, pe_raw(j)))
                        pfr = k.pf[5 - (j % 2)]
                        dput(vector.tensor_tensor(out=yst[:], in0=pfr[:],
                                                  in1=eclb[:, (j % 2) * 512:(j % 2 + 1) * 512],
                                                  op=ALU.mult), dv_t1(j))
                        vector.wait_ge(k.dv.h, base + dv_t1(j))
                        jj = 8 * c + j
                        if jj >= 2:
                            vector.wait_ge(k.q[4 + jj % 2].h, qb4[jj % 2] + 16 * (jj // 2))
                        pfy = k.pf[2 + (j % 2)]
                        dput(vector.tensor_add(out=yst2[:, (j % 2) * 512:(j % 2 + 1) * 512],
                                               in0=yst[:], in1=pfy[:]), dv_y(j))
                    for s in range(8):
                        vector.wait_ge(k.pe.h, peC(c, pe_wrep(s)))
                        pfw = k.pf[4 + (s % 2)]
                        dput(vector.tensor_tensor(out=wx[:, (s % 2) * 512:(s % 2 + 1) * 512],
                                                  in0=xtt[c % 2][:, s * 512:(s + 1) * 512],
                                                  in1=pfw[:], op=ALU.mult), dv_wx(s))
                        vector.wait_ge(k.pe.h, peC(c, pe_gmm(s)))
                        pfg = k.pf[2 + (s % 2)]
                        dput(vector.tensor_tensor(out=hmt[:], in0=k.H[:, s * 512:(s + 1) * 512],
                                                  in1=pfg[:], op=ALU.mult), dv_hm(s))
                        vector.wait_ge(k.dv.h, base + dv_hm(s))
                        vector.wait_ge(k.pe.h, peC(c, pe_smm(s)))
                        pfs = k.pf[5 - (s % 2)]
                        dput(vector.tensor_add(out=k.H[:, s * 512:(s + 1) * 512],
                                               in0=hmt[:], in1=pfs[:]), dv_hadd(s))
                        vector.wait_ge(k.dv.h, base + dv_hadd(s))

            @blk.scalar
            def _(scalar):
                for c in range(NCH):
                    if c > 0:
                        scalar.wait_ge(k.dv.h, dvC(c - 1, dv_eclT))
                    scalar.activation(out=tmpF[:], in_=k.cl_t[:, c * 128:(c + 1) * 128],
                                      func=AF.Exp).then_inc(k.ac.h, 1)
                    k.ac.n += 1
                    scalar.wait_ge(k.dv.h, dvC(c, dv_tsG))
                    scalar.activation(out=tmpG2[:], in_=tmpG[:], func=AF.Exp).then_inc(k.ac.h, 1)
                    k.ac.n += 1
                    for i in range(16):
                        scalar.wait_ge(k.dv.h, dvC(c, dv_em(i)))
                        scalar.activation(out=e2[:, (i % 2) * 512:(i % 2 + 1) * 512],
                                          in_=estg[:, (i % 2) * 512:(i % 2 + 1) * 512],
                                          func=AF.Exp).then_inc(k.ac.h, 1)
                        k.ac.n += 1
                    for j in range(8):
                        jj = 8 * c + j
                        scalar.wait_ge(k.dv.h, dvC(c, dv_y(j)))
                        scalar.dma_start(out=k.y_scr[c * 128:(c + 1) * 128, j * 512:(j + 1) * 512],
                                         in_=yst2[:, (j % 2) * 512:(j % 2 + 1) * 512]).then_inc(k.q[4 + jj % 2].h, 16)
                        k.q[4 + jj % 2].n += 16

            @blk.tensor
            def _(tensor):
                bankw = {i: 0 for i in range(6)}

                def bwait(tensor, b, consumer_dv):
                    if bankw[b]:
                        tensor.wait_ge(k.dv.h, bankw[b])
                    bankw[b] = consumer_dv

                for c in range(NCH):
                    base_pe = pe0 + PE_ * c

                    def pput(ins_, pos):
                        ins_.then_inc(k.pe.h, 1)
                        k.pe.n += 1
                        assert k.pe.n == base_pe + pos, (c, pos, k.pe.n - base_pe)

                    bwait(tensor, 4, dvC(c, dv_cbr))
                    pput(tensor.matmul(k.pf[4][:, 0:128], k.bB[:, c * 128:(c + 1) * 128],
                                       k.bC[:, c * 128:(c + 1) * 128], start=True, stop=True), pe_cb)
                    tensor.wait_ge(k.q[c % 2].h, qb0[c % 2] + 16 * (c // 2 + 1))
                    for i in range(16):
                        r = i // 4
                        gr = 4 * c + r
                        tensor.wait_ge(k.q[6 + gr % 2].h, qb2[gr % 2] + 32 * (gr // 2 + 1))
                        bwait(tensor, i % 2, dvC(c, dv_em(i)))
                        pfD = k.pf[i % 2]
                        sl = r % 2
                        last = None
                        for hh in range(4):
                            hl = (i % 4) * 4 + hh
                            last = tensor.matmul(pfD[:, hh * 128:(hh + 1) * 128],
                                                 combL[0:6, sl * 2048 + hl * 128: sl * 2048 + (hl + 1) * 128],
                                                 combR[0:6, sl * 2048 + hl * 128: sl * 2048 + (hl + 1) * 128],
                                                 start=(hh == 0), stop=(hh == 3))
                        pput(last, pe_dg(i))
                    for j in range(8):
                        be = 4 + (j % 2)
                        bwait(tensor, be, dvC(c, dv_eclb(j)))
                        tensor.wait_ge(k.dv.h, dvC(c, dv_eclT))
                        pput(tensor.matmul(k.pf[be][:], eclT[:],
                                           k.r64[:, j * 512:(j + 1) * 512], start=True, stop=True), pe_eclB(j))
                        by = 2 + (j % 2)
                        bwait(tensor, by, dvC(c, dv_y(j)))
                        i_hi = 2 * j + 1
                        tensor.wait_ge(k.dv.h, dvC(c, dv_diag(i_hi)))
                        last = None
                        for jj2 in range(8):
                            head = 8 * j + jj2
                            gcol = head * 128
                            last = tensor.matmul(k.pf[by][:, jj2 * 64:(jj2 + 1) * 64],
                                                 G[:, gcol:gcol + 128],
                                                 xtt[c % 2][:, head * 64:(head + 1) * 64],
                                                 start=(jj2 == 0), stop=(jj2 == 7))
                        pput(last, pe_intra(j))
                        br = 5 - (j % 2)
                        bwait(tensor, br, dvC(c, dv_t1(j)))
                        tensor.wait_ge(k.dv.h, dvC(c, dv_hbf(j)))
                        pput(tensor.matmul(k.pf[br][:], k.bC[:, c * 128:(c + 1) * 128],
                                           hbf[:, j * 512:(j + 1) * 512], start=True, stop=True), pe_raw(j))
                    for s in range(8):
                        bw = 4 + (s % 2)
                        bwait(tensor, bw, dvC(c, dv_wx(s)))
                        tensor.wait_ge(k.dv.h, dvC(c, dv_wfT))
                        pput(tensor.matmul(k.pf[bw][:], wfT[:], k.r64[:, s * 512:(s + 1) * 512],
                                           start=True, stop=True), pe_wrep(s))
                        bs = 5 - (s % 2)
                        bwait(tensor, bs, dvC(c, dv_hadd(s)))
                        tensor.wait_ge(k.dv.h, dvC(c, dv_wx(s)))
                        pput(tensor.matmul(k.pf[bs][:], k.bBt[:, c * 128:(c + 1) * 128],
                                           wx[:, (s % 2) * 512:(s % 2 + 1) * 512],
                                           start=True, stop=True), pe_smm(s))
                        bg = 2 + (s % 2)
                        bwait(tensor, bg, dvC(c, dv_hm(s)))
                        tensor.wait_ge(k.dv.h, dvC(c, dv_gmT))
                        pput(tensor.matmul(k.pf[bg][:], gmT[:], k.r64[:, s * 512:(s + 1) * 512],
                                           start=True, stop=True), pe_gmm(s))
    _barrier(k)

    with nc.Block() as blk:
        @blk.scalar
        def _(scalar):
            scalar.dma_start(out=k.cc_in[:, :], in_=k.H[:]).then_inc(k.ls.h, 16)
            k.ls.n += 16

        @blk.gpsimd
        def _(g):
            g.wait_ge(k.ls.h, k.ls.n)
            g.collective_compute("AllGather", ALU.bypass,
                                 replica_groups=[[i, i + 1] for i in range(0, NCORES, 2)],
                                 ins=[k.cc_in.ap().opt()], outs=[k.cc_out.ap().opt()]).then_inc(k.cc.h, 1)
            k.cc.n += 1
            g.wait_ge(k.cc.h, k.cc.n)
    _barrier(k)



def _p5_gate(k):
    nc = k.nc
    with ExitStack() as ctx:
        e = ctx.enter_context
        hbf5 = e(nc.sbuf_tensor("hbf5", [128, DM], BF16))
        ytile = e(nc.sbuf_tensor("ytile", [128, DM], F32))
        ztile = e(nc.sbuf_tensor("ztile", [128, DM], F32))
        mupz5 = e(nc.sbuf_tensor("mupz5", [128, DM], BF16))
        eglb5 = e(nc.sbuf_tensor("eglb5", [128, 512], BF16))
        yst5 = e(nc.sbuf_tensor("yst5", [128, 512], F32))
        sq5 = e(nc.sbuf_tensor("sq5", [128, 512], F32))
        acc5 = e(nc.sbuf_tensor("acc5", [128, 16], F32))

        with nc.Block() as blk0:
            lg00 = k.lg.n

            @blk0.gpsimd
            def _(g):
                g.wait_ge(k.cc.h, k.cc.n)
                g.dma_start(out=mupz5[:], in_=k.mupz).then_inc(k.lg.h, 16)
                k.lg.n += 16
                for sl in range(8):
                    if sl >= 1:
                        g.wait_ge(k.dv.h, k.dv.n + 2 * sl - 1)
                    g.dma_start(out=sq5[:], in_=k.cc_out[0:128, sl * 512:(sl + 1) * 512]).then_inc(k.lg.h, 16)
                    k.lg.n += 16

            @blk0.vector
            def _(vector):
                for sl in range(8):
                    vector.wait_ge(k.lg.h, lg00 + 16 * (sl + 2))
                    vector.tensor_scalar(out=yst5[:], in0=sq5[:], scalar1=k.hm[:, 0:1],
                                         scalar2=None, op0=ALU.mult).then_inc(k.dv.h, 1)
                    k.dv.n += 1
                    vector.wait_ge(k.dv.h, k.dv.n)
                    vector.tensor_copy(out=hbf5[:, sl * 512:(sl + 1) * 512], in_=yst5[:]).then_inc(k.dv.h, 1)
                    k.dv.n += 1
        _barrier(k)
        if STOP == "P5p":
            return

        ld0, dv0, pe0, ac0 = k.ld.n, k.dv.n, k.pe.n, k.ac.n
        DV5, PE5, AC5 = 45, 24, 17
        dv_eglb = lambda j: 1 + 3 * j
        dv_t15 = lambda j: 2 + 3 * j
        dv_ya = lambda j: 3 + 3 * j
        dv_zm = 25
        dv_gm = lambda s: 26 + s
        dv_red, dv_ts, dv_rcp, dv_yn = 34, 35, 36, 37
        dv_ev = lambda g: 38 + g
        pe_egl5 = lambda j: 1 + 2 * j
        pe_cor5 = lambda j: 2 + 2 * j
        pe_tr5 = lambda g: 17 + g
        ac_sil = lambda s: 1 + s
        ac_sq = lambda s: 9 + s
        ac_rst = 17

        def dvC(c, p):
            return dv0 + DV5 * c + p

        def peC(c, p):
            return pe0 + PE5 * c + p

        def acC(c, p):
            return ac0 + AC5 * c + p

        with nc.Block() as blk:
            @blk.sync
            def _(sync):
                for c in range(P5LIM):
                    if c >= 1:
                        sync.wait_ge(k.pe.h, peC(c - 1, pe_tr5(7)))
                        sync.wait_ge(k.dv.h, dvC(c - 1, dv_gm(7)))
                    sync.dma_start(out=ytile[:], in_=k.y_scr[c * 128:(c + 1) * 128, :]).then_inc(k.ld.h, 16)
                    k.ld.n += 16
                    sync.dma_start(out=ztile[:], in_=k.z_scr[c * 128:(c + 1) * 128, :]).then_inc(k.ld.h, 16)
                    k.ld.n += 16

            @blk.tensor
            def _(tensor):
                bankw = {i: 0 for i in range(6)}

                def bwait(b, cons):
                    if bankw[b]:
                        tensor.wait_ge(k.dv.h, bankw[b])
                    bankw[b] = cons

                for c in range(P5LIM):
                    base_pe = pe0 + PE5 * c

                    def pput(ins_, pos):
                        ins_.then_inc(k.pe.h, 1)
                        k.pe.n += 1
                        assert k.pe.n == base_pe + pos, (c, pos, k.pe.n - base_pe)

                    for j in range(8):
                        be = 4 + (j % 2)
                        bwait(be, dvC(c, dv_eglb(j)))
                        pput(tensor.matmul(k.pf[be][:], k.egl_bf[:, c * 128:(c + 1) * 128],
                                           k.r64[:, j * 512:(j + 1) * 512], start=True, stop=True), pe_egl5(j))
                        by = 2 + (j % 2)
                        bwait(by, dvC(c, dv_t15(j)))
                        pput(tensor.matmul(k.pf[by][:], k.bC[:, c * 128:(c + 1) * 128],
                                           hbf5[:, j * 512:(j + 1) * 512], start=True, stop=True), pe_cor5(j))
                    tensor.wait_ge(k.dv.h, dvC(c, dv_yn))
                    for g in range(8):
                        b = g % 2
                        bwait(b, dvC(c, dv_ev(g)))
                        last = None
                        for j in range(4):
                            kk = g * 4 + j
                            last = tensor.transpose(k.pf[b][:, j * 128:(j + 1) * 128],
                                                    ytile[:, kk * 128:(kk + 1) * 128], k.ident_f[:])
                        pput(last, pe_tr5(g))

            @blk.vector
            def _(vector):
                for c in range(P5LIM):
                    base = dv0 + DV5 * c

                    def dput(ins_, pos):
                        ins_.then_inc(k.dv.h, 1)
                        k.dv.n += 1
                        assert k.dv.n == base + pos, (c, pos, k.dv.n - base)

                    vector.wait_ge(k.ld.h, ld0 + 32 * (c + 1))
                    for j in range(8):
                        vector.wait_ge(k.pe.h, peC(c, pe_egl5(j)))
                        dput(vector.tensor_copy(out=eglb5[:], in_=k.pf[4 + (j % 2)][:]), dv_eglb(j))
                        vector.wait_ge(k.dv.h, base + dv_eglb(j))
                        vector.wait_ge(k.pe.h, peC(c, pe_cor5(j)))
                        dput(vector.tensor_tensor(out=yst5[:], in0=k.pf[2 + (j % 2)][:],
                                                  in1=eglb5[:], op=ALU.mult), dv_t15(j))
                        vector.wait_ge(k.dv.h, base + dv_t15(j))
                        dput(vector.tensor_add(out=ytile[:, j * 512:(j + 1) * 512],
                                               in0=yst5[:], in1=ytile[:, j * 512:(j + 1) * 512]), dv_ya(j))
                    dput(vector.tensor_tensor(out=ztile[:], in0=ztile[:], in1=mupz5[:], op=ALU.mult), dv_zm)
                    vector.wait_ge(k.dv.h, base + dv_zm)
                    for s in range(8):
                        vector.wait_ge(k.ac.h, acC(c, ac_sil(s)))
                        dput(vector.tensor_tensor(out=ytile[:, s * 512:(s + 1) * 512],
                                                  in0=ytile[:, s * 512:(s + 1) * 512],
                                                  in1=sq5[:] if False else ztile[:, s * 512:(s + 1) * 512],
                                                  op=ALU.mult), dv_gm(s))
                        vector.wait_ge(k.dv.h, base + dv_gm(s))
                    vector.wait_ge(k.ac.h, acC(c, ac_sq(7)))
                    dput(vector.reduce_sum(out=acc5[:, 8:9], in_=acc5[:, 0:8], axis=AX), dv_red)
                    vector.wait_ge(k.dv.h, base + dv_red)
                    dput(vector.tensor_scalar(out=acc5[:, 9:10], in0=acc5[:, 8:9],
                                              scalar1=1.0 / DM, scalar2=EPS, op0=ALU.mult, op1=ALU.add), dv_ts)
                    vector.wait_ge(k.dv.h, base + dv_ts)
                    dput(vector.reciprocal(out=acc5[:, 10:11], in_=acc5[:, 9:10]), dv_rcp)
                    vector.wait_ge(k.ac.h, acC(c, ac_rst))
                    dput(vector.tensor_scalar(out=ytile[:], in0=ytile[:], scalar1=acc5[:, 11:12],
                                              scalar2=None, op0=ALU.mult), dv_yn)
                    for g in range(8):
                        vector.wait_ge(k.pe.h, peC(c, pe_tr5(g)))
                        last = None
                        for j in range(4):
                            kk = g * 4 + j
                            last = vector.tensor_copy(out=k.yfm(kk)[:, c * 128:(c + 1) * 128],
                                                      in_=k.pf[g % 2][:, j * 128:(j + 1) * 128])
                        dput(last, dv_ev(g))

            @blk.scalar
            def _(scalar):
                for c in range(P5LIM):
                    for s in range(8):
                        scalar.wait_ge(k.dv.h, dvC(c, dv_zm))
                        scalar.activation(out=ztile[:, s * 512:(s + 1) * 512],
                                          in_=ztile[:, s * 512:(s + 1) * 512],
                                          func=AF.Silu).then_inc(k.ac.h, 1)
                        k.ac.n += 1
                    for s in range(8):
                        scalar.wait_ge(k.dv.h, dvC(c, dv_gm(s)))
                        scalar.activation(out=sq5[:], in_=ytile[:, s * 512:(s + 1) * 512],
                                          func=AF.Square,
                                          accum_out=acc5[:, s:s + 1]).then_inc(k.ac.h, 1)
                        k.ac.n += 1
                    scalar.wait_ge(k.dv.h, dvC(c, dv_rcp))
                    scalar.activation(out=acc5[:, 11:12], in_=acc5[:, 10:11],
                                      func=AF.Sqrt).then_inc(k.ac.h, 1)
                    k.ac.n += 1
    _barrier(k)


def _p6_outproj(k):
    nc = k.nc
    with ExitStack() as ctx:
        e = ctx.enter_context
        wst = [e(nc.sbuf_tensor(f"wst6{i}", [128, 512], F32)) for i in range(2)]
        wsb = [e(nc.sbuf_tensor(f"wsb6{i}", [128, 32 * 512], BF16)) for i in range(2)]
        ast = [e(nc.sbuf_tensor(f"ast{i}", [128, 512], F32)) for i in range(2)]
        ld0, dv0, pe0, ls0 = k.ld.n, k.dv.n, k.pe.n, k.ls.n
        qb0 = [k.q[0].n, k.q[1].n]
        qb2 = [k.q[2].n, k.q[3].n]
        NT = 16

        def cast_dv(g):
            si, kk = g // 32, g % 32
            return dv0 + si * 48 + kk + 1

        def evac_dv(si, j):
            return dv0 + si * 48 + 32 + j + 1

        def tile_pe(si, j):
            return pe0 + si * NT + j + 1

        with nc.Block() as blk:
            @blk.sync
            def _(sync):
                for si in range(8):
                    for kk in range(32):
                        g = si * 32 + kk
                        if g >= 2:
                            sync.wait_ge(k.dv.h, cast_dv(g - 2))
                        sync.dma_start(out=wst[g % 2][:],
                                       in_=k.w_op[kk * 128:(kk + 1) * 128, si * 512:(si + 1) * 512]).then_inc(k.q[g % 2].h, 16)
                        k.q[g % 2].n += 16

            @blk.vector
            def _(vector):
                for si in range(8):
                    for kk in range(32):
                        g = si * 32 + kk
                        vector.wait_ge(k.q[g % 2].h, qb0[g % 2] + 16 * (g // 2 + 1))
                        if si >= 2 and kk == 0:
                            vector.wait_ge(k.pe.h, tile_pe(si - 2, NT - 1))
                        vector.tensor_scalar(out=wsb[si % 2][:, kk * 512:(kk + 1) * 512],
                                             in0=wst[g % 2][:],
                                             scalar1=k.nrmw[:, kk:kk + 1], scalar2=None,
                                             op0=ALU.mult).then_inc(k.dv.h, 1)
                        k.dv.n += 1
                    for j in range(NT):
                        ej = si * NT + j
                        vector.wait_ge(k.pe.h, tile_pe(si, j))
                        if ej >= 2:
                            vector.wait_ge(k.q[2 + ej % 2].h, qb2[ej % 2] + 16 * (ej // 2))
                        vector.tensor_copy(out=ast[j % 2][:], in_=k.pf[j % 2][:]).then_inc(k.dv.h, 1)
                        k.dv.n += 1

            @blk.tensor
            def _(tensor):
                for si in range(8):
                    tensor.wait_ge(k.dv.h, dv0 + si * 48 + 32)
                    for j in range(NT):
                        if si * NT + j >= 2:
                            jj = si * NT + j - 2
                            tensor.wait_ge(k.dv.h, evac_dv(jj // NT, jj % NT))
                        pb = k.pf[j % 2]
                        last = None
                        for kk in range(32):
                            last = tensor.matmul(pb[:], k.yfm(kk)[:, j * 128:(j + 1) * 128],
                                                 wsb[si % 2][:, kk * 512:(kk + 1) * 512],
                                                 start=(kk == 0), stop=(kk == 31))
                        last.then_inc(k.pe.h, 1)
                        k.pe.n += 1

            @blk.scalar
            def _(scalar):
                for si in range(8):
                    for j in range(NT):
                        ej = si * NT + j
                        scalar.wait_ge(k.dv.h, evac_dv(si, j))
                        scalar.dma_start(out=k.attn_scr[j * 128:(j + 1) * 128, si * 512:(si + 1) * 512],
                                         in_=ast[j % 2][:]).then_inc(k.q[2 + ej % 2].h, 16)
                        k.q[2 + ej % 2].n += 16
    _barrier(k)



def _p7_router(k):
    nc = k.nc
    with ExitStack() as ctx:
        e = ctx.enter_context
        atb = e(nc.sbuf_tensor("at7b", [128, DM], BF16))
        wrtb = e(nc.sbuf_tensor("wrtb", [128, 256], BF16))
        w8tb = e(nc.sbuf_tensor("w8tb", [8, TR], BF16))
        lg8 = e(nc.sbuf_tensor("lg8", [128, 48], F32))
        wr_sb = e(nc.sbuf_tensor("wr_sb", [128, 512], F32))
        wst7 = [e(nc.sbuf_tensor(f"wst7{i}", [128, 512], F32)) for i in range(2)]
        ld0, dv0, pe0, ac0 = k.ld.n, k.dv.n, k.pe.n, k.ac.n
        qb0 = [k.q[0].n, k.q[1].n]

        with nc.Block() as blk:
            @blk.gpsimd
            def _(g):
                g.dma_start(out=k.bbb[:], in_=k.bb).then_inc(k.lg.h, 16)
                k.lg.n += 16
                g.wait_ge(k.lg.h, k.lg.n)

            @blk.sync
            def _(sync):
                for kk in range(32):
                    if kk >= 2:
                        sync.wait_ge(k.dv.h, dv0 + kk - 1)
                    sync.dma_start(out=wst7[kk % 2][:, 0:128],
                                   in_=k.aT[kk * 128:(kk + 1) * 128, :]).then_inc(k.q[kk % 2].h, 16)
                    k.q[kk % 2].n += 16
                for kk in range(32):
                    g = 32 + kk
                    sync.wait_ge(k.dv.h, dv0 + g - 1)
                    sync.dma_start(out=wst7[g % 2][:, 0:8],
                                   in_=k.wrT[kk * 128:(kk + 1) * 128, :]).then_inc(k.q[g % 2].h, 16)
                    k.q[g % 2].n += 16
                for sl in range(8):
                    g = 64 + sl
                    sync.wait_ge(k.dv.h, dv0 + g - 1)
                    sync.dma_start(out=wst7[g % 2][:, 0:512],
                                   in_=k.bstk[:, sl * 512:(sl + 1) * 512]).then_inc(k.q[g % 2].h, 16)
                    k.q[g % 2].n += 16


            @blk.vector
            def _(vector):
                def dstep(ins_):
                    ins_.then_inc(k.dv.h, 1)
                    k.dv.n += 1
                    vector.wait_ge(k.dv.h, k.dv.n)
                for kk in range(32):
                    vector.wait_ge(k.q[kk % 2].h, qb0[kk % 2] + 16 * (kk // 2 + 1))
                    dstep(vector.tensor_copy(out=atb[:, kk * 128:(kk + 1) * 128], in_=wst7[kk % 2][:, 0:128]))
                for kk in range(32):
                    g = 32 + kk
                    vector.wait_ge(k.q[g % 2].h, qb0[g % 2] + 16 * (g // 2 + 1))
                    dstep(vector.tensor_copy(out=wrtb[:, kk * 8:(kk + 1) * 8], in_=wst7[g % 2][:, 0:8]))
                for sl in range(8):
                    g = 64 + sl
                    vector.wait_ge(k.q[g % 2].h, qb0[g % 2] + 16 * (g // 2 + 1))
                    dstep(vector.tensor_scalar(out=k.bstb[:, sl * 512:(sl + 1) * 512],
                                               in0=wst7[g % 2][:, 0:512], scalar1=2.0,
                                               scalar2=None, op0=ALU.mult))
                k._p7base = k.dv.n
                k._p7evac, k._p7sub, k._p7fin, k._p7w8, k._p7h = [], [], [], [], []
                for j in range(NCH if P7PART >= 2 else 0):
                    vector.wait_ge(k.pe.h, pe0 + 2 * j + 1)
                    dstep(vector.tensor_copy(out=lg8[:, 0:8], in_=k.pf[4][:, 0:8]))
                    k._p7evac.append(k.dv.n)
                    dstep(vector.reduce_max(out=lg8[:, 40:41], in_=lg8[:, 0:8], axis=AX))
                    dstep(vector.tensor_scalar(out=lg8[:, 8:16], in0=lg8[:, 0:8], scalar1=lg8[:, 40:41],
                                               scalar2=None, op0=ALU.is_equal))
                    dstep(vector.scalar_tensor_tensor(out=lg8[:, 16:24], in0=lg8[:, 8:16], scalar=-1e30,
                                                      in1=lg8[:, 0:8], op0=ALU.mult, op1=ALU.add))
                    dstep(vector.reduce_max(out=lg8[:, 41:42], in_=lg8[:, 16:24], axis=AX))
                    dstep(vector.tensor_sub(out=lg8[:, 42:43], in0=lg8[:, 40:41], in1=lg8[:, 41:42]))
                    k._p7sub.append(k.dv.n)
                    vector.wait_ge(k.ac.h, ac0 + j + 1)
                    dstep(vector.tensor_scalar(out=lg8[:, 44:45], in0=lg8[:, 43:44], scalar1=-1.0,
                                               scalar2=1.0, op0=ALU.mult, op1=ALU.add))
                    dstep(vector.tensor_scalar(out=lg8[:, 24:32], in0=lg8[:, 16:24], scalar1=lg8[:, 41:42],
                                               scalar2=None, op0=ALU.is_equal))
                    dstep(vector.tensor_scalar(out=lg8[:, 32:40], in0=lg8[:, 8:16], scalar1=lg8[:, 43:44],
                                               scalar2=None, op0=ALU.mult))
                    dstep(vector.scalar_tensor_tensor(out=lg8[:, 32:40], in0=lg8[:, 24:32],
                                                      scalar=lg8[:, 44:45], in1=lg8[:, 32:40],
                                                      op0=ALU.mult, op1=ALU.add))
                    k._p7fin.append(k.dv.n)
                    vector.wait_ge(k.pe.h, pe0 + 2 * j + 2)
                    dstep(vector.tensor_copy(out=w8tb[:, j * 128:(j + 1) * 128], in_=k.pf[5][0:8, 0:128]))
                    k._p7w8.append(k.dv.n)
                for t5 in range(4 if P7PART >= 3 else 0):
                    vector.wait_ge(k.pe.h, pe0 + 32 + 2 * t5 + 1)
                    dstep(vector.tensor_copy(out=wr_sb[:], in_=k.pf[3][:]))
                    vector.wait_ge(k.pe.h, pe0 + 32 + 2 * t5 + 2)
                    dstep(vector.tensor_tensor(out=k.hwb[:, t5 * 512:(t5 + 1) * 512],
                                               in0=k.pf[2][:], in1=wr_sb[:], op=ALU.mult))
                    k._p7h.append(k.dv.n)

            @blk.scalar
            def _(scalar):
                for j in range(NCH if P7PART >= 2 else 0):
                    scalar.wait_ge(k.dv.h, k._p7sub[j])
                    scalar.activation(out=lg8[:, 43:44], in_=lg8[:, 42:43],
                                      func=AF.Sigmoid).then_inc(k.ac.h, 1)
                    k.ac.n += 1

            @blk.tensor
            def _(tensor):
                for j in range(NCH if P7PART >= 2 else 0):
                    tensor.wait_ge(k.dv.h, dv0 + 64)
                    if j >= 1:
                        tensor.wait_ge(k.dv.h, k._p7evac[j - 1])
                    last = None
                    for kk in range(32):
                        last = tensor.matmul(k.pf[4][:, 0:8], k.yfm(kk)[:, j * 128:(j + 1) * 128],
                                             wrtb[:, kk * 8:(kk + 1) * 8],
                                             start=(kk == 0), stop=(kk == 31))
                    last.then_inc(k.pe.h, 1)
                    k.pe.n += 1
                    tensor.wait_ge(k.dv.h, k._p7fin[j])
                    if j >= 1:
                        tensor.wait_ge(k.dv.h, k._p7w8[j - 1])
                    tensor.transpose(k.pf[5][0:8, 0:128], lg8[:, 32:40], k.ident_f[:]).then_inc(k.pe.h, 1)
                    k.pe.n += 1
                for t5 in range(4 if P7PART >= 3 else 0):
                    tensor.wait_ge(k.dv.h, dv0 + 32)
                    if t5 >= 1:
                        tensor.wait_ge(k.dv.h, k._p7h[t5 - 1])
                    tensor.wait_ge(k.dv.h, k._p7w8[NCH - 1])
                    tensor.matmul(k.pf[3][:], k.r8[:], w8tb[:, t5 * 512:(t5 + 1) * 512],
                                  start=True, stop=True).then_inc(k.pe.h, 1)
                    k.pe.n += 1
                    last = None
                    for kk in range(32):
                        last = tensor.matmul(k.pf[2][:],
                                             atb[:, kk * 128:(kk + 1) * 128],
                                             k.yfm(kk)[:, t5 * 512:(t5 + 1) * 512],
                                             start=(kk == 0), stop=(kk == 31))
                    last.then_inc(k.pe.h, 1)
                    k.pe.n += 1
    _barrier(k)


def _p8_ffn(k):
    nc = k.nc
    with ExitStack() as ctx:
        e = ctx.enter_context
        CW = 256
        slices = [(s * CW, CW) for s in range(16)]
        wst = [e(nc.sbuf_tensor(f"wst8{i}", [128, CW], F32)) for i in range(2)]
        wsb = [e(nc.sbuf_tensor(f"wsb8{i}", [128, 32 * CW], BF16)) for i in range(2)]
        x1s = [e(nc.sbuf_tensor(f"x1s{i}", [128, CW], F32)) for i in range(2)]
        ost = [e(nc.sbuf_tensor(f"ost{i}", [128, CW], F32)) for i in range(2)]
        ld0, dv0, pe0, ls0, lg0 = k.ld.n, k.dv.n, k.pe.n, k.ls.n, k.lg.n
        qb0 = [k.q[0].n, k.q[1].n]
        qb2 = [k.q[8].n, k.q[9].n]
        qb4 = [k.q[4].n, k.q[5].n]
        NT = 16

        def cast_dv(g):
            si, kk = g // 32, g % 32
            return dv0 + si * (32 + NT) + kk + 1

        def evac_dv(si, j):
            return dv0 + si * (32 + NT) + 32 + j + 1

        def tile_pe(si, j):
            return pe0 + si * NT + j + 1

        with nc.Block() as blk:
            @blk.sync
            def _(sync):
                for si, (c0, cw) in enumerate(slices):
                    for kk in range(32):
                        g = si * 32 + kk
                        if g >= 2:
                            sync.wait_ge(k.dv.h, cast_dv(g - 2))
                        sync.dma_start(out=wst[g % 2][:, 0:cw],
                                       in_=k.wbT[kk * 128:(kk + 1) * 128, c0:c0 + cw]).then_inc(k.q[g % 2].h, 16)
                        k.q[g % 2].n += 16

            @blk.gpsimd
            def _(g_):
                for si, (c0, cw) in enumerate(slices):
                    for j in range(NT):
                        ej = si * NT + j
                        if ej >= 2:
                            ej2 = ej - 2
                            g_.wait_ge(k.dv.h, evac_dv(ej2 // NT, ej2 % NT))
                        g_.dma_start(out=x1s[j % 2][:, 0:cw],
                                     in_=k.x1_scr[j * 128:(j + 1) * 128, c0:c0 + cw]).then_inc(k.q[8 + ej % 2].h, 16)
                        k.q[8 + ej % 2].n += 16

            @blk.vector
            def _(vector):
                for si, (c0, cw) in enumerate(slices):
                    for kk in range(32):
                        g = si * 32 + kk
                        vector.wait_ge(k.q[g % 2].h, qb0[g % 2] + 16 * (g // 2 + 1))
                        if si >= 2 and kk == 0:
                            vector.wait_ge(k.pe.h, tile_pe(si - 2, NT - 1))
                        vector.tensor_copy(out=wsb[si % 2][:, kk * CW:kk * CW + cw],
                                           in_=wst[g % 2][:, 0:cw]).then_inc(k.dv.h, 1)
                        k.dv.n += 1
                    for j in range(NT):
                        ej = si * NT + j
                        vector.wait_ge(k.pe.h, tile_pe(si, j))
                        vector.wait_ge(k.q[8 + ej % 2].h, qb2[ej % 2] + 16 * (ej // 2 + 1))
                        if ej >= 2:
                            vector.wait_ge(k.q[4 + ej % 2].h, qb4[ej % 2] + 16 * (ej // 2))
                        vector.tensor_add(out=ost[j % 2][:, 0:cw], in0=x1s[j % 2][:, 0:cw],
                                          in1=k.pf[j % 2][:, 0:cw]).then_inc(k.dv.h, 1)
                        k.dv.n += 1

            @blk.tensor
            def _(tensor):
                for si, (c0, cw) in enumerate(slices):
                    tensor.wait_ge(k.dv.h, dv0 + si * (32 + NT) + 32)
                    for j in range(NT):
                        if si * NT + j >= 2:
                            jj = si * NT + j - 2
                            tensor.wait_ge(k.dv.h, evac_dv(jj // NT, jj % NT))
                        pb = k.pf[j % 2]
                        last = None
                        for kk in range(32):
                            last = tensor.matmul(pb[:, 0:cw], k.yfm(kk)[:, j * 128:(j + 1) * 128],
                                                 wsb[si % 2][:, kk * CW:kk * CW + cw],
                                                 start=(kk == 0), stop=False)
                        tensor.matmul(pb[:, 0:cw], k.hwb[:, j * 128:(j + 1) * 128],
                                      k.bstb[:, c0:c0 + cw], start=False, stop=False)
                        last = tensor.matmul(pb[:, 0:cw], k.ones1[:],
                                             k.bbb[0:1, c0:c0 + cw], start=False, stop=True)
                        last.then_inc(k.pe.h, 1)
                        k.pe.n += 1

            @blk.scalar
            def _(scalar):
                for si, (c0, cw) in enumerate(slices):
                    for j in range(NT):
                        scalar.wait_ge(k.dv.h, evac_dv(si, j))
                        ej = si * NT + j
                        scalar.dma_start(out=k.out_d[j * 128:(j + 1) * 128, c0:c0 + cw],
                                         in_=ost[j % 2][:, 0:cw]).then_inc(k.q[4 + ej % 2].h, 16)
                        k.q[4 + ej % 2].n += 16
    _barrier(k)

def build_nc():
    nc = bass.Bass("TRN2", target_bir_lowering=False, detect_race_conditions=False)
    k = K()
    k.nc = nc

    dr = lambda n, s, d=F32, kind="ExternalInput": nc.dram_tensor(n, s, d, kind=kind).ap()
    k.x0h = dr("x0h", [TH, DM])
    k.w_in = dr("w_in", [DM, PROJ])
    k.w_op = dr("w_op", [DM, DM])
    k.wbT = dr("wbT", [DM, DM])
    k.wrT = dr("wrT", [DM, 8])
    k.aT = dr("aT", [DM, 128])
    k.bstk = dr("bstk", [128, DM])
    k.bb = dr("bb", [1, DM])
    k.conv_fm = dr("conv_fm", [128, 136])
    k.lnw_fm = dr("lnw_fm", [128, 32])
    k.nrmw_fm = dr("nrmw_fm", [128, 32])
    k.mupz = dr("mupz", [128, DM])
    k.mup_fm = dr("mup_fm", [128, 35])
    k.dt_bias = dr("dt_bias", [64, 1])
    k.a_log = dr("a_log", [64, 1])
    k.d_bc = dr("d_bc", [128, 64])
    k.hmask = dr("hmask", [128, 1])
    k.ident_bf_d = dr("ident_bf", [128, 128], BF16)
    k.ident_f_d = dr("ident_f", [128, 128], F32)
    k.ones1_d = dr("ones1", [1, 128], BF16)
    k.r64_d = dr("r64", [64, DM], BF16)
    k.r8_d = dr("r8", [8, 128], BF16)
    k.m4_d = dr("m4", [128, 512], F32)
    k.ones64_d = dr("ones64", [64, 128], F32)
    k.out_d = dr("out", [TR, DM], F32, kind="ExternalOutput")

    k.z_scr = nc.dram_tensor("z_scr", [TR, DM], F32).ap()
    k.xbc_scr = nc.dram_tensor("xbc_scr", [CONVD, TH], F32).ap()
    k.y_scr = nc.dram_tensor("y_scr", [TR, DM], F32).ap()
    k.attn_scr = nc.dram_tensor("attn_scr", [TR, DM], F32).ap()
    k.x1_scr = nc.dram_tensor("x1_scr", [TR, DM], F32).ap()
    k.vsp_scr = nc.dram_tensor("vsp_scr", [64, 3 * TR], BF16).ap()
    k.csp_scr = nc.dram_tensor("csp_scr", [64, 3 * TR], BF16).ap()
    k.dt_scr = nc.dram_tensor("dt_scr", [64, TH], F32).ap()
    k.xtm_scr = nc.dram_tensor("xtm_scr", [TR, DM], BF16).ap()
    k.cc_in = nc.dram_tensor("cc_in", [128, DM], F32)
    k.cc_out = nc.dram_tensor("cc_out", [256, DM], F32)

    dbg_outs = {}
    k.dbgG = None
    k.dbgCB = None
    for name, shape in [("z", [TR, DM]), ("xbc", [CONVD, TH]), ("y", [TR, DM]),
                        ("attn", [TR, DM]), ("x1", [TR, DM]), ("H", [128, DM]),
                        ("dt", [64, TH]), ("xtm", [TR, DM]), ("xn", [128, 69632]), ("G", [128, 2048]), ("CB", [128, 512])]:
        if name in DBG:
            dbg_outs[name] = dr("dbg_" + name, shape, F32, kind="ExternalOutput")
            if name == "G":
                k.dbgG = dbg_outs[name]
            if name == "CB":
                k.dbgCB = dbg_outs[name]

    with ExitStack() as top:
        e = top.enter_context
        k.ld = Sem(top, nc, "ld")
        k.lg = Sem(top, nc, "lg")
        k.ls = Sem(top, nc, "ls")
        k.out = Sem(top, nc, "outs")
        k.pe = Sem(top, nc, "pe")
        k.dv = Sem(top, nc, "dv")
        k.ac = Sem(top, nc, "ac")
        k.po = Sem(top, nc, "po")
        k.cc = Sem(top, nc, "cc")
        k.q = [Sem(top, nc, f"q{i}") for i in range(10)]

        k.pf = [e(nc.psum_tensor(f"pf{i}", [128, 512], F32)) for i in range(6)]
        k.pbf = [e(nc.psum_tensor(f"pbf{i}", [128, 1024], BF16)) for i in range(2)]

        k.ident_bf = e(nc.sbuf_tensor("identbf", [128, 128], BF16))
        k.ident_f = e(nc.sbuf_tensor("identf", [128, 128], F32))
        k.ones1 = e(nc.sbuf_tensor("sones1", [1, 128], BF16))
        k.r8 = e(nc.sbuf_tensor("sr8", [8, 128], BF16))
        k.ones64 = e(nc.sbuf_tensor("sones64", [64, 128], F32))
        k.lnw = e(nc.sbuf_tensor("slnw", [128, 32], F32))
        k.nrmw = e(nc.sbuf_tensor("snrmw", [128, 32], F32))
        k.mupf = e(nc.sbuf_tensor("smupf", [128, 35], F32))
        k.convw = e(nc.sbuf_tensor("sconvw", [128, 136], F32))
        k.dbias = e(nc.sbuf_tensor("sdbias", [64, 1], F32))
        k.aA = e(nc.sbuf_tensor("saA", [64, 1], F32))
        k.dbc = e(nc.sbuf_tensor("sdbc", [128, 64], F32))
        k.hm = e(nc.sbuf_tensor("shm", [128, 1], F32))

        k.xn = lambda kk: k.arena[:, kk * TH:(kk + 1) * TH]

        with nc.Block() as blk:
            @blk.sync
            def _(sync):
                for dst, src in [(k.ident_bf, k.ident_bf_d), (k.ident_f, k.ident_f_d),
                                 (k.ones1, k.ones1_d), (k.r8, k.r8_d),
                                 (k.ones64, k.ones64_d), (k.lnw, k.lnw_fm),
                                 (k.nrmw, k.nrmw_fm), (k.mupf, k.mup_fm), (k.convw, k.conv_fm),
                                 (k.dbias, k.dt_bias), (k.aA, k.a_log), (k.dbc, k.d_bc),
                                 (k.hm, k.hmask)]:
                    sync.dma_start(out=dst[:], in_=src).then_inc(k.ld.h, 16)
                    k.ld.n += 16

            @blk.scalar
            def _(scalar):
                scalar.wait_ge(k.ld.h, k.ld.n)
                scalar.activation(out=k.aA[:], in_=k.aA[:], func=AF.Exp).then_inc(k.ac.h, 1)
                k.ac.n += 1

            @blk.vector
            def _(vector):
                vector.wait_ge(k.ac.h, k.ac.n)
                vector.tensor_scalar(out=k.aA[:], in0=k.aA[:], scalar1=-1.0, scalar2=None,
                                     op0=ALU.mult).then_inc(k.dv.h, 1)
                k.dv.n += 1
        _barrier(k)

        with ExitStack() as scA:
            k.arena = scA.enter_context(nc.sbuf_tensor("arena", [128, 69632], BF16))
            _ln_phase(k, k.x0h, lambda kk, tc: k.xn(kk)[:, tc * 128:(tc + 1) * 128], 17, with_rms=True)
            _barrier(k)
            if "xn" in DBG:
                with nc.Block() as blkx:
                    @blkx.gpsimd
                    def _(g):
                        g.dma_start(out=dbg_outs["xn"], in_=k.arena[:, 0:69632]).then_inc(k.lg.h, 16)
                        k.lg.n += 16
                        g.wait_ge(k.lg.h, k.lg.n)
                _barrier(k)
            if STOP == "P1":
                _dbg_dump(k, dbg_outs)
                return nc, dbg_outs

            _p2_inproj(k)
            _barrier(k)
        if STOP == "P2":
            _dbg_dump(k, dbg_outs)
            return nc, dbg_outs

        with ExitStack() as scP:
            eP = scP.enter_context
            k.bC = eP(nc.sbuf_tensor("bC", [128, TR], BF16))
            k.egl_bf = eP(nc.sbuf_tensor("eglbf", [64, TR], BF16))
            k.r64 = eP(nc.sbuf_tensor("sr64", [64, DM], BF16))
            with nc.Block() as blk0:
                @blk0.sync
                def _(sync):
                    sync.dma_start(out=k.r64[:], in_=k.r64_d).then_inc(k.ld.h, 16)
                    k.ld.n += 16

            with ExitStack() as scB:
                eB = scB.enter_context
                k.dt_t = eB(nc.sbuf_tensor("dtT", [64, TR], F32))
                k.cl_t = eB(nc.sbuf_tensor("clT", [64, TR], F32))
                k.bB = eB(nc.sbuf_tensor("bB", [128, TR], BF16))
                k.bBt = eB(nc.sbuf_tensor("bBt", [128, TR], BF16))
                k.t64g = eB(nc.sbuf_tensor("t64g", [64, 64], F32))
                _barrier(k)
                _p3_dt(k)
                _p3_conv(k)
                if STOP == "P3" or "xtm" in DBG:
                    with nc.Block() as blkx:
                        @blkx.gpsimd
                        def _(g):
                            if "xtm" in DBG:
                                g.dma_start(out=dbg_outs["xtm"], in_=k.xtm_scr).then_inc(k.lg.h, 16)
                                k.lg.n += 16
                            g.wait_ge(k.lg.h, k.lg.n)
                    _barrier(k)
                if STOP == "P3":
                    _dbg_dump(k, dbg_outs)
                    return nc, dbg_outs

                _p4_ssd(k)
                if STOP == "P4" and "H" in DBG:
                    with nc.Block() as blkh:
                        @blkh.gpsimd
                        def _(g):
                            g.dma_start(out=dbg_outs["H"], in_=k.H[:]).then_inc(k.lg.h, 16)
                            k.lg.n += 16
                            g.wait_ge(k.lg.h, k.lg.n)
                    _barrier(k)
                if STOP == "P4":
                    _dbg_dump(k, dbg_outs)
                    return nc, dbg_outs

            k.arena2 = e(nc.sbuf_tensor("arena2", [128, 65536], BF16, side="right"))
            k.yfm = lambda kk: k.arena2[:, kk * TR:(kk + 1) * TR]
            _p5_gate(k)
            if STOP == "P5p":
                _dbg_dump(k, dbg_outs)
                return nc, dbg_outs
            if STOP == "P5":
                _dbg_dump(k, dbg_outs)
                return nc, dbg_outs

        _p6_outproj(k)
        if STOP == "P6":
            _dbg_dump(k, dbg_outs)
            return nc, dbg_outs

        _ln_phase(k, k.x0h[128:TH, :], lambda kk, tc: k.yfm(kk)[:, tc * 128:(tc + 1) * 128],
                  16, with_rms=False, x1_mode=True, tag="b")
        _barrier(k)
        if STOP == "P7a":
            _dbg_dump(k, dbg_outs)
            return nc, dbg_outs

        k.bstb = e(nc.sbuf_tensor("bstb", [128, DM], BF16))
        k.bbb = e(nc.sbuf_tensor("bbb", [1, DM], BF16))
        k.hwb = e(nc.sbuf_tensor("hwb", [128, TR], BF16))
        _p7_router(k)
        if STOP == "P7":
            _dbg_dump(k, dbg_outs)
            return nc, dbg_outs

        _p8_ffn(k)
        _dbg_dump(k, dbg_outs)
    return nc, dbg_outs


def _dbg_dump(k, dbg_outs):
    nc = k.nc
    with nc.Block() as blk:
        @blk.gpsimd
        def _(g):
            for s in (k.ld, k.ls, k.lg, k.dv, k.pe, k.ac):
                if s.n:
                    g.wait_ge(s.h, s.n)
            for name, d in dbg_outs.items():
                src = {"z": k.z_scr, "xbc": k.xbc_scr, "y": k.y_scr,
                       "attn": k.attn_scr, "x1": k.x1_scr}.get(name)
                if src is not None:
                    g.dma_start(out=d, in_=src).then_inc(k.lg.h, 16)
                    k.lg.n += 16
                elif name == "dt":
                    g.dma_start(out=d, in_=k.dt_scr).then_inc(k.lg.h, 16)
                    k.lg.n += 16

            g.wait_ge(k.lg.h, k.lg.n)
            if k.out.n:
                g.wait_ge(k.out.h, k.out.n)


def _host_inputs(inp):
    x0 = np.asarray(inp["L0_attn_hidden_states"])
    ln_w = np.asarray(inp["L0_attn_ln_w"])
    w_in = np.ascontiguousarray(np.asarray(inp["L0_attn_in_proj_w"], np.float32))
    conv_w = np.asarray(inp["L0_attn_conv_w"])
    a_log = np.asarray(inp["L0_attn_A_log"])
    dt_b = np.asarray(inp["L0_attn_dt_bias"])
    Dv = np.asarray(inp["L0_attn_D"])
    norm_w = np.asarray(inp["L0_attn_norm_w"])
    w_op = np.ascontiguousarray(np.asarray(inp["L0_attn_out_proj_w"], np.float32))
    mup = np.asarray(inp["L0_attn_mup_vector"])
    wb = np.asarray(inp["L0_ffn_W_base"])
    bb = np.asarray(inp["L0_ffn_b_base"])
    wr = np.asarray(inp["L0_ffn_W_router"])
    Am = np.asarray(inp["L0_ffn_A"])
    Bm = np.asarray(inp["L0_ffn_B"])

    shared = {
        "w_in": w_in,
        "w_op": w_op,
        "wbT": np.ascontiguousarray(wb.T.astype(np.float32)),
        "wrT": np.ascontiguousarray(wr.T.astype(np.float32)),
        "aT": np.ascontiguousarray(Am.reshape(128, DM).T.astype(np.float32)),
        "bstk": np.ascontiguousarray(Bm.transpose(0, 2, 1).reshape(128, DM).astype(np.float32)),
        "bb": np.ascontiguousarray(bb.reshape(1, DM).astype(np.float32)),
        "conv_fm": np.ascontiguousarray(conv_w.reshape(34, 128, 4).transpose(1, 0, 2).reshape(128, 136).astype(np.float32)),
        "lnw_fm": np.ascontiguousarray(ln_w.reshape(32, 128).T.astype(np.float32)),
        "nrmw_fm": np.ascontiguousarray(norm_w.reshape(32, 128).T.astype(np.float32)),
        "mupz": np.ascontiguousarray(np.broadcast_to(mup[:DM], (128, DM)).astype(np.float32)),
        "mup_fm": np.ascontiguousarray(np.pad(mup[DM:].astype(np.float32), (0, 64)).reshape(35, 128).T),
        "dt_bias": np.ascontiguousarray(dt_b.reshape(64, 1).astype(np.float32)),
        "a_log": np.ascontiguousarray(a_log.reshape(64, 1).astype(np.float32)),
        "d_bc": np.ascontiguousarray(np.broadcast_to(np.asarray(Dv)[None, :], (128, 64)).astype(np.float32)),
        "ident_bf": np.eye(128, dtype=BF),
        "ident_f": np.eye(128, dtype=np.float32),
        "ones1": np.ones((1, 128), dtype=BF),
        "r64": np.ascontiguousarray(np.repeat(np.eye(64, dtype=np.float32), 64, axis=1).astype(BF)),
        "r8": np.ascontiguousarray(np.repeat(np.eye(8, dtype=np.float32), 16, axis=1).astype(BF)),
        "m4": np.ascontiguousarray(np.tile(np.where(np.arange(128)[:, None] <= np.arange(128)[None, :], 0.0, -1e30).astype(np.float32), (1, 4))),
        "ones64": np.ones((64, 128), dtype=np.float32),
    }
    maps = []
    for c in range(8):
        b, h = c // 2, c % 2
        if h == 0:
            xh = np.concatenate([np.zeros((128, DM), np.float32), np.asarray(x0[b, 0:TR], np.float32)], axis=0)
        else:
            xh = np.ascontiguousarray(np.asarray(x0[b, TR - 128:2 * TR], np.float32))
        m = dict(shared)
        m["x0h"] = xh
        m["hmask"] = np.full((128, 1), float(h), np.float32)
        maps.append(m)
    return maps


def kernel(**inputs):
    maps = _host_inputs(inputs)
    nc, _ = build_nc()
    res = run_bass_kernel_spmd(nc, maps, core_ids=list(range(8)))
    out = np.empty((4, 4096, DM), np.float32)
    for c in range(8):
        b, h = c // 2, c % 2
        out[b, h * TR:(h + 1) * TR] = np.asarray(res.results[c]["out"])
    return out


# version 03
# speedup vs baseline: 1.0846x; 1.0846x over previous
import numpy as np
import ml_dtypes
from contextlib import ExitStack

import concourse.bass as bass
import concourse.mybir as mybir
from concourse.bass_utils import run_bass_kernel_spmd

F32 = mybir.dt.float32
BF16 = mybir.dt.bfloat16
AF = mybir.ActivationFunctionType
ALU = mybir.AluOpType
AX = mybir.AxisListType.X
BF = ml_dtypes.bfloat16

DM = 4096
PROJ = 8512
CONVD = 4352
NH, HD, SS = 64, 64, 128
TR = 2048
TH = 2176
NCH = 16
EPS = 1e-5
P5LIM = 16
P7PART = 3

DBG = set()
STOP = None
NCORES = 8


class Sem:
    def __init__(self, ctx, nc, name):
        self.h = ctx.enter_context(nc.semaphore(name))
        self.n = 0

    def inc(self, instr, k=1):
        instr.then_inc(self.h, k)
        self.n += k
        return self.n


class K:
    pass


def _barrier(k):
    nc = k.nc
    sems = [k.ld, k.lg, k.ls, k.pe, k.dv, k.ac, k.po, k.cc, k.out] + list(k.q)
    vals = [(s.h, s.n) for s in sems if s.n]
    with nc.Block() as blk:
        for eng_name in ("sync", "tensor", "vector", "scalar", "gpsimd"):
            def mk(eng, _v=vals):
                for h, n in _v:
                    eng.wait_ge(h, n)
            getattr(blk, eng_name)(mk)


def _ln_phase(k, src_dram, dst_view, nchunks, with_rms, x1_mode=False, tag=""):
    nc = k.nc
    npc = 15 if x1_mode else 14
    ldpc = 32 if x1_mode else 16
    ld0, dv0, ac0, pe0, ls0 = k.ld.n, k.dv.n, k.ac.n, k.pe.n, k.ls.n
    q0_base = [k.q[0].n, k.q[1].n]
    q2_base = k.q[2].n
    off = 1 if x1_mode else 0

    with ExitStack() as ctx:
        e = ctx.enter_context
        xt = [e(nc.sbuf_tensor(f"xt{tag}{i}", [128, DM], F32)) for i in range(2)]
        xb = [e(nc.sbuf_tensor(f"xb{tag}{i}", [128, DM], BF16)) for i in range(2)]
        at = e(nc.sbuf_tensor(f"at{tag}", [128, DM], F32)) if x1_mode else None
        stats = e(nc.sbuf_tensor(f"lnst{tag}", [128, 48], F32))
        mv = e(nc.sbuf_tensor(f"lnmv{tag}", [128, 8], F32))

        with nc.Block() as blk:
            @blk.sync
            def _(sync):
                for i in range(nchunks):
                    if i >= 2:
                        sync.wait_ge(k.dv.h, dv0 + npc * (i - 2) + off + 10)
                        if x1_mode:
                            sync.wait_ge(k.ls.h, ls0 + 16 * (i - 1))
                    qs = k.q[i % 2]
                    sync.dma_start(out=xt[i % 2][:], in_=src_dram[i * 128:(i + 1) * 128, :]).then_inc(qs.h, 16)
                    qs.n += 16
                    if x1_mode:
                        if i >= 1:
                            sync.wait_ge(k.dv.h, dv0 + npc * (i - 1) + 1)
                        sync.dma_start(out=at[:], in_=k.attn_scr[i * 128:(i + 1) * 128, :]).then_inc(k.q[2].h, 16)
                        k.q[2].n += 16

            @blk.vector
            def _(vector):
                for i in range(nchunks):
                    sl = i % 2
                    base = dv0 + npc * i
                    vector.wait_ge(k.q[i % 2].h, q0_base[i % 2] + 16 * (i // 2 + 1))
                    if x1_mode:
                        vector.wait_ge(k.q[2].h, q2_base + 16 * (i + 1))
                        vector.tensor_add(out=xt[sl][:], in0=xt[sl][:], in1=at[:]).then_inc(k.dv.h, 1)
                        k.dv.n += 1
                    for q in range(8):
                        ins_ = vector.bn_stats(out=stats[:, q * 6:(q + 1) * 6],
                                               in_=xt[sl][:, q * 512:(q + 1) * 512])
                    ins_.then_inc(k.dv.h, 1)
                    k.dv.n += 1
                    vector.wait_ge(k.dv.h, base + off + 1)
                    def _step(ins_):
                        ins_.then_inc(k.dv.h, 1)
                        k.dv.n += 1
                        k._tmpc += 1
                        vector.wait_ge(k.dv.h, k.dv.n)
                    k._tmpc = 0
                    _step(vector.bn_aggr(out=mv[:, 0:2], in_=stats[:, 0:48]))
                    _step(vector.tensor_scalar(out=mv[:, 2:3], in0=mv[:, 1:2], scalar1=EPS, scalar2=None, op0=ALU.add))
                    _step(vector.reciprocal(out=mv[:, 3:4], in_=mv[:, 2:3]))
                    if with_rms:
                        _step(vector.tensor_tensor(out=mv[:, 4:5], in0=mv[:, 1:2], in1=mv[:, 3:4], op=ALU.mult))
                        _step(vector.tensor_scalar(out=mv[:, 4:5], in0=mv[:, 4:5], scalar1=EPS, scalar2=None, op0=ALU.add))
                        _step(vector.reciprocal(out=mv[:, 5:6], in_=mv[:, 4:5]))
                    else:
                        for _ in range(3):
                            _step(vector.engine_nop())
                    vector.engine_nop().then_inc(k.dv.h, 1)
                    k.dv.n += 1
                    vector.wait_ge(k.ac.h, ac0 + (i + 1))
                    if with_rms:
                        vector.tensor_tensor(out=mv[:, 6:7], in0=mv[:, 6:7], in1=mv[:, 7:8], op=ALU.mult).then_inc(k.dv.h, 1)
                        k.dv.n += 1
                        vector.wait_ge(k.dv.h, base + off + 9)
                    else:
                        vector.engine_nop().then_inc(k.dv.h, 1)
                        k.dv.n += 1
                    vector.tensor_scalar(out=xb[sl][:], in0=xt[sl][:], scalar1=mv[:, 0:1], scalar2=mv[:, 6:7],
                                         op0=ALU.subtract, op1=ALU.mult).then_inc(k.dv.h, 1)
                    k.dv.n += 1
                    for g in range(4):
                        vector.wait_ge(k.pe.h, pe0 + 32 * i + 8 * (g + 1))
                        pb = k.pbf[g % 2]
                        last = None
                        for j in range(8):
                            kk = g * 8 + j
                            last = vector.tensor_copy(out=dst_view(kk, i), in_=pb[:, j * 128:(j + 1) * 128])
                        last.then_inc(k.dv.h, 1)
                        k.dv.n += 1

            @blk.scalar
            def _(scalar):
                for i in range(nchunks):
                    base = dv0 + npc * i
                    scalar.wait_ge(k.dv.h, base + off + 8)
                    if with_rms:
                        scalar.activation(out=mv[:, 6:7], in_=mv[:, 3:4], func=AF.Sqrt)
                        scalar.activation(out=mv[:, 7:8], in_=mv[:, 5:6], func=AF.Sqrt).then_inc(k.ac.h, 1)
                    else:
                        scalar.activation(out=mv[:, 6:7], in_=mv[:, 3:4], func=AF.Sqrt).then_inc(k.ac.h, 1)
                    k.ac.n += 1
                    if x1_mode:
                        scalar.wait_ge(k.dv.h, base + 1)
                        scalar.dma_start(out=k.x1_scr[i * 128:(i + 1) * 128, :], in_=xt[i % 2][:]).then_inc(k.ls.h, 16)
                        k.ls.n += 16

            @blk.tensor
            def _(tensor):
                for i in range(nchunks):
                    base = dv0 + npc * i
                    tensor.wait_ge(k.dv.h, base + off + 10)
                    for g in range(4):
                        gg = 4 * i + g
                        if gg >= 2:
                            i2, g2 = (gg - 2) // 4, (gg - 2) % 4
                            tensor.wait_ge(k.dv.h, dv0 + npc * i2 + off + 10 + g2 + 1)
                        pb = k.pbf[g % 2]
                        last = None
                        for j in range(8):
                            kk = g * 8 + j
                            last = tensor.transpose(pb[:, j * 128:(j + 1) * 128],
                                                    xb[i % 2][:, kk * 128:(kk + 1) * 128], k.ident_bf[:])
                        last.then_inc(k.pe.h, 8)
                        k.pe.n += 8


def _p2_inproj(k):
    nc = k.nc
    ZT = list(range(1, 17))
    TWS = [(0, 512), (512, 512), (1024, 512), (1536, 512), (2048, 128)]
    CW = 384
    FM0 = 11
    slices = [(s * CW, CW, "z") for s in range(10)] + [(3840, 256, "z")] + \
             [(4096 + s * CW, CW, "fm") for s in range(11)] + [(8320, 192, "fm")]

    def nevac(sl):
        c0, cw, kind = sl
        return 16 if kind == "z" else ((cw + 127) // 128) * 5

    def ntile(sl):
        return nevac(sl)

    dcum = [0]
    for sl in slices:
        dcum.append(dcum[-1] + 32 + nevac(sl))
    pcum = [0]
    for sl in slices:
        pcum.append(pcum[-1] + ntile(sl))

    def cast_dv(g):
        si, kk = g // 32, g % 32
        return dcum[si] + kk + 1

    def evac_dv(si, e):
        return dcum[si] + 32 + e + 1

    def tile_pe(si, j):
        return pcum[si] + j + 1

    ld0, dv0, pe0, ls0 = k.ld.n, k.dv.n, k.pe.n, k.ls.n
    QM = [0, 1, 10, 11]
    qb2 = [k.q[i].n for i in QM]
    qb3 = [k.q[2].n, k.q[3].n]
    qb4 = [k.q[4].n, k.q[5].n]
    k.fm_evac_ctr = 0
    k.fm_store_ctr = 0

    with ExitStack() as ctx:
        e = ctx.enter_context
        wst = [e(nc.sbuf_tensor(f"wst{i}", [128, 384], F32)) for i in range(4)]
        wsb = [e(nc.sbuf_tensor(f"wsb{i}", [128, 32 * 384], BF16)) for i in range(2)]
        zst = [e(nc.sbuf_tensor(f"zst{i}", [128, 384], F32)) for i in range(2)]
        fst = [e(nc.sbuf_tensor(f"fst{i}", [128, 512], F32)) for i in range(2)]

        with nc.Block() as blk:
            @blk.sync
            def _(sync):
                for si, (c0, cw, kind) in enumerate(slices):
                    for kk in range(32):
                        g = si * 32 + kk
                        if g >= 4:
                            sync.wait_ge(k.dv.h, dv0 + cast_dv(g - 4))
                        sync.dma_start(out=wst[g % 4][:, 0:cw],
                                       in_=k.w_in[kk * 128:(kk + 1) * 128, c0:c0 + cw]).then_inc(k.q[QM[g % 4]].h, 16)
                        k.q[QM[g % 4]].n += 16

            @blk.vector
            def _(vector):
                for si, (c0, cw, kind) in enumerate(slices):
                    for kk in range(32):
                        g = si * 32 + kk
                        vector.wait_ge(k.q[QM[g % 4]].h, qb2[g % 4] + 16 * (g // 4 + 1))
                        if si >= 2 and kk == 0:
                            vector.wait_ge(k.pe.h, pe0 + pcum[si - 1])
                        vector.tensor_scalar(out=wsb[si % 2][:, kk * CW:kk * CW + cw],
                                             in0=wst[g % 4][:, 0:cw],
                                             scalar1=k.lnw[:, kk:kk + 1], scalar2=None,
                                             op0=ALU.mult).then_inc(k.dv.h, 1)
                        k.dv.n += 1
                    if kind == "z":
                        for j in range(16):
                            ze = si * 16 + j
                            vector.wait_ge(k.pe.h, pe0 + tile_pe(si, j))
                            if ze >= 2:
                                vector.wait_ge(k.q[2 + ze % 2].h, qb3[ze % 2] + 16 * (ze // 2))
                            pb = k.pf[j % 2]
                            vector.tensor_copy(out=zst[j % 2][:, 0:cw],
                                                 in_=pb[:, 0:cw]).then_inc(k.dv.h, 1)
                            k.dv.n += 1
                    else:
                        nmc = (cw + 127) // 128
                        for m in range(nmc):
                            mw = min(128, cw - m * 128)
                            ch = (c0 + m * 128 - 4096) // 128
                            for j, (t0, tw) in enumerate(TWS):
                                idx = m * 5 + j
                                fe = k.fm_evac_ctr; k.fm_evac_ctr += 1
                                vector.wait_ge(k.pe.h, pe0 + tile_pe(si, idx))
                                if fe >= 2:
                                    vector.wait_ge(k.q[4 + fe % 2].h, qb4[fe % 2] + 16 * (fe // 2))
                                pb = k.pf[2 + idx % 2]
                                vector.tensor_scalar(out=fst[idx % 2][0:mw, 0:tw], in0=pb[0:mw, 0:tw],
                                                     scalar1=k.mupf[0:mw, ch:ch + 1],
                                                     scalar2=None, op0=ALU.mult).then_inc(k.dv.h, 1)
                                k.dv.n += 1

            @blk.tensor
            def _(tensor):
                for si, (c0, cw, kind) in enumerate(slices):
                    tensor.wait_ge(k.dv.h, dv0 + dcum[si] + 32)
                    if kind == "z":
                        for j, tc in enumerate(ZT):
                            if si * 16 + j >= 2:
                                jj = si * 16 + j - 2
                                si2, j2 = jj // 16, jj % 16
                                tensor.wait_ge(k.dv.h, dv0 + evac_dv(si2, j2))
                            pb = k.pf[j % 2]
                            last = None
                            for kk in range(32):
                                last = tensor.matmul(pb[:, 0:cw], k.xn(kk)[:, tc * 128:(tc + 1) * 128],
                                                     wsb[si % 2][:, kk * CW:kk * CW + cw],
                                                     start=(kk == 0), stop=(kk == 31))
                            last.then_inc(k.pe.h, 1)
                            k.pe.n += 1
                    else:
                        nmc = (cw + 127) // 128
                        for m in range(nmc):
                            mw = min(128, cw - m * 128)
                            for j, (t0, tw) in enumerate(TWS):
                                idx = m * 5 + j
                                gidx = sum(ntile(slices[s2]) for s2 in range(FM0, si)) + idx
                                if gidx >= 2:
                                    pj = gidx - 2
                                    acc = 0
                                    for s2 in range(FM0, len(slices)):
                                        nt = ntile(slices[s2])
                                        if pj < acc + nt:
                                            tensor.wait_ge(k.dv.h, dv0 + evac_dv(s2, pj - acc))
                                            break
                                        acc += nt
                                pb = k.pf[2 + idx % 2]
                                last = None
                                for kk in range(32):
                                    last = tensor.matmul(pb[0:mw, 0:tw],
                                                         wsb[si % 2][:, kk * CW + m * 128: kk * CW + m * 128 + mw],
                                                         k.xn(kk)[:, t0:t0 + tw],
                                                         start=(kk == 0), stop=(kk == 31))
                                last.then_inc(k.pe.h, 1)
                                k.pe.n += 1

            @blk.scalar
            def _(scalar):
                for si, (c0, cw, kind) in enumerate(slices):
                    if kind == "z":
                        for j, tc in enumerate(ZT):
                            scalar.wait_ge(k.dv.h, dv0 + evac_dv(si, j))
                            ze2 = si * 16 + j
                            scalar.dma_start(out=k.z_scr[(tc - 1) * 128:tc * 128, c0:c0 + cw],
                                             in_=zst[j % 2][:, 0:cw]).then_inc(k.q[2 + ze2 % 2].h, 16)
                            k.q[2 + ze2 % 2].n += 16
                    else:
                        nmc = (cw + 127) // 128
                        for m in range(nmc):
                            mw = min(128, cw - m * 128)
                            ch0 = c0 + m * 128 - 4096
                            for j, (t0, tw) in enumerate(TWS):
                                idx = m * 5 + j
                                scalar.wait_ge(k.dv.h, dv0 + evac_dv(si, idx))
                                fe2 = k.fm_store_ctr; k.fm_store_ctr += 1
                                if ch0 < CONVD:
                                    scalar.dma_start(out=k.xbc_scr[ch0:ch0 + mw, t0:t0 + tw],
                                                     in_=fst[idx % 2][0:mw, 0:tw]).then_inc(k.q[4 + fe2 % 2].h, 16)
                                else:
                                    scalar.dma_start(out=k.dt_scr[0:64, t0:t0 + tw],
                                                     in_=fst[idx % 2][0:64, 0:tw]).then_inc(k.q[4 + fe2 % 2].h, 16)
                                k.q[4 + fe2 % 2].n += 16




def _p3_dt(k):
    nc = k.nc
    with ExitStack() as ctx:
        e = ctx.enter_context
        dtf = e(nc.sbuf_tensor("dtf", [64, TH], F32))
        tA = e(nc.sbuf_tensor("tA", [64, TR], F32))
        tB = e(nc.sbuf_tensor("tB", [64, TR], F32))
        sp = e(nc.sbuf_tensor("sp", [64, TR], BF16))
        ldt = dtf[:, 0:TR]
        spf = dtf[:, 0:TR]
        ld0, dv0, ac0, lg0 = k.ld.n, k.dv.n, k.ac.n, k.lg.n
        t64 = k.t64g
        k._sp_pos = []

        with nc.Block() as blk:
            @blk.sync
            def _(sync):
                sync.dma_start(out=dtf[:], in_=k.dt_scr).then_inc(k.ld.h, 16)
                k.ld.n += 16

            @blk.vector
            def _(vector):
                def dstep(ins_):
                    ins_.then_inc(k.dv.h, 1)
                    k.dv.n += 1
                    vector.wait_ge(k.dv.h, k.dv.n)
                vector.wait_ge(k.ld.h, ld0 + 16)
                dstep(vector.tensor_scalar(out=tA[:], in0=dtf[:, 128:TH], scalar1=k.dbias[:, 0:1],
                                           scalar2=None, op0=ALU.add))
                dstep(vector.tensor_scalar(out=tB[:], in0=tA[:], scalar1=0.0, scalar2=None, op0=ALU.max))
                dstep(vector.engine_nop())
                vector.wait_ge(k.ac.h, ac0 + 3)
                dstep(vector.tensor_add(out=k.dt_t[:], in0=tB[:], in1=ldt[:]))
                dstep(vector.tensor_scalar(out=tB[:], in0=k.dt_t[:], scalar1=1e-30, scalar2=None, op0=ALU.max))
                dstep(vector.tensor_scalar(out=tA[:], in0=k.dt_t[:], scalar1=k.aA[:, 0:1],
                                           scalar2=None, op0=ALU.mult))
                for c in range(NCH):
                    dstep(vector.tensor_tensor_scan(out=k.cl_t[:, c * 128:(c + 1) * 128],
                                                    data0=k.ones64[:, 0:128],
                                                    data1=tA[:, c * 128:(c + 1) * 128],
                                                    initial=0.0, op0=ALU.mult, op1=ALU.add))
                dstep(vector.memset(t64[:, 16:17], 0.0))
                for c in range(NCH):
                    dstep(vector.tensor_copy(out=t64[:, c:c + 1], in_=k.cl_t[:, c * 128 + 127:c * 128 + 128]))
                for c in range(1, NCH):
                    dstep(vector.tensor_add(out=t64[:, 16 + c:17 + c], in0=t64[:, 15 + c:16 + c],
                                            in1=t64[:, c - 1:c]))
                vector.wait_ge(k.ac.h, ac0 + 4)
                for c in range(NCH):
                    dstep(vector.tensor_scalar(out=tB[:, c * 128:(c + 1) * 128],
                                               in0=k.cl_t[:, c * 128:(c + 1) * 128],
                                               scalar1=t64[:, 16 + c:17 + c], scalar2=None, op0=ALU.add))
                vector.wait_ge(k.ac.h, ac0 + 5)
                dstep(vector.tensor_copy(out=k.egl_bf[:], in_=tA[:]))
                dstep(vector.tensor_sub(out=tB[:], in0=ldt[:], in1=k.cl_t[:]))
                nd = 0
                for which, val in enumerate((tB, k.cl_t)):
                    cur = val
                    for part in range(3):
                        if nd:
                            vector.wait_ge(k.lg.h, lg0 + 16 * nd)
                        dstep(vector.tensor_copy(out=sp[:], in_=cur[:]))
                        k._sp_pos.append(k.dv.n)
                        nd += 1
                        if part < 2:
                            dstep(vector.tensor_copy(out=spf[:], in_=sp[:]))
                            dstep(vector.tensor_sub(out=tA[:], in0=cur[:], in1=spf[:]))
                            cur = tA
                dstep(vector.tensor_copy(out=t64[:, 32:48], in_=t64[:, 0:16]))
                vector.wait_ge(k.ac.h, ac0 + 6)
                vector.wait_ge(k.lg.h, lg0 + 16 * nd)

            @blk.scalar
            def _(scalar):
                def astep(ins_):
                    ins_.then_inc(k.ac.h, 1)
                    k.ac.n += 1
                    scalar.wait_ge(k.ac.h, k.ac.n)
                scalar.wait_ge(k.dv.h, dv0 + 3)
                astep(scalar.activation(out=ldt[:], in_=tA[:], func=AF.Abs))
                astep(scalar.activation(out=tA[:], in_=ldt[:], func=AF.Exp, scale=-1.0))
                astep(scalar.activation(out=ldt[:], in_=tA[:], func=AF.Ln, bias=1.0))
                scalar.wait_ge(k.dv.h, dv0 + 5)
                astep(scalar.activation(out=ldt[:], in_=tB[:], func=AF.Ln))
                scalar.wait_ge(k.dv.h, dv0 + 70)
                astep(scalar.activation(out=tA[:], in_=tB[:], func=AF.Exp))
                scalar.wait_ge(k.dv.h, dv0 + 87)
                astep(scalar.activation(out=t64[:, 48:64], in_=t64[:, 32:48], func=AF.Exp))

            @blk.gpsimd
            def _(g):
                for di, dvpos in enumerate(k._sp_pos):
                    which, part = di // 3, di % 3
                    scr = k.vsp_scr if which == 0 else k.csp_scr
                    g.wait_ge(k.dv.h, dvpos)
                    g.dma_start(out=scr[:, part * TR:(part + 1) * TR], in_=sp[:]).then_inc(k.lg.h, 16)
                    k.lg.n += 16
                g.wait_ge(k.lg.h, k.lg.n)
    _barrier(k)


def _p3_conv(k):
    nc = k.nc
    with ExitStack() as ctx:
        e = ctx.enter_context
        cbuf = [e(nc.sbuf_tensor(f"cbuf{i}", [128, TH - 124], F32)) for i in range(2)]
        co = [e(nc.sbuf_tensor(f"co{i}", [128, TR], F32)) for i in range(2)]
        xsb = e(nc.sbuf_tensor("xsb", [128, TR], BF16))
        xts = [e(nc.sbuf_tensor(f"xts{i}", [128, TR], BF16)) for i in range(2)]
        ld0, dv0, pe0, ac0, ls0 = k.ld.n, k.dv.n, k.pe.n, k.ac.n, k.ls.n
        qb0 = [k.q[0].n, k.q[1].n]
        qb2 = [k.q[2].n, k.q[3].n]
        DPQ = 9

        with nc.Block() as blk:
            @blk.sync
            def _(sync):
                for q in range(34):
                    if q >= 2:
                        sync.wait_ge(k.dv.h, dv0 + DPQ * (q - 2) + 4)
                    sync.dma_start(out=cbuf[q % 2][:], in_=k.xbc_scr[q * 128:(q + 1) * 128, 124:TH]).then_inc(k.q[q % 2].h, 16)
                    k.q[q % 2].n += 16

            @blk.vector
            def _(vector):
                for q in range(34):
                    vector.wait_ge(k.q[q % 2].h, qb0[q % 2] + 16 * (q // 2 + 1))
                    cb, c0_, c1_ = cbuf[q % 2], co[0], co[1]
                    vector.tensor_scalar(out=c0_[:], in0=cb[:, 1:1 + TR], scalar1=k.convw[:, 4 * q:4 * q + 1],
                                         scalar2=None, op0=ALU.mult).then_inc(k.dv.h, 1)
                    k.dv.n += 1
                    vector.wait_ge(k.dv.h, k.dv.n)
                    vector.scalar_tensor_tensor(out=c1_[:], in0=cb[:, 2:2 + TR], scalar=k.convw[:, 4 * q + 1:4 * q + 2],
                                                in1=c0_[:], op0=ALU.mult, op1=ALU.add).then_inc(k.dv.h, 1)
                    k.dv.n += 1
                    vector.wait_ge(k.dv.h, k.dv.n)
                    vector.scalar_tensor_tensor(out=c0_[:], in0=cb[:, 3:3 + TR], scalar=k.convw[:, 4 * q + 2:4 * q + 3],
                                                in1=c1_[:], op0=ALU.mult, op1=ALU.add).then_inc(k.dv.h, 1)
                    k.dv.n += 1
                    vector.wait_ge(k.dv.h, k.dv.n)
                    vector.scalar_tensor_tensor(out=c1_[:], in0=cb[:, 4:4 + TR], scalar=k.convw[:, 4 * q + 3:4 * q + 4],
                                                in1=c0_[:], op0=ALU.mult, op1=ALU.add).then_inc(k.dv.h, 1)
                    k.dv.n += 1
                    vector.wait_ge(k.ac.h, ac0 + q + 1)
                    if q < 32:
                        vector.tensor_copy(out=xsb[:], in_=c0_[:]).then_inc(k.dv.h, 1)
                        k.dv.n += 1
                        for g in range(2):
                            vector.wait_ge(k.pe.h, pe0 + 16 * q + 8 * (g + 1))
                            if q >= 2:
                                vector.wait_ge(k.q[2 + q % 2].h, qb2[q % 2] + 256 * (q // 2))
                            pb = k.pbf[g % 2]
                            last = None
                            for j in range(8):
                                tc = g * 8 + j
                                last = vector.tensor_copy(out=xts[q % 2][:, tc * 128:(tc + 1) * 128],
                                                          in_=pb[:, j * 128:(j + 1) * 128])
                            last.then_inc(k.dv.h, 1)
                            k.dv.n += 1
                    elif q == 32:
                        vector.tensor_copy(out=k.bB[:], in_=c0_[:]).then_inc(k.dv.h, 1)
                        k.dv.n += 1
                        for g in range(2):
                            vector.wait_ge(k.pe.h, pe0 + 16 * 32 + 8 * (g + 1))
                            pb = k.pbf[g % 2]
                            last = None
                            for j in range(8):
                                tc = g * 8 + j
                                last = vector.tensor_copy(out=k.bBt[:, tc * 128:(tc + 1) * 128],
                                                          in_=pb[:, j * 128:(j + 1) * 128])
                            last.then_inc(k.dv.h, 1)
                            k.dv.n += 1
                    else:
                        vector.tensor_copy(out=k.bC[:], in_=c0_[:]).then_inc(k.dv.h, 1)
                        k.dv.n += 1
                        for _ in range(2):
                            vector.engine_nop().then_inc(k.dv.h, 1)
                            k.dv.n += 1
                    for _ in range(2):
                        vector.engine_nop().then_inc(k.dv.h, 1)
                        k.dv.n += 1

            @blk.scalar
            def _(scalar):
                for q in range(34):
                    scalar.wait_ge(k.dv.h, dv0 + DPQ * q + 4)
                    scalar.activation(out=co[0][:], in_=co[1][:], func=AF.Silu).then_inc(k.ac.h, 1)
                    k.ac.n += 1
                    if q < 32:
                        for g in range(2):
                            scalar.wait_ge(k.dv.h, dv0 + DPQ * q + 5 + g + 1)
                            for j in range(8):
                                tc = g * 8 + j
                                scalar.dma_start(out=k.xtm_scr[tc * 128:(tc + 1) * 128, q * 128:(q + 1) * 128],
                                                 in_=xts[q % 2][:, tc * 128:(tc + 1) * 128]).then_inc(k.q[2 + q % 2].h, 16)
                                k.q[2 + q % 2].n += 16

            @blk.tensor
            def _(tensor):
                for q in range(33):
                    src2 = xsb if q < 32 else k.bB
                    tensor.wait_ge(k.dv.h, dv0 + DPQ * q + 5)
                    for g in range(2):
                        gg = 2 * q + g
                        if gg >= 2:
                            q2, g2 = (gg - 2) // 2, (gg - 2) % 2
                            tensor.wait_ge(k.dv.h, dv0 + DPQ * q2 + 5 + g2 + 1)
                        pb = k.pbf[g % 2]
                        last = None
                        for j in range(8):
                            tc = g * 8 + j
                            last = tensor.transpose(pb[:, j * 128:(j + 1) * 128],
                                                    src2[:, tc * 128:(tc + 1) * 128], k.ident_bf[:])
                        last.then_inc(k.pe.h, 8)
                        k.pe.n += 8
    _barrier(k)




def _p4_ssd(k):
    nc = k.nc
    with ExitStack() as ctx:
        e = ctx.enter_context
        xtt = [e(nc.sbuf_tensor(f"xtt{i}", [128, DM], BF16)) for i in range(2)]
        m4b = e(nc.sbuf_tensor("m4b", [128, 512], BF16))
        combL = e(nc.sbuf_tensor("combL", [6, 4096], BF16))
        combR = e(nc.sbuf_tensor("combR", [6, 4096], BF16))
        G = e(nc.sbuf_tensor("G", [128, 8192], BF16))
        estg = e(nc.sbuf_tensor("estg", [128, 1024], F32))
        e2 = e(nc.sbuf_tensor("e2", [128, 1024], F32))
        CBr = e(nc.sbuf_tensor("CBr", [128, 512], F32))
        eclT = e(nc.sbuf_tensor("eclT", [64, 128], BF16))
        wfT = e(nc.sbuf_tensor("wfT", [64, 128], BF16))
        gmT = e(nc.sbuf_tensor("gmT", [64, 128], BF16))
        tmpF = e(nc.sbuf_tensor("tmpF", [64, 128], F32))
        tmpG = e(nc.sbuf_tensor("tmpG", [64, 128], F32))
        tmpG2 = e(nc.sbuf_tensor("tmpG2", [64, 128], F32))
        tmpF2 = e(nc.sbuf_tensor("tmpF2", [64, 128], F32))
        eclb = e(nc.sbuf_tensor("eclb", [128, 1024], BF16))
        hbf = e(nc.sbuf_tensor("hbf", [128, 4096], BF16))
        wx = e(nc.sbuf_tensor("wx", [128, 1024], BF16))
        xds = e(nc.sbuf_tensor("xds", [128, 1024], BF16))
        drep = e(nc.sbuf_tensor("drep", [128, 4096], BF16))
        drow = e(nc.sbuf_tensor("drow", [1, 4096], BF16))
        yst = e(nc.sbuf_tensor("yst", [128, 512], F32))
        yst2 = e(nc.sbuf_tensor("yst2", [128, 1024], F32))
        hmt = e(nc.sbuf_tensor("hmt", [128, 512], F32))
        k.H = e(nc.sbuf_tensor("H", [128, DM], F32))

        with nc.Block() as blk0:
            ac00, pe00 = k.ac.n, k.pe.n

            @blk0.vector
            def _(vector):
                vector.memset(k.H[:], 0.0)
                vector.memset(combL[:], 1.0)
                vector.memset(combR[:], 1.0).then_inc(k.dv.h, 1)
                k.dv.n += 1

            @blk0.gpsimd
            def _(g):
                g.dma_start(out=m4b[:], in_=k.m4_d).then_inc(k.lg.h, 16)
                k.lg.n += 16
                g.dma_start(out=drow[:], in_=k.d_row).then_inc(k.lg.h, 16)
                k.lg.n += 16

            @blk0.tensor
            def _(tensor):
                tensor.wait_ge(k.lg.h, k.lg.n)
                for sl8 in range(8):
                    if sl8 >= 2:
                        tensor.wait_ge(k.ac.h, ac00 + sl8 - 1)
                    tensor.matmul(k.pf[sl8 % 2][:], k.ones1[:], drow[0:1, sl8 * 512:(sl8 + 1) * 512],
                                  start=True, stop=True).then_inc(k.pe.h, 1)
                    k.pe.n += 1

            @blk0.scalar
            def _(scalar):
                for sl8 in range(8):
                    scalar.wait_ge(k.pe.h, pe00 + sl8 + 1)
                    scalar.copy(out=drep[:, sl8 * 512:(sl8 + 1) * 512],
                                in_=k.pf[sl8 % 2][:]).then_inc(k.ac.h, 1)
                    k.ac.n += 1
        _barrier(k)

        ld0, lg0, ls0, dv0, pe0, ac0, po0 = k.ld.n, k.lg.n, k.ls.n, k.dv.n, k.pe.n, k.ac.n, k.po.n
        qb0 = [k.q[0].n, k.q[1].n]
        qb2 = [k.q[6].n, k.q[7].n]
        qb4 = [k.q[4].n, k.q[5].n]
        DV, PE_, AC, PO = 96, 73, 18, 8

        def dvC(c, p):
            return dv0 + DV * c + p

        def peC(c, p):
            return pe0 + PE_ * c + p

        def acC(c, p):
            return ac0 + AC * c + p

        def poC(c, p):
            return po0 + PO * c + p

        pe_cb = 1
        pe_dg = lambda i: 2 + i
        pe_eclB = lambda j: 18 + 4 * j
        pe_xd = lambda j: 19 + 4 * j
        pe_intra = lambda j: 20 + 4 * j
        pe_raw = lambda j: 21 + 4 * j
        pe_wrep = lambda s: 50 + 3 * s
        pe_smm = lambda s: 51 + 3 * s
        pe_gmm = lambda s: 52 + 3 * s
        dv_eclT, dv_tsG, dv_wmul, dv_wfT, dv_gmT = 1, 2, 3, 4, 5
        dv_cbr = 6
        dv_em = lambda i: 9 + 2 * i
        dv_g = lambda i: 10 + 2 * i
        dv_eclb = lambda j: 41 + 4 * j
        dv_xd = lambda j: 42 + 4 * j
        dv_t1 = lambda j: 43 + 4 * j
        dv_y = lambda j: 44 + 4 * j
        dv_wx = lambda s: 73 + 3 * s
        dv_hm = lambda s: 74 + 3 * s
        dv_hadd = lambda s: 75 + 3 * s
        po_hbf = lambda j: 1 + j
        ac_e1, ac_e3 = 1, 2
        ac_exp = lambda i: 3 + i

        with nc.Block() as blk:
            @blk.sync
            def _(sync):
                for c in range(NCH):
                    if c >= 2:
                        sync.wait_ge(k.dv.h, dvC(c - 2, dv_wx(7)))
                        sync.wait_ge(k.pe.h, peC(c - 2, pe_raw(7)))
                    sync.dma_start(out=xtt[c % 2][:], in_=k.xtm_scr[c * 128:(c + 1) * 128, :]).then_inc(k.q[c % 2].h, 16)
                    k.q[c % 2].n += 16

            @blk.gpsimd
            def _(g):
                for c in range(NCH):
                    for r in range(4):
                        gr = 4 * c + r
                        if gr >= 2:
                            c2, r2 = (gr - 2) // 4, (gr - 2) % 4
                            g.wait_ge(k.pe.h, peC(c2, pe_dg(4 * r2 + 3)))
                        sl = r % 2
                        hb = 16 * r
                        g.dma_start(
                            out=combL[0:3, sl * 2048:(sl + 1) * 2048].rearrange("s (r t) -> s r t", r=16),
                            in_=k.vsp_scr[hb:hb + 16, :].rearrange("r (s ch t) -> s ch r t", s=3, ch=NCH)[:, c],
                        ).then_inc(k.q[6 + gr % 2].h, 16)
                        k.q[6 + gr % 2].n += 16
                        g.dma_start(
                            out=combR[3:6, sl * 2048:(sl + 1) * 2048].rearrange("s (r t) -> s r t", r=16),
                            in_=k.csp_scr[hb:hb + 16, :].rearrange("r (s ch t) -> s ch r t", s=3, ch=NCH)[:, c],
                        ).then_inc(k.q[6 + gr % 2].h, 16)
                        k.q[6 + gr % 2].n += 16
                    if c > 0:
                        g.wait_ge(k.dv.h, dvC(c - 1, dv_hadd(7)))
                    for j in range(8):
                        g.tensor_copy(out=hbf[:, j * 512:(j + 1) * 512],
                                      in_=k.H[:, j * 512:(j + 1) * 512]).then_inc(k.po.h, 1)
                        k.po.n += 1


            @blk.vector
            def _(vector):
                for c in range(NCH):
                    base = dv0 + DV * c

                    def dput(ins_, pos):
                        ins_.then_inc(k.dv.h, 1)
                        k.dv.n += 1
                        assert k.dv.n == base + pos, (c, pos, k.dv.n - base)

                    vector.wait_ge(k.q[c % 2].h, qb0[c % 2] + 16 * (c // 2 + 1))
                    vector.wait_ge(k.ac.h, acC(c, ac_e1))
                    dput(vector.tensor_copy(out=eclT[:], in_=tmpF[:]), dv_eclT)
                    dput(vector.tensor_scalar(out=tmpG[:], in0=k.cl_t[:, c * 128:(c + 1) * 128],
                                              scalar1=-1.0, scalar2=k.t64g[:, c:c + 1],
                                              op0=ALU.mult, op1=ALU.add), dv_tsG)
                    vector.wait_ge(k.dv.h, base + dv_tsG)
                    vector.wait_ge(k.ac.h, acC(c, ac_e3))
                    dput(vector.tensor_tensor(out=tmpF2[:], in0=tmpG2[:],
                                              in1=k.dt_t[:, c * 128:(c + 1) * 128], op=ALU.mult), dv_wmul)
                    vector.wait_ge(k.dv.h, base + dv_wmul)
                    dput(vector.tensor_copy(out=wfT[:], in_=tmpF2[:]), dv_wfT)
                    dput(vector.tensor_scalar(out=gmT[:], in0=k.ones64[:, 0:128],
                                              scalar1=k.t64g[:, 48 + c:49 + c], scalar2=None,
                                              op0=ALU.mult), dv_gmT)
                    vector.wait_ge(k.pe.h, peC(c, pe_cb))
                    dput(vector.tensor_copy(out=CBr[:, 0:128], in_=k.pf[4][:, 0:128]), dv_cbr)
                    vector.wait_ge(k.dv.h, base + dv_cbr)
                    dput(vector.tensor_copy(out=CBr[:, 128:256], in_=CBr[:, 0:128]), dv_cbr + 1)
                    vector.wait_ge(k.dv.h, base + dv_cbr + 1)
                    dput(vector.tensor_copy(out=CBr[:, 256:512], in_=CBr[:, 0:256]), dv_cbr + 2)
                    for i in range(16):
                        if c > 0 or i >= 2:
                            tgt = acC(c, ac_exp(i - 2)) if i >= 2 else acC(c - 1, ac_exp(14 + i))
                            vector.wait_ge(k.ac.h, tgt)
                        vector.wait_ge(k.pe.h, peC(c, pe_dg(i)))
                        pfD = k.pf[i % 2]
                        dput(vector.tensor_tensor(out=estg[:, (i % 2) * 512:(i % 2 + 1) * 512],
                                                  in0=pfD[:], in1=m4b[:], op=ALU.add), dv_em(i))
                        vector.wait_ge(k.ac.h, acC(c, ac_exp(i)))
                        if c > 0:
                            vector.wait_ge(k.pe.h, peC(c - 1, pe_intra(i // 2)))
                        dput(vector.tensor_tensor(out=G[:, i * 512:(i + 1) * 512],
                                                  in0=e2[:, (i % 2) * 512:(i % 2 + 1) * 512],
                                                  in1=CBr[:], op=ALU.mult), dv_g(i))
                    for j in range(8):
                        vector.wait_ge(k.pe.h, peC(c, pe_eclB(j)))
                        pfe = k.pf[4 + (j % 2)]
                        dput(vector.tensor_copy(out=eclb[:, (j % 2) * 512:(j % 2 + 1) * 512],
                                                in_=pfe[:]), dv_eclb(j))
                        dput(vector.tensor_tensor(out=xds[:, (j % 2) * 512:(j % 2 + 1) * 512],
                                                  in0=xtt[c % 2][:, j * 512:(j + 1) * 512],
                                                  in1=drep[:, j * 512:(j + 1) * 512], op=ALU.mult), dv_xd(j))
                        vector.wait_ge(k.dv.h, base + dv_xd(j))
                        vector.wait_ge(k.pe.h, peC(c, pe_raw(j)))
                        pfr = k.pf[5 - (j % 2)]
                        dput(vector.tensor_tensor(out=yst[:], in0=pfr[:],
                                                  in1=eclb[:, (j % 2) * 512:(j % 2 + 1) * 512],
                                                  op=ALU.mult), dv_t1(j))
                        vector.wait_ge(k.dv.h, base + dv_t1(j))
                        jj = 8 * c + j
                        if jj >= 2:
                            vector.wait_ge(k.q[4 + jj % 2].h, qb4[jj % 2] + 16 * (jj // 2))
                        pfy = k.pf[2 + (j % 2)]
                        dput(vector.tensor_add(out=yst2[:, (j % 2) * 512:(j % 2 + 1) * 512],
                                               in0=yst[:], in1=pfy[:]), dv_y(j))
                    for s in range(8):
                        vector.wait_ge(k.pe.h, peC(c, pe_wrep(s)))
                        pfw = k.pf[4 + (s % 2)]
                        dput(vector.tensor_tensor(out=wx[:, (s % 2) * 512:(s % 2 + 1) * 512],
                                                  in0=xtt[c % 2][:, s * 512:(s + 1) * 512],
                                                  in1=pfw[:], op=ALU.mult), dv_wx(s))
                        vector.wait_ge(k.pe.h, peC(c, pe_gmm(s)))
                        pfg = k.pf[2 + (s % 2)]
                        dput(vector.tensor_tensor(out=hmt[:], in0=k.H[:, s * 512:(s + 1) * 512],
                                                  in1=pfg[:], op=ALU.mult), dv_hm(s))
                        vector.wait_ge(k.dv.h, base + dv_hm(s))
                        vector.wait_ge(k.pe.h, peC(c, pe_smm(s)))
                        pfs = k.pf[5 - (s % 2)]
                        dput(vector.tensor_add(out=k.H[:, s * 512:(s + 1) * 512],
                                               in0=hmt[:], in1=pfs[:]), dv_hadd(s))
                        vector.wait_ge(k.dv.h, base + dv_hadd(s))

            @blk.scalar
            def _(scalar):
                for c in range(NCH):
                    if c > 0:
                        scalar.wait_ge(k.dv.h, dvC(c - 1, dv_eclT))
                    scalar.activation(out=tmpF[:], in_=k.cl_t[:, c * 128:(c + 1) * 128],
                                      func=AF.Exp).then_inc(k.ac.h, 1)
                    k.ac.n += 1
                    scalar.wait_ge(k.dv.h, dvC(c, dv_tsG))
                    scalar.activation(out=tmpG2[:], in_=tmpG[:], func=AF.Exp).then_inc(k.ac.h, 1)
                    k.ac.n += 1
                    for i in range(16):
                        scalar.wait_ge(k.dv.h, dvC(c, dv_em(i)))
                        scalar.activation(out=e2[:, (i % 2) * 512:(i % 2 + 1) * 512],
                                          in_=estg[:, (i % 2) * 512:(i % 2 + 1) * 512],
                                          func=AF.Exp).then_inc(k.ac.h, 1)
                        k.ac.n += 1
                    for j in range(8):
                        jj = 8 * c + j
                        scalar.wait_ge(k.dv.h, dvC(c, dv_y(j)))
                        scalar.dma_start(out=k.y_scr[c * 128:(c + 1) * 128, j * 512:(j + 1) * 512],
                                         in_=yst2[:, (j % 2) * 512:(j % 2 + 1) * 512]).then_inc(k.q[4 + jj % 2].h, 16)
                        k.q[4 + jj % 2].n += 16

            @blk.tensor
            def _(tensor):
                bankw = {i: 0 for i in range(6)}

                def bwait(tensor, b, consumer_dv):
                    if bankw[b]:
                        tensor.wait_ge(k.dv.h, bankw[b])
                    bankw[b] = consumer_dv

                for c in range(NCH):
                    base_pe = pe0 + PE_ * c

                    def pput(ins_, pos):
                        ins_.then_inc(k.pe.h, 1)
                        k.pe.n += 1
                        assert k.pe.n == base_pe + pos, (c, pos, k.pe.n - base_pe)

                    bwait(tensor, 4, dvC(c, dv_cbr))
                    pput(tensor.matmul(k.pf[4][:, 0:128], k.bB[:, c * 128:(c + 1) * 128],
                                       k.bC[:, c * 128:(c + 1) * 128], start=True, stop=True), pe_cb)
                    tensor.wait_ge(k.q[c % 2].h, qb0[c % 2] + 16 * (c // 2 + 1))
                    for i in range(16):
                        r = i // 4
                        gr = 4 * c + r
                        tensor.wait_ge(k.q[6 + gr % 2].h, qb2[gr % 2] + 32 * (gr // 2 + 1))
                        bwait(tensor, i % 2, dvC(c, dv_em(i)))
                        pfD = k.pf[i % 2]
                        sl = r % 2
                        last = None
                        for hh in range(4):
                            hl = (i % 4) * 4 + hh
                            last = tensor.matmul(pfD[:, hh * 128:(hh + 1) * 128],
                                                 combL[0:6, sl * 2048 + hl * 128: sl * 2048 + (hl + 1) * 128],
                                                 combR[0:6, sl * 2048 + hl * 128: sl * 2048 + (hl + 1) * 128],
                                                 start=(hh == 0), stop=(hh == 3))
                        pput(last, pe_dg(i))
                    for j in range(8):
                        be = 4 + (j % 2)
                        bwait(tensor, be, dvC(c, dv_eclb(j)))
                        tensor.wait_ge(k.dv.h, dvC(c, dv_eclT))
                        pput(tensor.matmul(k.pf[be][:], eclT[:],
                                           k.r64[:, j * 512:(j + 1) * 512], start=True, stop=True), pe_eclB(j))
                        by = 2 + (j % 2)
                        bwait(tensor, by, dvC(c, dv_y(j)))
                        tensor.wait_ge(k.dv.h, dvC(c, dv_xd(j)))
                        pput(tensor.matmul(k.pf[by][:], k.ident_bf[:],
                                           xds[:, (j % 2) * 512:(j % 2 + 1) * 512],
                                           start=True, stop=False), pe_xd(j))
                        tensor.wait_ge(k.dv.h, dvC(c, dv_g(2 * j + 1)))
                        last = None
                        for jj2 in range(8):
                            head = 8 * j + jj2
                            gcol = head * 128
                            last = tensor.matmul(k.pf[by][:, jj2 * 64:(jj2 + 1) * 64],
                                                 G[:, gcol:gcol + 128],
                                                 xtt[c % 2][:, head * 64:(head + 1) * 64],
                                                 start=False, stop=(jj2 == 7))
                        pput(last, pe_intra(j))
                        br = 5 - (j % 2)
                        bwait(tensor, br, dvC(c, dv_t1(j)))
                        tensor.wait_ge(k.po.h, poC(c, po_hbf(j)))
                        pput(tensor.matmul(k.pf[br][:], k.bC[:, c * 128:(c + 1) * 128],
                                           hbf[:, j * 512:(j + 1) * 512], start=True, stop=True), pe_raw(j))
                    for s in range(8):
                        bw = 4 + (s % 2)
                        bwait(tensor, bw, dvC(c, dv_wx(s)))
                        tensor.wait_ge(k.dv.h, dvC(c, dv_wfT))
                        pput(tensor.matmul(k.pf[bw][:], wfT[:], k.r64[:, s * 512:(s + 1) * 512],
                                           start=True, stop=True), pe_wrep(s))
                        bs = 5 - (s % 2)
                        bwait(tensor, bs, dvC(c, dv_hadd(s)))
                        tensor.wait_ge(k.dv.h, dvC(c, dv_wx(s)))
                        pput(tensor.matmul(k.pf[bs][:], k.bBt[:, c * 128:(c + 1) * 128],
                                           wx[:, (s % 2) * 512:(s % 2 + 1) * 512],
                                           start=True, stop=True), pe_smm(s))
                        bg = 2 + (s % 2)
                        bwait(tensor, bg, dvC(c, dv_hm(s)))
                        tensor.wait_ge(k.dv.h, dvC(c, dv_gmT))
                        pput(tensor.matmul(k.pf[bg][:], gmT[:], k.r64[:, s * 512:(s + 1) * 512],
                                           start=True, stop=True), pe_gmm(s))
    _barrier(k)

    with nc.Block() as blk:
        @blk.scalar
        def _(scalar):
            scalar.dma_start(out=k.cc_in[:, :], in_=k.H[:]).then_inc(k.ls.h, 16)
            k.ls.n += 16

        @blk.gpsimd
        def _(g):
            g.wait_ge(k.ls.h, k.ls.n)
            g.collective_compute("AllGather", ALU.bypass,
                                 replica_groups=[[i, i + 1] for i in range(0, NCORES, 2)],
                                 ins=[k.cc_in.ap().opt()], outs=[k.cc_out.ap().opt()]).then_inc(k.cc.h, 1)
            k.cc.n += 1
            g.wait_ge(k.cc.h, k.cc.n)
    _barrier(k)


def _p5_gate(k):
    nc = k.nc
    with ExitStack() as ctx:
        e = ctx.enter_context
        hbf5 = e(nc.sbuf_tensor("hbf5", [128, DM], BF16))
        ytile = e(nc.sbuf_tensor("ytile", [128, DM], F32))
        ztile = e(nc.sbuf_tensor("ztile", [128, DM], F32))
        mupz5 = e(nc.sbuf_tensor("mupz5", [128, DM], BF16))
        eglb5 = e(nc.sbuf_tensor("eglb5", [128, 512], BF16))
        yst5 = e(nc.sbuf_tensor("yst5", [128, 512], F32))
        sq5 = e(nc.sbuf_tensor("sq5", [128, 512], F32))
        acc5 = e(nc.sbuf_tensor("acc5", [128, 16], F32))

        with nc.Block() as blk0:
            lg00 = k.lg.n

            @blk0.gpsimd
            def _(g):
                g.wait_ge(k.cc.h, k.cc.n)
                g.dma_start(out=mupz5[:], in_=k.mupz).then_inc(k.lg.h, 16)
                k.lg.n += 16
                for sl in range(8):
                    if sl >= 1:
                        g.wait_ge(k.dv.h, k.dv.n + 2 * sl - 1)
                    g.dma_start(out=sq5[:], in_=k.cc_out[0:128, sl * 512:(sl + 1) * 512]).then_inc(k.lg.h, 16)
                    k.lg.n += 16

            @blk0.vector
            def _(vector):
                for sl in range(8):
                    vector.wait_ge(k.lg.h, lg00 + 16 * (sl + 2))
                    vector.tensor_scalar(out=yst5[:], in0=sq5[:], scalar1=k.hm[:, 0:1],
                                         scalar2=None, op0=ALU.mult).then_inc(k.dv.h, 1)
                    k.dv.n += 1
                    vector.wait_ge(k.dv.h, k.dv.n)
                    vector.tensor_copy(out=hbf5[:, sl * 512:(sl + 1) * 512], in_=yst5[:]).then_inc(k.dv.h, 1)
                    k.dv.n += 1
        _barrier(k)
        if STOP == "P5p":
            return

        ld0, dv0, pe0, ac0 = k.ld.n, k.dv.n, k.pe.n, k.ac.n
        DV5, PE5, AC5 = 45, 24, 17
        dv_eglb = lambda j: 1 + 3 * j
        dv_t15 = lambda j: 2 + 3 * j
        dv_ya = lambda j: 3 + 3 * j
        dv_zm = 25
        dv_gm = lambda s: 26 + s
        dv_red, dv_ts, dv_rcp, dv_yn = 34, 35, 36, 37
        dv_ev = lambda g: 38 + g
        pe_egl5 = lambda j: 1 + 2 * j
        pe_cor5 = lambda j: 2 + 2 * j
        pe_tr5 = lambda g: 17 + g
        ac_sil = lambda s: 1 + s
        ac_sq = lambda s: 9 + s
        ac_rst = 17

        def dvC(c, p):
            return dv0 + DV5 * c + p

        def peC(c, p):
            return pe0 + PE5 * c + p

        def acC(c, p):
            return ac0 + AC5 * c + p

        with nc.Block() as blk:
            @blk.sync
            def _(sync):
                for c in range(P5LIM):
                    if c >= 1:
                        sync.wait_ge(k.pe.h, peC(c - 1, pe_tr5(7)))
                        sync.wait_ge(k.dv.h, dvC(c - 1, dv_gm(7)))
                    sync.dma_start(out=ytile[:], in_=k.y_scr[c * 128:(c + 1) * 128, :]).then_inc(k.ld.h, 16)
                    k.ld.n += 16
                    sync.dma_start(out=ztile[:], in_=k.z_scr[c * 128:(c + 1) * 128, :]).then_inc(k.ld.h, 16)
                    k.ld.n += 16

            @blk.tensor
            def _(tensor):
                bankw = {i: 0 for i in range(6)}

                def bwait(b, cons):
                    if bankw[b]:
                        tensor.wait_ge(k.dv.h, bankw[b])
                    bankw[b] = cons

                for c in range(P5LIM):
                    base_pe = pe0 + PE5 * c

                    def pput(ins_, pos):
                        ins_.then_inc(k.pe.h, 1)
                        k.pe.n += 1
                        assert k.pe.n == base_pe + pos, (c, pos, k.pe.n - base_pe)

                    for j in range(8):
                        be = 4 + (j % 2)
                        bwait(be, dvC(c, dv_eglb(j)))
                        pput(tensor.matmul(k.pf[be][:], k.egl_bf[:, c * 128:(c + 1) * 128],
                                           k.r64[:, j * 512:(j + 1) * 512], start=True, stop=True), pe_egl5(j))
                        by = 2 + (j % 2)
                        bwait(by, dvC(c, dv_t15(j)))
                        pput(tensor.matmul(k.pf[by][:], k.bC[:, c * 128:(c + 1) * 128],
                                           hbf5[:, j * 512:(j + 1) * 512], start=True, stop=True), pe_cor5(j))
                    tensor.wait_ge(k.dv.h, dvC(c, dv_yn))
                    for g in range(8):
                        b = g % 2
                        bwait(b, dvC(c, dv_ev(g)))
                        last = None
                        for j in range(4):
                            kk = g * 4 + j
                            last = tensor.transpose(k.pf[b][:, j * 128:(j + 1) * 128],
                                                    ytile[:, kk * 128:(kk + 1) * 128], k.ident_f[:])
                        pput(last, pe_tr5(g))

            @blk.vector
            def _(vector):
                for c in range(P5LIM):
                    base = dv0 + DV5 * c

                    def dput(ins_, pos):
                        ins_.then_inc(k.dv.h, 1)
                        k.dv.n += 1
                        assert k.dv.n == base + pos, (c, pos, k.dv.n - base)

                    vector.wait_ge(k.ld.h, ld0 + 32 * (c + 1))
                    for j in range(8):
                        vector.wait_ge(k.pe.h, peC(c, pe_egl5(j)))
                        dput(vector.tensor_copy(out=eglb5[:], in_=k.pf[4 + (j % 2)][:]), dv_eglb(j))
                        vector.wait_ge(k.dv.h, base + dv_eglb(j))
                        vector.wait_ge(k.pe.h, peC(c, pe_cor5(j)))
                        dput(vector.tensor_tensor(out=yst5[:], in0=k.pf[2 + (j % 2)][:],
                                                  in1=eglb5[:], op=ALU.mult), dv_t15(j))
                        vector.wait_ge(k.dv.h, base + dv_t15(j))
                        dput(vector.tensor_add(out=ytile[:, j * 512:(j + 1) * 512],
                                               in0=yst5[:], in1=ytile[:, j * 512:(j + 1) * 512]), dv_ya(j))
                    dput(vector.tensor_tensor(out=ztile[:], in0=ztile[:], in1=mupz5[:], op=ALU.mult), dv_zm)
                    vector.wait_ge(k.dv.h, base + dv_zm)
                    for s in range(8):
                        vector.wait_ge(k.ac.h, acC(c, ac_sil(s)))
                        dput(vector.tensor_tensor(out=ytile[:, s * 512:(s + 1) * 512],
                                                  in0=ytile[:, s * 512:(s + 1) * 512],
                                                  in1=sq5[:] if False else ztile[:, s * 512:(s + 1) * 512],
                                                  op=ALU.mult), dv_gm(s))
                        vector.wait_ge(k.dv.h, base + dv_gm(s))
                    vector.wait_ge(k.ac.h, acC(c, ac_sq(7)))
                    dput(vector.reduce_sum(out=acc5[:, 8:9], in_=acc5[:, 0:8], axis=AX), dv_red)
                    vector.wait_ge(k.dv.h, base + dv_red)
                    dput(vector.tensor_scalar(out=acc5[:, 9:10], in0=acc5[:, 8:9],
                                              scalar1=1.0 / DM, scalar2=EPS, op0=ALU.mult, op1=ALU.add), dv_ts)
                    vector.wait_ge(k.dv.h, base + dv_ts)
                    dput(vector.reciprocal(out=acc5[:, 10:11], in_=acc5[:, 9:10]), dv_rcp)
                    vector.wait_ge(k.ac.h, acC(c, ac_rst))
                    dput(vector.tensor_scalar(out=ytile[:], in0=ytile[:], scalar1=acc5[:, 11:12],
                                              scalar2=None, op0=ALU.mult), dv_yn)
                    for g in range(8):
                        vector.wait_ge(k.pe.h, peC(c, pe_tr5(g)))
                        last = None
                        for j in range(4):
                            kk = g * 4 + j
                            last = vector.tensor_copy(out=k.yfm(kk)[:, c * 128:(c + 1) * 128],
                                                      in_=k.pf[g % 2][:, j * 128:(j + 1) * 128])
                        dput(last, dv_ev(g))

            @blk.scalar
            def _(scalar):
                for c in range(P5LIM):
                    for s in range(8):
                        scalar.wait_ge(k.dv.h, dvC(c, dv_zm))
                        scalar.activation(out=ztile[:, s * 512:(s + 1) * 512],
                                          in_=ztile[:, s * 512:(s + 1) * 512],
                                          func=AF.Silu).then_inc(k.ac.h, 1)
                        k.ac.n += 1
                    for s in range(8):
                        scalar.wait_ge(k.dv.h, dvC(c, dv_gm(s)))
                        scalar.activation(out=sq5[:], in_=ytile[:, s * 512:(s + 1) * 512],
                                          func=AF.Square,
                                          accum_out=acc5[:, s:s + 1]).then_inc(k.ac.h, 1)
                        k.ac.n += 1
                    scalar.wait_ge(k.dv.h, dvC(c, dv_rcp))
                    scalar.activation(out=acc5[:, 11:12], in_=acc5[:, 10:11],
                                      func=AF.Sqrt).then_inc(k.ac.h, 1)
                    k.ac.n += 1
    _barrier(k)


def _p6_outproj(k):
    nc = k.nc
    with ExitStack() as ctx:
        e = ctx.enter_context
        wst = [e(nc.sbuf_tensor(f"wst6{i}", [128, 512], F32)) for i in range(4)]
        wsb = [e(nc.sbuf_tensor(f"wsb6{i}", [128, 32 * 512], BF16)) for i in range(2)]
        ast = [e(nc.sbuf_tensor(f"ast{i}", [128, 512], F32)) for i in range(2)]
        ld0, dv0, pe0, ls0 = k.ld.n, k.dv.n, k.pe.n, k.ls.n
        QM = [0, 1, 10, 11]
        qb0 = [k.q[i].n for i in QM]
        qb2 = [k.q[2].n, k.q[3].n]
        NT = 16

        def cast_dv(g):
            si, kk = g // 32, g % 32
            return dv0 + si * 48 + kk + 1

        def evac_dv(si, j):
            return dv0 + si * 48 + 32 + j + 1

        def tile_pe(si, j):
            return pe0 + si * NT + j + 1

        with nc.Block() as blk:
            @blk.sync
            def _(sync):
                for si in range(8):
                    for kk in range(32):
                        g = si * 32 + kk
                        if g >= 4:
                            sync.wait_ge(k.dv.h, cast_dv(g - 4))
                        sync.dma_start(out=wst[g % 4][:],
                                       in_=k.w_op[kk * 128:(kk + 1) * 128, si * 512:(si + 1) * 512]).then_inc(k.q[QM[g % 4]].h, 16)
                        k.q[QM[g % 4]].n += 16

            @blk.vector
            def _(vector):
                for si in range(8):
                    for kk in range(32):
                        g = si * 32 + kk
                        vector.wait_ge(k.q[QM[g % 4]].h, qb0[g % 4] + 16 * (g // 4 + 1))
                        if si >= 2 and kk == 0:
                            vector.wait_ge(k.pe.h, tile_pe(si - 2, NT - 1))
                        vector.tensor_scalar(out=wsb[si % 2][:, kk * 512:(kk + 1) * 512],
                                             in0=wst[g % 4][:],
                                             scalar1=k.nrmw[:, kk:kk + 1], scalar2=None,
                                             op0=ALU.mult).then_inc(k.dv.h, 1)
                        k.dv.n += 1
                    for j in range(NT):
                        ej = si * NT + j
                        vector.wait_ge(k.pe.h, tile_pe(si, j))
                        if ej >= 2:
                            vector.wait_ge(k.q[2 + ej % 2].h, qb2[ej % 2] + 16 * (ej // 2))
                        vector.tensor_copy(out=ast[j % 2][:], in_=k.pf[j % 2][:]).then_inc(k.dv.h, 1)
                        k.dv.n += 1

            @blk.tensor
            def _(tensor):
                for si in range(8):
                    tensor.wait_ge(k.dv.h, dv0 + si * 48 + 32)
                    for j in range(NT):
                        if si * NT + j >= 2:
                            jj = si * NT + j - 2
                            tensor.wait_ge(k.dv.h, evac_dv(jj // NT, jj % NT))
                        pb = k.pf[j % 2]
                        last = None
                        for kk in range(32):
                            last = tensor.matmul(pb[:], k.yfm(kk)[:, j * 128:(j + 1) * 128],
                                                 wsb[si % 2][:, kk * 512:(kk + 1) * 512],
                                                 start=(kk == 0), stop=(kk == 31))
                        last.then_inc(k.pe.h, 1)
                        k.pe.n += 1

            @blk.scalar
            def _(scalar):
                for si in range(8):
                    for j in range(NT):
                        ej = si * NT + j
                        scalar.wait_ge(k.dv.h, evac_dv(si, j))
                        scalar.dma_start(out=k.attn_scr[j * 128:(j + 1) * 128, si * 512:(si + 1) * 512],
                                         in_=ast[j % 2][:]).then_inc(k.q[2 + ej % 2].h, 16)
                        k.q[2 + ej % 2].n += 16
    _barrier(k)



def _p7_router(k):
    nc = k.nc
    with ExitStack() as ctx:
        e = ctx.enter_context
        atb = e(nc.sbuf_tensor("at7b", [128, DM], BF16))
        wrtb = e(nc.sbuf_tensor("wrtb", [128, 256], BF16))
        w8tb = e(nc.sbuf_tensor("w8tb", [8, TR], BF16))
        lg8 = e(nc.sbuf_tensor("lg8", [128, 48], F32))
        wr_sb = e(nc.sbuf_tensor("wr_sb", [128, 512], F32))
        wst7 = [e(nc.sbuf_tensor(f"wst7{i}", [128, 512], F32)) for i in range(2)]
        ld0, dv0, pe0, ac0 = k.ld.n, k.dv.n, k.pe.n, k.ac.n
        qb0 = [k.q[0].n, k.q[1].n]

        with nc.Block() as blk:
            @blk.gpsimd
            def _(g):
                g.dma_start(out=k.bbb[:], in_=k.bb).then_inc(k.lg.h, 16)
                k.lg.n += 16
                g.wait_ge(k.lg.h, k.lg.n)

            @blk.sync
            def _(sync):
                for kk in range(32):
                    if kk >= 2:
                        sync.wait_ge(k.dv.h, dv0 + kk - 1)
                    sync.dma_start(out=wst7[kk % 2][:, 0:128],
                                   in_=k.aT[kk * 128:(kk + 1) * 128, :]).then_inc(k.q[kk % 2].h, 16)
                    k.q[kk % 2].n += 16
                for kk in range(32):
                    g = 32 + kk
                    sync.wait_ge(k.dv.h, dv0 + g - 1)
                    sync.dma_start(out=wst7[g % 2][:, 0:8],
                                   in_=k.wrT[kk * 128:(kk + 1) * 128, :]).then_inc(k.q[g % 2].h, 16)
                    k.q[g % 2].n += 16
                for sl in range(8):
                    g = 64 + sl
                    sync.wait_ge(k.dv.h, dv0 + g - 1)
                    sync.dma_start(out=wst7[g % 2][:, 0:512],
                                   in_=k.bstk[:, sl * 512:(sl + 1) * 512]).then_inc(k.q[g % 2].h, 16)
                    k.q[g % 2].n += 16


            @blk.vector
            def _(vector):
                def dstep(ins_):
                    ins_.then_inc(k.dv.h, 1)
                    k.dv.n += 1
                    vector.wait_ge(k.dv.h, k.dv.n)
                for kk in range(32):
                    vector.wait_ge(k.q[kk % 2].h, qb0[kk % 2] + 16 * (kk // 2 + 1))
                    dstep(vector.tensor_copy(out=atb[:, kk * 128:(kk + 1) * 128], in_=wst7[kk % 2][:, 0:128]))
                for kk in range(32):
                    g = 32 + kk
                    vector.wait_ge(k.q[g % 2].h, qb0[g % 2] + 16 * (g // 2 + 1))
                    dstep(vector.tensor_copy(out=wrtb[:, kk * 8:(kk + 1) * 8], in_=wst7[g % 2][:, 0:8]))
                for sl in range(8):
                    g = 64 + sl
                    vector.wait_ge(k.q[g % 2].h, qb0[g % 2] + 16 * (g // 2 + 1))
                    dstep(vector.tensor_scalar(out=k.bstb[:, sl * 512:(sl + 1) * 512],
                                               in0=wst7[g % 2][:, 0:512], scalar1=2.0,
                                               scalar2=None, op0=ALU.mult))
                k._p7base = k.dv.n
                k._p7evac, k._p7sub, k._p7fin, k._p7w8, k._p7h = [], [], [], [], []
                for j in range(NCH if P7PART >= 2 else 0):
                    vector.wait_ge(k.pe.h, pe0 + 2 * j + 1)
                    dstep(vector.tensor_copy(out=lg8[:, 0:8], in_=k.pf[4][:, 0:8]))
                    k._p7evac.append(k.dv.n)
                    dstep(vector.reduce_max(out=lg8[:, 40:41], in_=lg8[:, 0:8], axis=AX))
                    dstep(vector.tensor_scalar(out=lg8[:, 8:16], in0=lg8[:, 0:8], scalar1=lg8[:, 40:41],
                                               scalar2=None, op0=ALU.is_equal))
                    dstep(vector.scalar_tensor_tensor(out=lg8[:, 16:24], in0=lg8[:, 8:16], scalar=-1e30,
                                                      in1=lg8[:, 0:8], op0=ALU.mult, op1=ALU.add))
                    dstep(vector.reduce_max(out=lg8[:, 41:42], in_=lg8[:, 16:24], axis=AX))
                    dstep(vector.tensor_sub(out=lg8[:, 42:43], in0=lg8[:, 40:41], in1=lg8[:, 41:42]))
                    k._p7sub.append(k.dv.n)
                    vector.wait_ge(k.ac.h, ac0 + j + 1)
                    dstep(vector.tensor_scalar(out=lg8[:, 44:45], in0=lg8[:, 43:44], scalar1=-1.0,
                                               scalar2=1.0, op0=ALU.mult, op1=ALU.add))
                    dstep(vector.tensor_scalar(out=lg8[:, 24:32], in0=lg8[:, 16:24], scalar1=lg8[:, 41:42],
                                               scalar2=None, op0=ALU.is_equal))
                    dstep(vector.tensor_scalar(out=lg8[:, 32:40], in0=lg8[:, 8:16], scalar1=lg8[:, 43:44],
                                               scalar2=None, op0=ALU.mult))
                    dstep(vector.scalar_tensor_tensor(out=lg8[:, 32:40], in0=lg8[:, 24:32],
                                                      scalar=lg8[:, 44:45], in1=lg8[:, 32:40],
                                                      op0=ALU.mult, op1=ALU.add))
                    k._p7fin.append(k.dv.n)
                    vector.wait_ge(k.pe.h, pe0 + 2 * j + 2)
                    dstep(vector.tensor_copy(out=w8tb[:, j * 128:(j + 1) * 128], in_=k.pf[5][0:8, 0:128]))
                    k._p7w8.append(k.dv.n)
                for t5 in range(4 if P7PART >= 3 else 0):
                    vector.wait_ge(k.pe.h, pe0 + 32 + 2 * t5 + 1)
                    dstep(vector.tensor_copy(out=wr_sb[:], in_=k.pf[3][:]))
                    vector.wait_ge(k.pe.h, pe0 + 32 + 2 * t5 + 2)
                    dstep(vector.tensor_tensor(out=k.hwb[:, t5 * 512:(t5 + 1) * 512],
                                               in0=k.pf[2][:], in1=wr_sb[:], op=ALU.mult))
                    k._p7h.append(k.dv.n)

            @blk.scalar
            def _(scalar):
                for j in range(NCH if P7PART >= 2 else 0):
                    scalar.wait_ge(k.dv.h, k._p7sub[j])
                    scalar.activation(out=lg8[:, 43:44], in_=lg8[:, 42:43],
                                      func=AF.Sigmoid).then_inc(k.ac.h, 1)
                    k.ac.n += 1

            @blk.tensor
            def _(tensor):
                for j in range(NCH if P7PART >= 2 else 0):
                    tensor.wait_ge(k.dv.h, dv0 + 64)
                    if j >= 1:
                        tensor.wait_ge(k.dv.h, k._p7evac[j - 1])
                    last = None
                    for kk in range(32):
                        last = tensor.matmul(k.pf[4][:, 0:8], k.yfm(kk)[:, j * 128:(j + 1) * 128],
                                             wrtb[:, kk * 8:(kk + 1) * 8],
                                             start=(kk == 0), stop=(kk == 31))
                    last.then_inc(k.pe.h, 1)
                    k.pe.n += 1
                    tensor.wait_ge(k.dv.h, k._p7fin[j])
                    if j >= 1:
                        tensor.wait_ge(k.dv.h, k._p7w8[j - 1])
                    tensor.transpose(k.pf[5][0:8, 0:128], lg8[:, 32:40], k.ident_f[:]).then_inc(k.pe.h, 1)
                    k.pe.n += 1
                for t5 in range(4 if P7PART >= 3 else 0):
                    tensor.wait_ge(k.dv.h, dv0 + 32)
                    if t5 >= 1:
                        tensor.wait_ge(k.dv.h, k._p7h[t5 - 1])
                    tensor.wait_ge(k.dv.h, k._p7w8[NCH - 1])
                    tensor.matmul(k.pf[3][:], k.r8[:], w8tb[:, t5 * 512:(t5 + 1) * 512],
                                  start=True, stop=True).then_inc(k.pe.h, 1)
                    k.pe.n += 1
                    last = None
                    for kk in range(32):
                        last = tensor.matmul(k.pf[2][:],
                                             atb[:, kk * 128:(kk + 1) * 128],
                                             k.yfm(kk)[:, t5 * 512:(t5 + 1) * 512],
                                             start=(kk == 0), stop=(kk == 31))
                    last.then_inc(k.pe.h, 1)
                    k.pe.n += 1
    _barrier(k)


def _p8_ffn(k):
    nc = k.nc
    with ExitStack() as ctx:
        e = ctx.enter_context
        CW = 256
        slices = [(s * CW, CW) for s in range(16)]
        wst = [e(nc.sbuf_tensor(f"wst8{i}", [128, CW], F32)) for i in range(4)]
        wsb = [e(nc.sbuf_tensor(f"wsb8{i}", [128, 32 * CW], BF16)) for i in range(2)]
        x1s = [e(nc.sbuf_tensor(f"x1s{i}", [128, CW], F32)) for i in range(2)]
        ost = [e(nc.sbuf_tensor(f"ost{i}", [128, CW], F32)) for i in range(2)]
        ld0, dv0, pe0, ls0, lg0 = k.ld.n, k.dv.n, k.pe.n, k.ls.n, k.lg.n
        QM = [0, 1, 10, 11]
        qb0 = [k.q[i].n for i in QM]
        qb2 = [k.q[8].n, k.q[9].n]
        qb4 = [k.q[4].n, k.q[5].n]
        NT = 16

        def cast_dv(g):
            si, kk = g // 32, g % 32
            return dv0 + si * (32 + NT) + kk + 1

        def evac_dv(si, j):
            return dv0 + si * (32 + NT) + 32 + j + 1

        def tile_pe(si, j):
            return pe0 + si * NT + j + 1

        with nc.Block() as blk:
            @blk.sync
            def _(sync):
                for si, (c0, cw) in enumerate(slices):
                    for kk in range(32):
                        g = si * 32 + kk
                        if g >= 4:
                            sync.wait_ge(k.dv.h, cast_dv(g - 4))
                        sync.dma_start(out=wst[g % 4][:, 0:cw],
                                       in_=k.wbT[kk * 128:(kk + 1) * 128, c0:c0 + cw]).then_inc(k.q[QM[g % 4]].h, 16)
                        k.q[QM[g % 4]].n += 16

            @blk.gpsimd
            def _(g_):
                for si, (c0, cw) in enumerate(slices):
                    for j in range(NT):
                        ej = si * NT + j
                        if ej >= 2:
                            ej2 = ej - 2
                            g_.wait_ge(k.dv.h, evac_dv(ej2 // NT, ej2 % NT))
                        g_.dma_start(out=x1s[j % 2][:, 0:cw],
                                     in_=k.x1_scr[j * 128:(j + 1) * 128, c0:c0 + cw]).then_inc(k.q[8 + ej % 2].h, 16)
                        k.q[8 + ej % 2].n += 16

            @blk.vector
            def _(vector):
                for si, (c0, cw) in enumerate(slices):
                    for kk in range(32):
                        g = si * 32 + kk
                        vector.wait_ge(k.q[QM[g % 4]].h, qb0[g % 4] + 16 * (g // 4 + 1))
                        if si >= 2 and kk == 0:
                            vector.wait_ge(k.pe.h, tile_pe(si - 2, NT - 1))
                        vector.tensor_copy(out=wsb[si % 2][:, kk * CW:kk * CW + cw],
                                           in_=wst[g % 4][:, 0:cw]).then_inc(k.dv.h, 1)
                        k.dv.n += 1
                    for j in range(NT):
                        ej = si * NT + j
                        vector.wait_ge(k.pe.h, tile_pe(si, j))
                        vector.wait_ge(k.q[8 + ej % 2].h, qb2[ej % 2] + 16 * (ej // 2 + 1))
                        if ej >= 2:
                            vector.wait_ge(k.q[4 + ej % 2].h, qb4[ej % 2] + 16 * (ej // 2))
                        vector.tensor_add(out=ost[j % 2][:, 0:cw], in0=x1s[j % 2][:, 0:cw],
                                          in1=k.pf[j % 2][:, 0:cw]).then_inc(k.dv.h, 1)
                        k.dv.n += 1

            @blk.tensor
            def _(tensor):
                for si, (c0, cw) in enumerate(slices):
                    tensor.wait_ge(k.dv.h, dv0 + si * (32 + NT) + 32)
                    for j in range(NT):
                        if si * NT + j >= 2:
                            jj = si * NT + j - 2
                            tensor.wait_ge(k.dv.h, evac_dv(jj // NT, jj % NT))
                        pb = k.pf[j % 2]
                        last = None
                        for kk in range(32):
                            last = tensor.matmul(pb[:, 0:cw], k.yfm(kk)[:, j * 128:(j + 1) * 128],
                                                 wsb[si % 2][:, kk * CW:kk * CW + cw],
                                                 start=(kk == 0), stop=False)
                        tensor.matmul(pb[:, 0:cw], k.hwb[:, j * 128:(j + 1) * 128],
                                      k.bstb[:, c0:c0 + cw], start=False, stop=False)
                        last = tensor.matmul(pb[:, 0:cw], k.ones1[:],
                                             k.bbb[0:1, c0:c0 + cw], start=False, stop=True)
                        last.then_inc(k.pe.h, 1)
                        k.pe.n += 1

            @blk.scalar
            def _(scalar):
                for si, (c0, cw) in enumerate(slices):
                    for j in range(NT):
                        scalar.wait_ge(k.dv.h, evac_dv(si, j))
                        ej = si * NT + j
                        scalar.dma_start(out=k.out_d[j * 128:(j + 1) * 128, c0:c0 + cw],
                                         in_=ost[j % 2][:, 0:cw]).then_inc(k.q[4 + ej % 2].h, 16)
                        k.q[4 + ej % 2].n += 16
    _barrier(k)

def build_nc():
    nc = bass.Bass("TRN2", target_bir_lowering=False, detect_race_conditions=False)
    k = K()
    k.nc = nc

    dr = lambda n, s, d=F32, kind="ExternalInput": nc.dram_tensor(n, s, d, kind=kind).ap()
    k.x0h = dr("x0h", [TH, DM])
    k.w_in = dr("w_in", [DM, PROJ])
    k.w_op = dr("w_op", [DM, DM])
    k.wbT = dr("wbT", [DM, DM])
    k.wrT = dr("wrT", [DM, 8])
    k.aT = dr("aT", [DM, 128])
    k.bstk = dr("bstk", [128, DM])
    k.bb = dr("bb", [1, DM])
    k.conv_fm = dr("conv_fm", [128, 136])
    k.lnw_fm = dr("lnw_fm", [128, 32])
    k.nrmw_fm = dr("nrmw_fm", [128, 32])
    k.mupz = dr("mupz", [128, DM])
    k.mup_fm = dr("mup_fm", [128, 35])
    k.dt_bias = dr("dt_bias", [64, 1])
    k.a_log = dr("a_log", [64, 1])
    k.d_bc = dr("d_bc", [128, 64])
    k.d_row = dr("d_row", [1, DM])
    k.hmask = dr("hmask", [128, 1])
    k.ident_bf_d = dr("ident_bf", [128, 128], BF16)
    k.ident_f_d = dr("ident_f", [128, 128], F32)
    k.ones1_d = dr("ones1", [1, 128], BF16)
    k.r64_d = dr("r64", [64, DM], BF16)
    k.r8_d = dr("r8", [8, 128], BF16)
    k.m4_d = dr("m4", [128, 512], F32)
    k.ones64_d = dr("ones64", [64, 128], F32)
    k.out_d = dr("out", [TR, DM], F32, kind="ExternalOutput")

    k.z_scr = nc.dram_tensor("z_scr", [TR, DM], F32).ap()
    k.xbc_scr = nc.dram_tensor("xbc_scr", [CONVD, TH], F32).ap()
    k.y_scr = nc.dram_tensor("y_scr", [TR, DM], F32).ap()
    k.attn_scr = nc.dram_tensor("attn_scr", [TR, DM], F32).ap()
    k.x1_scr = nc.dram_tensor("x1_scr", [TR, DM], F32).ap()
    k.vsp_scr = nc.dram_tensor("vsp_scr", [64, 3 * TR], BF16).ap()
    k.csp_scr = nc.dram_tensor("csp_scr", [64, 3 * TR], BF16).ap()
    k.dt_scr = nc.dram_tensor("dt_scr", [64, TH], F32).ap()
    k.xtm_scr = nc.dram_tensor("xtm_scr", [TR, DM], BF16).ap()
    k.cc_in = nc.dram_tensor("cc_in", [128, DM], F32)
    k.cc_out = nc.dram_tensor("cc_out", [256, DM], F32)

    dbg_outs = {}
    k.dbgG = None
    k.dbgCB = None
    for name, shape in [("z", [TR, DM]), ("xbc", [CONVD, TH]), ("y", [TR, DM]),
                        ("attn", [TR, DM]), ("x1", [TR, DM]), ("H", [128, DM]),
                        ("dt", [64, TH]), ("xtm", [TR, DM]), ("xn", [128, 69632]), ("G", [128, 2048]), ("CB", [128, 512])]:
        if name in DBG:
            dbg_outs[name] = dr("dbg_" + name, shape, F32, kind="ExternalOutput")
            if name == "G":
                k.dbgG = dbg_outs[name]
            if name == "CB":
                k.dbgCB = dbg_outs[name]

    with ExitStack() as top:
        e = top.enter_context
        k.ld = Sem(top, nc, "ld")
        k.lg = Sem(top, nc, "lg")
        k.ls = Sem(top, nc, "ls")
        k.out = Sem(top, nc, "outs")
        k.pe = Sem(top, nc, "pe")
        k.dv = Sem(top, nc, "dv")
        k.ac = Sem(top, nc, "ac")
        k.po = Sem(top, nc, "po")
        k.cc = Sem(top, nc, "cc")
        k.q = [Sem(top, nc, f"q{i}") for i in range(12)]

        k.pf = [e(nc.psum_tensor(f"pf{i}", [128, 512], F32)) for i in range(6)]
        k.pbf = [e(nc.psum_tensor(f"pbf{i}", [128, 1024], BF16)) for i in range(2)]

        k.ident_bf = e(nc.sbuf_tensor("identbf", [128, 128], BF16))
        k.ident_f = e(nc.sbuf_tensor("identf", [128, 128], F32))
        k.ones1 = e(nc.sbuf_tensor("sones1", [1, 128], BF16))
        k.r8 = e(nc.sbuf_tensor("sr8", [8, 128], BF16))
        k.ones64 = e(nc.sbuf_tensor("sones64", [64, 128], F32))
        k.lnw = e(nc.sbuf_tensor("slnw", [128, 32], F32))
        k.nrmw = e(nc.sbuf_tensor("snrmw", [128, 32], F32))
        k.mupf = e(nc.sbuf_tensor("smupf", [128, 35], F32))
        k.convw = e(nc.sbuf_tensor("sconvw", [128, 136], F32))
        k.dbias = e(nc.sbuf_tensor("sdbias", [64, 1], F32))
        k.aA = e(nc.sbuf_tensor("saA", [64, 1], F32))
        k.dbc = e(nc.sbuf_tensor("sdbc", [128, 64], F32))
        k.hm = e(nc.sbuf_tensor("shm", [128, 1], F32))

        k.xn = lambda kk: k.arena[:, kk * TH:(kk + 1) * TH]

        with nc.Block() as blk:
            @blk.sync
            def _(sync):
                for dst, src in [(k.ident_bf, k.ident_bf_d), (k.ident_f, k.ident_f_d),
                                 (k.ones1, k.ones1_d), (k.r8, k.r8_d),
                                 (k.ones64, k.ones64_d), (k.lnw, k.lnw_fm),
                                 (k.nrmw, k.nrmw_fm), (k.mupf, k.mup_fm), (k.convw, k.conv_fm),
                                 (k.dbias, k.dt_bias), (k.aA, k.a_log), (k.dbc, k.d_bc),
                                 (k.hm, k.hmask)]:
                    sync.dma_start(out=dst[:], in_=src).then_inc(k.ld.h, 16)
                    k.ld.n += 16

            @blk.scalar
            def _(scalar):
                scalar.wait_ge(k.ld.h, k.ld.n)
                scalar.activation(out=k.aA[:], in_=k.aA[:], func=AF.Exp).then_inc(k.ac.h, 1)
                k.ac.n += 1

            @blk.vector
            def _(vector):
                vector.wait_ge(k.ac.h, k.ac.n)
                vector.tensor_scalar(out=k.aA[:], in0=k.aA[:], scalar1=-1.0, scalar2=None,
                                     op0=ALU.mult).then_inc(k.dv.h, 1)
                k.dv.n += 1
        _barrier(k)

        with ExitStack() as scA:
            k.arena = scA.enter_context(nc.sbuf_tensor("arena", [128, 69632], BF16))
            _ln_phase(k, k.x0h, lambda kk, tc: k.xn(kk)[:, tc * 128:(tc + 1) * 128], 17, with_rms=True)
            _barrier(k)
            if "xn" in DBG:
                with nc.Block() as blkx:
                    @blkx.gpsimd
                    def _(g):
                        g.dma_start(out=dbg_outs["xn"], in_=k.arena[:, 0:69632]).then_inc(k.lg.h, 16)
                        k.lg.n += 16
                        g.wait_ge(k.lg.h, k.lg.n)
                _barrier(k)
            if STOP == "P1":
                _dbg_dump(k, dbg_outs)
                return nc, dbg_outs

            _p2_inproj(k)
            _barrier(k)
        if STOP == "P2":
            _dbg_dump(k, dbg_outs)
            return nc, dbg_outs

        with ExitStack() as scP:
            eP = scP.enter_context
            k.bC = eP(nc.sbuf_tensor("bC", [128, TR], BF16))
            k.egl_bf = eP(nc.sbuf_tensor("eglbf", [64, TR], BF16))
            k.r64 = eP(nc.sbuf_tensor("sr64", [64, DM], BF16))
            with nc.Block() as blk0:
                @blk0.sync
                def _(sync):
                    sync.dma_start(out=k.r64[:], in_=k.r64_d).then_inc(k.ld.h, 16)
                    k.ld.n += 16

            with ExitStack() as scB:
                eB = scB.enter_context
                k.dt_t = eB(nc.sbuf_tensor("dtT", [64, TR], F32))
                k.cl_t = eB(nc.sbuf_tensor("clT", [64, TR], F32))
                k.bB = eB(nc.sbuf_tensor("bB", [128, TR], BF16))
                k.bBt = eB(nc.sbuf_tensor("bBt", [128, TR], BF16))
                k.t64g = eB(nc.sbuf_tensor("t64g", [64, 64], F32))
                _barrier(k)
                _p3_dt(k)
                _p3_conv(k)
                if STOP == "P3" or "xtm" in DBG:
                    with nc.Block() as blkx:
                        @blkx.gpsimd
                        def _(g):
                            if "xtm" in DBG:
                                g.dma_start(out=dbg_outs["xtm"], in_=k.xtm_scr).then_inc(k.lg.h, 16)
                                k.lg.n += 16
                            g.wait_ge(k.lg.h, k.lg.n)
                    _barrier(k)
                if STOP == "P3":
                    _dbg_dump(k, dbg_outs)
                    return nc, dbg_outs

                _p4_ssd(k)
                if STOP == "P4" and "H" in DBG:
                    with nc.Block() as blkh:
                        @blkh.gpsimd
                        def _(g):
                            g.dma_start(out=dbg_outs["H"], in_=k.H[:]).then_inc(k.lg.h, 16)
                            k.lg.n += 16
                            g.wait_ge(k.lg.h, k.lg.n)
                    _barrier(k)
                if STOP == "P4":
                    _dbg_dump(k, dbg_outs)
                    return nc, dbg_outs

            k.arena2 = e(nc.sbuf_tensor("arena2", [128, 65536], BF16, side="right"))
            k.yfm = lambda kk: k.arena2[:, kk * TR:(kk + 1) * TR]
            _p5_gate(k)
            if STOP == "P5p":
                _dbg_dump(k, dbg_outs)
                return nc, dbg_outs
            if STOP == "P5":
                _dbg_dump(k, dbg_outs)
                return nc, dbg_outs

        _p6_outproj(k)
        if STOP == "P6":
            _dbg_dump(k, dbg_outs)
            return nc, dbg_outs

        _ln_phase(k, k.x0h[128:TH, :], lambda kk, tc: k.yfm(kk)[:, tc * 128:(tc + 1) * 128],
                  16, with_rms=False, x1_mode=True, tag="b")
        _barrier(k)
        if STOP == "P7a":
            _dbg_dump(k, dbg_outs)
            return nc, dbg_outs

        k.bstb = e(nc.sbuf_tensor("bstb", [128, DM], BF16))
        k.bbb = e(nc.sbuf_tensor("bbb", [1, DM], BF16))
        k.hwb = e(nc.sbuf_tensor("hwb", [128, TR], BF16))
        _p7_router(k)
        if STOP == "P7":
            _dbg_dump(k, dbg_outs)
            return nc, dbg_outs

        _p8_ffn(k)
        _dbg_dump(k, dbg_outs)
    return nc, dbg_outs


def _dbg_dump(k, dbg_outs):
    nc = k.nc
    with nc.Block() as blk:
        @blk.gpsimd
        def _(g):
            for s in (k.ld, k.ls, k.lg, k.dv, k.pe, k.ac):
                if s.n:
                    g.wait_ge(s.h, s.n)
            for name, d in dbg_outs.items():
                src = {"z": k.z_scr, "xbc": k.xbc_scr, "y": k.y_scr,
                       "attn": k.attn_scr, "x1": k.x1_scr}.get(name)
                if src is not None:
                    g.dma_start(out=d, in_=src).then_inc(k.lg.h, 16)
                    k.lg.n += 16
                elif name == "dt":
                    g.dma_start(out=d, in_=k.dt_scr).then_inc(k.lg.h, 16)
                    k.lg.n += 16

            g.wait_ge(k.lg.h, k.lg.n)
            if k.out.n:
                g.wait_ge(k.out.h, k.out.n)


def _host_inputs(inp):
    x0 = np.asarray(inp["L0_attn_hidden_states"])
    ln_w = np.asarray(inp["L0_attn_ln_w"])
    w_in = np.ascontiguousarray(np.asarray(inp["L0_attn_in_proj_w"], np.float32))
    conv_w = np.asarray(inp["L0_attn_conv_w"])
    a_log = np.asarray(inp["L0_attn_A_log"])
    dt_b = np.asarray(inp["L0_attn_dt_bias"])
    Dv = np.asarray(inp["L0_attn_D"])
    norm_w = np.asarray(inp["L0_attn_norm_w"])
    w_op = np.ascontiguousarray(np.asarray(inp["L0_attn_out_proj_w"], np.float32))
    mup = np.asarray(inp["L0_attn_mup_vector"])
    wb = np.asarray(inp["L0_ffn_W_base"])
    bb = np.asarray(inp["L0_ffn_b_base"])
    wr = np.asarray(inp["L0_ffn_W_router"])
    Am = np.asarray(inp["L0_ffn_A"])
    Bm = np.asarray(inp["L0_ffn_B"])

    shared = {
        "w_in": w_in,
        "w_op": w_op,
        "wbT": np.ascontiguousarray(wb.T.astype(np.float32)),
        "wrT": np.ascontiguousarray(wr.T.astype(np.float32)),
        "aT": np.ascontiguousarray(Am.reshape(128, DM).T.astype(np.float32)),
        "bstk": np.ascontiguousarray(Bm.transpose(0, 2, 1).reshape(128, DM).astype(np.float32)),
        "bb": np.ascontiguousarray(bb.reshape(1, DM).astype(np.float32)),
        "conv_fm": np.ascontiguousarray(conv_w.reshape(34, 128, 4).transpose(1, 0, 2).reshape(128, 136).astype(np.float32)),
        "lnw_fm": np.ascontiguousarray(ln_w.reshape(32, 128).T.astype(np.float32)),
        "nrmw_fm": np.ascontiguousarray(norm_w.reshape(32, 128).T.astype(np.float32)),
        "mupz": np.ascontiguousarray(np.broadcast_to(mup[:DM], (128, DM)).astype(np.float32)),
        "mup_fm": np.ascontiguousarray(np.pad(mup[DM:].astype(np.float32), (0, 64)).reshape(35, 128).T),
        "dt_bias": np.ascontiguousarray(dt_b.reshape(64, 1).astype(np.float32)),
        "a_log": np.ascontiguousarray(a_log.reshape(64, 1).astype(np.float32)),
        "d_bc": np.ascontiguousarray(np.broadcast_to(np.asarray(Dv)[None, :], (128, 64)).astype(np.float32)),
        "d_row": np.ascontiguousarray(np.repeat(np.asarray(Dv), 64).reshape(1, DM).astype(np.float32)),
        "ident_bf": np.eye(128, dtype=BF),
        "ident_f": np.eye(128, dtype=np.float32),
        "ones1": np.ones((1, 128), dtype=BF),
        "r64": np.ascontiguousarray(np.repeat(np.eye(64, dtype=np.float32), 64, axis=1).astype(BF)),
        "r8": np.ascontiguousarray(np.repeat(np.eye(8, dtype=np.float32), 16, axis=1).astype(BF)),
        "m4": np.ascontiguousarray(np.tile(np.where(np.arange(128)[:, None] <= np.arange(128)[None, :], 0.0, -1e30).astype(np.float32), (1, 4))),
        "ones64": np.ones((64, 128), dtype=np.float32),
    }
    maps = []
    for c in range(8):
        b, h = c // 2, c % 2
        if h == 0:
            xh = np.concatenate([np.zeros((128, DM), np.float32), np.asarray(x0[b, 0:TR], np.float32)], axis=0)
        else:
            xh = np.ascontiguousarray(np.asarray(x0[b, TR - 128:2 * TR], np.float32))
        m = dict(shared)
        m["x0h"] = xh
        m["hmask"] = np.full((128, 1), float(h), np.float32)
        maps.append(m)
    return maps


def kernel(**inputs):
    maps = _host_inputs(inputs)
    nc, _ = build_nc()
    res = run_bass_kernel_spmd(nc, maps, core_ids=list(range(8)))
    out = np.empty((4, 4096, DM), np.float32)
    for c in range(8):
        b, h = c // 2, c % 2
        out[b, h * TR:(h + 1) * TR] = np.asarray(res.results[c]["out"])
    return out


# version 05
# speedup vs baseline: 1.1135x; 1.0266x over previous
import numpy as np
import ml_dtypes
from contextlib import ExitStack

import concourse.bass as bass
import concourse.mybir as mybir
from concourse.bass_utils import run_bass_kernel_spmd

F32 = mybir.dt.float32
BF16 = mybir.dt.bfloat16
AF = mybir.ActivationFunctionType
ALU = mybir.AluOpType
AX = mybir.AxisListType.X
BF = ml_dtypes.bfloat16

DM = 4096
PROJ = 8512
CONVD = 4352
NH, HD, SS = 64, 64, 128
TR = 2048
TH = 2176
NCH = 16
EPS = 1e-5
P5LIM = 16
P7PART = 3

DBG = set()
STOP = None
NCORES = 8


class Sem:
    def __init__(self, ctx, nc, name):
        self.h = ctx.enter_context(nc.semaphore(name))
        self.n = 0

    def inc(self, instr, k=1):
        instr.then_inc(self.h, k)
        self.n += k
        return self.n


class K:
    pass


def _barrier(k):
    nc = k.nc
    sems = [k.ld, k.lg, k.ls, k.pe, k.dv, k.ac, k.po, k.cc, k.out] + list(k.q)
    vals = [(s.h, s.n) for s in sems if s.n]
    with nc.Block() as blk:
        for eng_name in ("sync", "tensor", "vector", "scalar", "gpsimd"):
            def mk(eng, _v=vals):
                for h, n in _v:
                    eng.wait_ge(h, n)
            getattr(blk, eng_name)(mk)


def _ln_phase(k, src_dram, dst_view, nchunks, with_rms, x1_mode=False, tag=""):
    nc = k.nc
    npc = 15 if x1_mode else 14
    ldpc = 32 if x1_mode else 16
    ld0, dv0, ac0, pe0, ls0 = k.ld.n, k.dv.n, k.ac.n, k.pe.n, k.ls.n
    q0_base = [k.q[0].n, k.q[1].n]
    q2_base = k.q[2].n
    off = 1 if x1_mode else 0

    with ExitStack() as ctx:
        e = ctx.enter_context
        xt = [e(nc.sbuf_tensor(f"xt{tag}{i}", [128, DM], F32)) for i in range(2)]
        xb = [e(nc.sbuf_tensor(f"xb{tag}{i}", [128, DM], BF16)) for i in range(2)]
        at = e(nc.sbuf_tensor(f"at{tag}", [128, DM], F32)) if x1_mode else None
        stats = e(nc.sbuf_tensor(f"lnst{tag}", [128, 48], F32))
        mv = e(nc.sbuf_tensor(f"lnmv{tag}", [128, 8], F32))

        with nc.Block() as blk:
            @blk.sync
            def _(sync):
                for i in range(nchunks):
                    if i >= 2:
                        sync.wait_ge(k.dv.h, dv0 + npc * (i - 2) + off + 10)
                        if x1_mode:
                            sync.wait_ge(k.ls.h, ls0 + 16 * (i - 1))
                    qs = k.q[i % 2]
                    sync.dma_start(out=xt[i % 2][:], in_=src_dram[i * 128:(i + 1) * 128, :]).then_inc(qs.h, 16)
                    qs.n += 16
                    if x1_mode:
                        if i >= 1:
                            sync.wait_ge(k.dv.h, dv0 + npc * (i - 1) + 1)
                        sync.dma_start(out=at[:], in_=k.attn_scr[i * 128:(i + 1) * 128, :]).then_inc(k.q[2].h, 16)
                        k.q[2].n += 16

            @blk.vector
            def _(vector):
                for i in range(nchunks):
                    sl = i % 2
                    base = dv0 + npc * i
                    vector.wait_ge(k.q[i % 2].h, q0_base[i % 2] + 16 * (i // 2 + 1))
                    if x1_mode:
                        vector.wait_ge(k.q[2].h, q2_base + 16 * (i + 1))
                        vector.tensor_add(out=xt[sl][:], in0=xt[sl][:], in1=at[:]).then_inc(k.dv.h, 1)
                        k.dv.n += 1
                    for q in range(8):
                        ins_ = vector.bn_stats(out=stats[:, q * 6:(q + 1) * 6],
                                               in_=xt[sl][:, q * 512:(q + 1) * 512])
                    ins_.then_inc(k.dv.h, 1)
                    k.dv.n += 1
                    vector.wait_ge(k.dv.h, base + off + 1)
                    def _step(ins_):
                        ins_.then_inc(k.dv.h, 1)
                        k.dv.n += 1
                        k._tmpc += 1
                        vector.wait_ge(k.dv.h, k.dv.n)
                    k._tmpc = 0
                    _step(vector.bn_aggr(out=mv[:, 0:2], in_=stats[:, 0:48]))
                    _step(vector.tensor_scalar(out=mv[:, 2:3], in0=mv[:, 1:2], scalar1=EPS, scalar2=None, op0=ALU.add))
                    _step(vector.reciprocal(out=mv[:, 3:4], in_=mv[:, 2:3]))
                    if with_rms:
                        _step(vector.tensor_tensor(out=mv[:, 4:5], in0=mv[:, 1:2], in1=mv[:, 3:4], op=ALU.mult))
                        _step(vector.tensor_scalar(out=mv[:, 4:5], in0=mv[:, 4:5], scalar1=EPS, scalar2=None, op0=ALU.add))
                        _step(vector.reciprocal(out=mv[:, 5:6], in_=mv[:, 4:5]))
                    else:
                        for _ in range(3):
                            _step(vector.engine_nop())
                    vector.engine_nop().then_inc(k.dv.h, 1)
                    k.dv.n += 1
                    vector.wait_ge(k.ac.h, ac0 + (i + 1))
                    if with_rms:
                        vector.tensor_tensor(out=mv[:, 6:7], in0=mv[:, 6:7], in1=mv[:, 7:8], op=ALU.mult).then_inc(k.dv.h, 1)
                        k.dv.n += 1
                        vector.wait_ge(k.dv.h, base + off + 9)
                    else:
                        vector.engine_nop().then_inc(k.dv.h, 1)
                        k.dv.n += 1
                    vector.tensor_scalar(out=xb[sl][:], in0=xt[sl][:], scalar1=mv[:, 0:1], scalar2=mv[:, 6:7],
                                         op0=ALU.subtract, op1=ALU.mult).then_inc(k.dv.h, 1)
                    k.dv.n += 1
                    for g in range(4):
                        vector.wait_ge(k.pe.h, pe0 + 32 * i + 8 * (g + 1))
                        pb = k.pbf[g % 2]
                        last = None
                        for j in range(8):
                            kk = g * 8 + j
                            last = vector.tensor_copy(out=dst_view(kk, i), in_=pb[:, j * 128:(j + 1) * 128])
                        last.then_inc(k.dv.h, 1)
                        k.dv.n += 1

            @blk.scalar
            def _(scalar):
                for i in range(nchunks):
                    base = dv0 + npc * i
                    scalar.wait_ge(k.dv.h, base + off + 8)
                    if with_rms:
                        scalar.activation(out=mv[:, 6:7], in_=mv[:, 3:4], func=AF.Sqrt)
                        scalar.activation(out=mv[:, 7:8], in_=mv[:, 5:6], func=AF.Sqrt).then_inc(k.ac.h, 1)
                    else:
                        scalar.activation(out=mv[:, 6:7], in_=mv[:, 3:4], func=AF.Sqrt).then_inc(k.ac.h, 1)
                    k.ac.n += 1
                    if x1_mode:
                        scalar.wait_ge(k.dv.h, base + 1)
                        scalar.dma_start(out=k.x1_scr[i * 128:(i + 1) * 128, :], in_=xt[i % 2][:]).then_inc(k.ls.h, 16)
                        k.ls.n += 16

            @blk.tensor
            def _(tensor):
                for i in range(nchunks):
                    base = dv0 + npc * i
                    tensor.wait_ge(k.dv.h, base + off + 10)
                    for g in range(4):
                        gg = 4 * i + g
                        if gg >= 2:
                            i2, g2 = (gg - 2) // 4, (gg - 2) % 4
                            tensor.wait_ge(k.dv.h, dv0 + npc * i2 + off + 10 + g2 + 1)
                        pb = k.pbf[g % 2]
                        last = None
                        for j in range(8):
                            kk = g * 8 + j
                            last = tensor.transpose(pb[:, j * 128:(j + 1) * 128],
                                                    xb[i % 2][:, kk * 128:(kk + 1) * 128], k.ident_bf[:])
                        last.then_inc(k.pe.h, 8)
                        k.pe.n += 8


def _p2_inproj(k):
    nc = k.nc
    ZT = list(range(1, 17))
    TWS = [(0, 512), (512, 512), (1024, 512), (1536, 512), (2048, 128)]
    CW = 384
    FM0 = 11
    slices = [(s * CW, CW, "z") for s in range(10)] + [(3840, 256, "z")] + \
             [(4096 + s * CW, CW, "fm") for s in range(11)] + [(8320, 192, "fm")]

    def nevac(sl):
        c0, cw, kind = sl
        return 16 if kind == "z" else ((cw + 127) // 128) * 5

    def ntile(sl):
        return nevac(sl)

    dcum = [0]
    for sl in slices:
        dcum.append(dcum[-1] + 32 + nevac(sl))
    pcum = [0]
    for sl in slices:
        pcum.append(pcum[-1] + ntile(sl))

    def cast_dv(g):
        si, kk = g // 32, g % 32
        return dcum[si] + kk + 1

    def evac_dv(si, e):
        return dcum[si] + 32 + e + 1

    def tile_pe(si, j):
        return pcum[si] + j + 1

    ld0, dv0, pe0, ls0 = k.ld.n, k.dv.n, k.pe.n, k.ls.n
    QM = [0, 1, 10, 11]
    qb2 = [k.q[i].n for i in QM]
    qb3 = [k.q[2].n, k.q[3].n]
    qb4 = [k.q[4].n, k.q[5].n]
    k.fm_evac_ctr = 0
    k.fm_store_ctr = 0

    with ExitStack() as ctx:
        e = ctx.enter_context
        wst = [e(nc.sbuf_tensor(f"wst{i}", [128, 384], F32)) for i in range(4)]
        wsb = [e(nc.sbuf_tensor(f"wsb{i}", [128, 32 * 384], BF16)) for i in range(2)]
        zst = [e(nc.sbuf_tensor(f"zst{i}", [128, 384], F32)) for i in range(2)]
        fst = [e(nc.sbuf_tensor(f"fst{i}", [128, 512], F32)) for i in range(2)]

        with nc.Block() as blk:
            @blk.sync
            def _(sync):
                for si, (c0, cw, kind) in enumerate(slices):
                    for kk in range(32):
                        g = si * 32 + kk
                        if g >= 4:
                            sync.wait_ge(k.dv.h, dv0 + cast_dv(g - 4))
                        sync.dma_start(out=wst[g % 4][:, 0:cw],
                                       in_=k.w_in[kk * 128:(kk + 1) * 128, c0:c0 + cw]).then_inc(k.q[QM[g % 4]].h, 16)
                        k.q[QM[g % 4]].n += 16

            @blk.vector
            def _(vector):
                for si, (c0, cw, kind) in enumerate(slices):
                    for kk in range(32):
                        g = si * 32 + kk
                        vector.wait_ge(k.q[QM[g % 4]].h, qb2[g % 4] + 16 * (g // 4 + 1))
                        if si >= 2 and kk == 0:
                            vector.wait_ge(k.pe.h, pe0 + pcum[si - 1])
                        vector.tensor_scalar(out=wsb[si % 2][:, kk * CW:kk * CW + cw],
                                             in0=wst[g % 4][:, 0:cw],
                                             scalar1=k.lnw[:, kk:kk + 1], scalar2=None,
                                             op0=ALU.mult).then_inc(k.dv.h, 1)
                        k.dv.n += 1
                    if kind == "z":
                        for j in range(16):
                            ze = si * 16 + j
                            vector.wait_ge(k.pe.h, pe0 + tile_pe(si, j))
                            if ze >= 2:
                                vector.wait_ge(k.q[2 + ze % 2].h, qb3[ze % 2] + 16 * (ze // 2))
                            pb = k.pf[j % 2]
                            vector.tensor_copy(out=zst[j % 2][:, 0:cw],
                                                 in_=pb[:, 0:cw]).then_inc(k.dv.h, 1)
                            k.dv.n += 1
                    else:
                        nmc = (cw + 127) // 128
                        for m in range(nmc):
                            mw = min(128, cw - m * 128)
                            ch = (c0 + m * 128 - 4096) // 128
                            for j, (t0, tw) in enumerate(TWS):
                                idx = m * 5 + j
                                fe = k.fm_evac_ctr; k.fm_evac_ctr += 1
                                vector.wait_ge(k.pe.h, pe0 + tile_pe(si, idx))
                                if fe >= 2:
                                    vector.wait_ge(k.q[4 + fe % 2].h, qb4[fe % 2] + 16 * (fe // 2))
                                pb = k.pf[2 + idx % 2]
                                vector.tensor_scalar(out=fst[idx % 2][0:mw, 0:tw], in0=pb[0:mw, 0:tw],
                                                     scalar1=k.mupf[0:mw, ch:ch + 1],
                                                     scalar2=None, op0=ALU.mult).then_inc(k.dv.h, 1)
                                k.dv.n += 1

            @blk.tensor
            def _(tensor):
                for si, (c0, cw, kind) in enumerate(slices):
                    tensor.wait_ge(k.dv.h, dv0 + dcum[si] + 32)
                    if kind == "z":
                        for j, tc in enumerate(ZT):
                            if si * 16 + j >= 2:
                                jj = si * 16 + j - 2
                                si2, j2 = jj // 16, jj % 16
                                tensor.wait_ge(k.dv.h, dv0 + evac_dv(si2, j2))
                            pb = k.pf[j % 2]
                            last = None
                            for kk in range(32):
                                last = tensor.matmul(pb[:, 0:cw], k.xn(kk)[:, tc * 128:(tc + 1) * 128],
                                                     wsb[si % 2][:, kk * CW:kk * CW + cw],
                                                     start=(kk == 0), stop=(kk == 31))
                            last.then_inc(k.pe.h, 1)
                            k.pe.n += 1
                    else:
                        nmc = (cw + 127) // 128
                        for m in range(nmc):
                            mw = min(128, cw - m * 128)
                            for j, (t0, tw) in enumerate(TWS):
                                idx = m * 5 + j
                                gidx = sum(ntile(slices[s2]) for s2 in range(FM0, si)) + idx
                                if gidx >= 2:
                                    pj = gidx - 2
                                    acc = 0
                                    for s2 in range(FM0, len(slices)):
                                        nt = ntile(slices[s2])
                                        if pj < acc + nt:
                                            tensor.wait_ge(k.dv.h, dv0 + evac_dv(s2, pj - acc))
                                            break
                                        acc += nt
                                pb = k.pf[2 + idx % 2]
                                last = None
                                for kk in range(32):
                                    last = tensor.matmul(pb[0:mw, 0:tw],
                                                         wsb[si % 2][:, kk * CW + m * 128: kk * CW + m * 128 + mw],
                                                         k.xn(kk)[:, t0:t0 + tw],
                                                         start=(kk == 0), stop=(kk == 31))
                                last.then_inc(k.pe.h, 1)
                                k.pe.n += 1

            @blk.scalar
            def _(scalar):
                for si, (c0, cw, kind) in enumerate(slices):
                    if kind == "z":
                        for j, tc in enumerate(ZT):
                            scalar.wait_ge(k.dv.h, dv0 + evac_dv(si, j))
                            ze2 = si * 16 + j
                            scalar.dma_start(out=k.z_scr[(tc - 1) * 128:tc * 128, c0:c0 + cw],
                                             in_=zst[j % 2][:, 0:cw]).then_inc(k.q[2 + ze2 % 2].h, 16)
                            k.q[2 + ze2 % 2].n += 16
                    else:
                        nmc = (cw + 127) // 128
                        for m in range(nmc):
                            mw = min(128, cw - m * 128)
                            ch0 = c0 + m * 128 - 4096
                            for j, (t0, tw) in enumerate(TWS):
                                idx = m * 5 + j
                                scalar.wait_ge(k.dv.h, dv0 + evac_dv(si, idx))
                                fe2 = k.fm_store_ctr; k.fm_store_ctr += 1
                                if ch0 < CONVD:
                                    scalar.dma_start(out=k.xbc_scr[ch0:ch0 + mw, t0:t0 + tw],
                                                     in_=fst[idx % 2][0:mw, 0:tw]).then_inc(k.q[4 + fe2 % 2].h, 16)
                                else:
                                    scalar.dma_start(out=k.dt_scr[0:64, t0:t0 + tw],
                                                     in_=fst[idx % 2][0:64, 0:tw]).then_inc(k.q[4 + fe2 % 2].h, 16)
                                k.q[4 + fe2 % 2].n += 16




def _p3_dt(k):
    nc = k.nc
    with ExitStack() as ctx:
        e = ctx.enter_context
        dtf = e(nc.sbuf_tensor("dtf", [64, TH], F32))
        tA = e(nc.sbuf_tensor("tA", [64, TR], F32))
        tB = e(nc.sbuf_tensor("tB", [64, TR], F32))
        sp = e(nc.sbuf_tensor("sp", [64, TR], BF16))
        ldt = dtf[:, 0:TR]
        spf = dtf[:, 0:TR]
        ld0, dv0, ac0, lg0 = k.ld.n, k.dv.n, k.ac.n, k.lg.n
        t64 = k.t64g
        k._sp_pos = []

        with nc.Block() as blk:
            @blk.sync
            def _(sync):
                sync.dma_start(out=dtf[:], in_=k.dt_scr).then_inc(k.ld.h, 16)
                k.ld.n += 16

            @blk.vector
            def _(vector):
                def dstep(ins_):
                    ins_.then_inc(k.dv.h, 1)
                    k.dv.n += 1
                    vector.wait_ge(k.dv.h, k.dv.n)
                vector.wait_ge(k.ld.h, ld0 + 16)
                dstep(vector.tensor_scalar(out=tA[:], in0=dtf[:, 128:TH], scalar1=k.dbias[:, 0:1],
                                           scalar2=None, op0=ALU.add))
                dstep(vector.tensor_scalar(out=tB[:], in0=tA[:], scalar1=0.0, scalar2=None, op0=ALU.max))
                dstep(vector.engine_nop())
                vector.wait_ge(k.ac.h, ac0 + 3)
                dstep(vector.tensor_add(out=k.dt_t[:], in0=tB[:], in1=ldt[:]))
                dstep(vector.tensor_scalar(out=tB[:], in0=k.dt_t[:], scalar1=1e-30, scalar2=None, op0=ALU.max))
                dstep(vector.tensor_scalar(out=tA[:], in0=k.dt_t[:], scalar1=k.aA[:, 0:1],
                                           scalar2=None, op0=ALU.mult))
                for c in range(NCH):
                    dstep(vector.tensor_tensor_scan(out=k.cl_t[:, c * 128:(c + 1) * 128],
                                                    data0=k.ones64[:, 0:128],
                                                    data1=tA[:, c * 128:(c + 1) * 128],
                                                    initial=0.0, op0=ALU.mult, op1=ALU.add))
                dstep(vector.memset(t64[:, 16:17], 0.0))
                for c in range(NCH):
                    dstep(vector.tensor_copy(out=t64[:, c:c + 1], in_=k.cl_t[:, c * 128 + 127:c * 128 + 128]))
                for c in range(1, NCH):
                    dstep(vector.tensor_add(out=t64[:, 16 + c:17 + c], in0=t64[:, 15 + c:16 + c],
                                            in1=t64[:, c - 1:c]))
                vector.wait_ge(k.ac.h, ac0 + 4)
                for c in range(NCH):
                    dstep(vector.tensor_scalar(out=tB[:, c * 128:(c + 1) * 128],
                                               in0=k.cl_t[:, c * 128:(c + 1) * 128],
                                               scalar1=t64[:, 16 + c:17 + c], scalar2=None, op0=ALU.add))
                vector.wait_ge(k.ac.h, ac0 + 5)
                dstep(vector.tensor_copy(out=k.egl_bf[:], in_=tA[:]))
                dstep(vector.tensor_sub(out=tB[:], in0=ldt[:], in1=k.cl_t[:]))
                nd = 0
                for which, val in enumerate((tB, k.cl_t)):
                    cur = val
                    for part in range(3):
                        if nd:
                            vector.wait_ge(k.lg.h, lg0 + 16 * nd)
                        dstep(vector.tensor_copy(out=sp[:], in_=cur[:]))
                        k._sp_pos.append(k.dv.n)
                        nd += 1
                        if part < 2:
                            dstep(vector.tensor_copy(out=spf[:], in_=sp[:]))
                            dstep(vector.tensor_sub(out=tA[:], in0=cur[:], in1=spf[:]))
                            cur = tA
                dstep(vector.tensor_copy(out=t64[:, 32:48], in_=t64[:, 0:16]))
                vector.wait_ge(k.ac.h, ac0 + 6)
                vector.wait_ge(k.lg.h, lg0 + 16 * nd)

            @blk.scalar
            def _(scalar):
                def astep(ins_):
                    ins_.then_inc(k.ac.h, 1)
                    k.ac.n += 1
                    scalar.wait_ge(k.ac.h, k.ac.n)
                scalar.wait_ge(k.dv.h, dv0 + 3)
                astep(scalar.activation(out=ldt[:], in_=tA[:], func=AF.Abs))
                astep(scalar.activation(out=tA[:], in_=ldt[:], func=AF.Exp, scale=-1.0))
                astep(scalar.activation(out=ldt[:], in_=tA[:], func=AF.Ln, bias=1.0))
                scalar.wait_ge(k.dv.h, dv0 + 5)
                astep(scalar.activation(out=ldt[:], in_=tB[:], func=AF.Ln))
                scalar.wait_ge(k.dv.h, dv0 + 70)
                astep(scalar.activation(out=tA[:], in_=tB[:], func=AF.Exp))
                scalar.wait_ge(k.dv.h, dv0 + 87)
                astep(scalar.activation(out=t64[:, 48:64], in_=t64[:, 32:48], func=AF.Exp))

            @blk.gpsimd
            def _(g):
                for di, dvpos in enumerate(k._sp_pos):
                    which, part = di // 3, di % 3
                    scr = k.vsp_scr if which == 0 else k.csp_scr
                    g.wait_ge(k.dv.h, dvpos)
                    g.dma_start(out=scr[:, part * TR:(part + 1) * TR], in_=sp[:]).then_inc(k.lg.h, 16)
                    k.lg.n += 16
                g.wait_ge(k.lg.h, k.lg.n)
    _barrier(k)


def _p3_conv(k):
    nc = k.nc
    with ExitStack() as ctx:
        e = ctx.enter_context
        cbuf = [e(nc.sbuf_tensor(f"cbuf{i}", [128, TH - 124], F32)) for i in range(2)]
        co = [e(nc.sbuf_tensor(f"co{i}", [128, TR], F32)) for i in range(2)]
        cs = [e(nc.sbuf_tensor(f"cs{i}", [128, TR], F32)) for i in range(2)]
        xsb = [e(nc.sbuf_tensor(f"xsb{i}", [128, TR], BF16)) for i in range(2)]
        xts = [e(nc.sbuf_tensor(f"xts{i}", [128, TR], BF16)) for i in range(2)]
        qb0 = [k.q[0].n, k.q[1].n]
        qb2 = [k.q[2].n, k.q[3].n]
        dv0, pe0, ac0, po0 = k.dv.n, k.pe.n, k.ac.n, k.po.n
        tap4 = {}
        castp = {}
        evacg = {}
        ndv = dv0
        npo = po0
        for q in range(34):
            ndv += 4
            tap4[q] = ndv
            npo += 1
            castp[q] = npo
            qq = q - 1
            if 0 <= qq <= 32:
                for g in range(2):
                    ndv += 1
                    evacg[(qq, g)] = ndv

        with nc.Block() as blk:
            @blk.sync
            def _(sync):
                for q in range(34):
                    if q >= 2:
                        sync.wait_ge(k.dv.h, tap4[q - 2])
                    sync.dma_start(out=cbuf[q % 2][:], in_=k.xbc_scr[q * 128:(q + 1) * 128, 124:TH]).then_inc(k.q[q % 2].h, 16)
                    k.q[q % 2].n += 16

            @blk.vector
            def _(vector):
                for q in range(34):
                    cb = cbuf[q % 2]
                    c0_, c1_ = co[q % 2], cs[q % 2]
                    vector.wait_ge(k.q[q % 2].h, qb0[q % 2] + 16 * (q // 2 + 1))
                    if q >= 2:
                        vector.wait_ge(k.po.h, castp[q - 2])
                    vector.tensor_scalar(out=c0_[:], in0=cb[:, 1:1 + TR], scalar1=k.convw[:, 4 * q:4 * q + 1],
                                         scalar2=None, op0=ALU.mult).then_inc(k.dv.h, 1)
                    k.dv.n += 1
                    vector.scalar_tensor_tensor(out=c1_[:], in0=cb[:, 2:2 + TR], scalar=k.convw[:, 4 * q + 1:4 * q + 2],
                                                in1=c0_[:], op0=ALU.mult, op1=ALU.add).then_inc(k.dv.h, 1)
                    k.dv.n += 1
                    vector.scalar_tensor_tensor(out=c0_[:], in0=cb[:, 3:3 + TR], scalar=k.convw[:, 4 * q + 2:4 * q + 3],
                                                in1=c1_[:], op0=ALU.mult, op1=ALU.add).then_inc(k.dv.h, 1)
                    k.dv.n += 1
                    vector.scalar_tensor_tensor(out=c1_[:], in0=cb[:, 4:4 + TR], scalar=k.convw[:, 4 * q + 3:4 * q + 4],
                                                in1=c0_[:], op0=ALU.mult, op1=ALU.add).then_inc(k.dv.h, 1)
                    k.dv.n += 1
                    qq = q - 1
                    if 0 <= qq <= 32:
                        for g in range(2):
                            vector.wait_ge(k.pe.h, pe0 + 16 * qq + 8 * (g + 1))
                            if qq >= 2:
                                vector.wait_ge(k.q[2 + qq % 2].h, qb2[qq % 2] + 256 * (qq // 2))
                            pb = k.pbf[g % 2]
                            last = None
                            for j in range(8):
                                tc = g * 8 + j
                                dstv = (xts[qq % 2][:, tc * 128:(tc + 1) * 128] if qq < 32
                                        else k.bBt[:, tc * 128:(tc + 1) * 128])
                                last = vector.tensor_copy(out=dstv, in_=pb[:, j * 128:(j + 1) * 128])
                            last.then_inc(k.dv.h, 1)
                            k.dv.n += 1

            @blk.gpsimd
            def _(g_):
                for q in range(34):
                    g_.wait_ge(k.ac.h, ac0 + q + 1)
                    dst = xsb[q % 2] if q < 32 else (k.bB if q == 32 else k.bC)
                    g_.tensor_copy(out=dst[:], in_=co[q % 2][:]).then_inc(k.po.h, 1)
                    k.po.n += 1

            @blk.scalar
            def _(scalar):
                for q in range(34):
                    scalar.wait_ge(k.dv.h, tap4[q])
                    scalar.activation(out=co[q % 2][:], in_=cs[q % 2][:], func=AF.Silu).then_inc(k.ac.h, 1)
                    k.ac.n += 1
                    qq = q - 1
                    if 0 <= qq < 32:
                        for g in range(2):
                            scalar.wait_ge(k.dv.h, evacg[(qq, g)])
                            for j in range(8):
                                tc = g * 8 + j
                                scalar.dma_start(out=k.xtm_scr[tc * 128:(tc + 1) * 128, qq * 128:(qq + 1) * 128],
                                                 in_=xts[qq % 2][:, tc * 128:(tc + 1) * 128]).then_inc(k.q[2 + qq % 2].h, 16)
                                k.q[2 + qq % 2].n += 16

            @blk.tensor
            def _(tensor):
                for q in range(33):
                    tensor.wait_ge(k.po.h, castp[q])
                    src2 = (xsb[q % 2] if q < 32 else k.bB)
                    for g in range(2):
                        gg = 2 * q + g
                        if gg >= 2:
                            q2, g2 = (gg - 2) // 2, (gg - 2) % 2
                            tensor.wait_ge(k.dv.h, evacg[(q2, g2)])
                        pb = k.pbf[g % 2]
                        last = None
                        for j in range(8):
                            tc = g * 8 + j
                            last = tensor.transpose(pb[:, j * 128:(j + 1) * 128],
                                                    src2[:, tc * 128:(tc + 1) * 128], k.ident_bf[:])
                        last.then_inc(k.pe.h, 8)
                        k.pe.n += 8
    _barrier(k)


def _p4_ssd(k):
    nc = k.nc
    with ExitStack() as ctx:
        e = ctx.enter_context
        xtt = [e(nc.sbuf_tensor(f"xtt{i}", [128, DM], BF16)) for i in range(2)]
        m4b = e(nc.sbuf_tensor("m4b", [128, 512], BF16))
        combL = e(nc.sbuf_tensor("combL", [6, 4096], BF16))
        combR = e(nc.sbuf_tensor("combR", [6, 4096], BF16))
        G = e(nc.sbuf_tensor("G", [128, 8192], BF16))
        estg = e(nc.sbuf_tensor("estg", [128, 1024], F32))
        e2 = e(nc.sbuf_tensor("e2", [128, 1024], F32))
        CBr = e(nc.sbuf_tensor("CBr", [128, 512], F32))
        eclT = e(nc.sbuf_tensor("eclT", [64, 128], BF16))
        wfT = e(nc.sbuf_tensor("wfT", [64, 128], BF16))
        gmT = e(nc.sbuf_tensor("gmT", [64, 128], BF16))
        tmpF = e(nc.sbuf_tensor("tmpF", [64, 128], F32))
        tmpG = e(nc.sbuf_tensor("tmpG", [64, 128], F32))
        tmpG2 = e(nc.sbuf_tensor("tmpG2", [64, 128], F32))
        tmpF2 = e(nc.sbuf_tensor("tmpF2", [64, 128], F32))
        eclb = e(nc.sbuf_tensor("eclb", [128, 1024], BF16))
        hbf = e(nc.sbuf_tensor("hbf", [128, 4096], BF16))
        wx = e(nc.sbuf_tensor("wx", [128, 1024], BF16))
        xds = e(nc.sbuf_tensor("xds", [128, 1024], BF16))
        drep = e(nc.sbuf_tensor("drep", [128, 4096], BF16))
        drow = e(nc.sbuf_tensor("drow", [1, 4096], BF16))
        yst = e(nc.sbuf_tensor("yst", [128, 512], F32))
        yst2 = e(nc.sbuf_tensor("yst2", [128, 1024], F32))
        hmt = e(nc.sbuf_tensor("hmt", [128, 512], F32))
        k.H = e(nc.sbuf_tensor("H", [128, DM], F32))

        with nc.Block() as blk0:
            ac00, pe00 = k.ac.n, k.pe.n

            @blk0.vector
            def _(vector):
                vector.memset(k.H[:], 0.0)
                vector.memset(combL[:], 1.0)
                vector.memset(combR[:], 1.0).then_inc(k.dv.h, 1)
                k.dv.n += 1

            @blk0.gpsimd
            def _(g):
                g.dma_start(out=m4b[:], in_=k.m4_d).then_inc(k.lg.h, 16)
                k.lg.n += 16
                g.dma_start(out=drow[:], in_=k.d_row).then_inc(k.lg.h, 16)
                k.lg.n += 16

            @blk0.tensor
            def _(tensor):
                tensor.wait_ge(k.lg.h, k.lg.n)
                for sl8 in range(8):
                    if sl8 >= 2:
                        tensor.wait_ge(k.ac.h, ac00 + sl8 - 1)
                    tensor.matmul(k.pf[sl8 % 2][:], k.ones1[:], drow[0:1, sl8 * 512:(sl8 + 1) * 512],
                                  start=True, stop=True).then_inc(k.pe.h, 1)
                    k.pe.n += 1

            @blk0.scalar
            def _(scalar):
                for sl8 in range(8):
                    scalar.wait_ge(k.pe.h, pe00 + sl8 + 1)
                    scalar.copy(out=drep[:, sl8 * 512:(sl8 + 1) * 512],
                                in_=k.pf[sl8 % 2][:]).then_inc(k.ac.h, 1)
                    k.ac.n += 1
        _barrier(k)

        ld0, lg0, ls0, dv0, pe0, ac0, po0 = k.ld.n, k.lg.n, k.ls.n, k.dv.n, k.pe.n, k.ac.n, k.po.n
        qb0 = [k.q[0].n, k.q[1].n]
        qb2 = [k.q[6].n, k.q[7].n]
        qb4 = [k.q[4].n, k.q[5].n]
        DV, PE_, AC, PO = 96, 73, 18, 8

        def dvC(c, p):
            return dv0 + DV * c + p

        def peC(c, p):
            return pe0 + PE_ * c + p

        def acC(c, p):
            return ac0 + AC * c + p

        def poC(c, p):
            return po0 + PO * c + p

        pe_cb = 1
        pe_dg = lambda i: 2 + i
        pe_eclB = lambda j: 18 + 4 * j
        pe_xd = lambda j: 19 + 4 * j
        pe_intra = lambda j: 20 + 4 * j
        pe_raw = lambda j: 21 + 4 * j
        pe_wrep = lambda s: 50 + 3 * s
        pe_smm = lambda s: 51 + 3 * s
        pe_gmm = lambda s: 52 + 3 * s
        dv_eclT, dv_tsG, dv_wmul, dv_wfT, dv_gmT = 1, 2, 3, 4, 5
        dv_cbr = 6
        dv_em = lambda i: 9 + 2 * i
        dv_g = lambda i: 10 + 2 * i
        dv_eclb = lambda j: 41 + 4 * j
        dv_xd = lambda j: 42 + 4 * j
        dv_t1 = lambda j: 43 + 4 * j
        dv_y = lambda j: 44 + 4 * j
        dv_wx = lambda s: 73 + 3 * s
        dv_hm = lambda s: 74 + 3 * s
        dv_hadd = lambda s: 75 + 3 * s
        po_hbf = lambda j: 1 + j
        ac_e1, ac_e3 = 1, 2
        ac_exp = lambda i: 3 + i

        with nc.Block() as blk:
            @blk.sync
            def _(sync):
                for c in range(NCH):
                    if c >= 2:
                        sync.wait_ge(k.dv.h, dvC(c - 2, dv_wx(7)))
                        sync.wait_ge(k.pe.h, peC(c - 2, pe_raw(7)))
                    sync.dma_start(out=xtt[c % 2][:], in_=k.xtm_scr[c * 128:(c + 1) * 128, :]).then_inc(k.q[c % 2].h, 16)
                    k.q[c % 2].n += 16

            @blk.gpsimd
            def _(g):
                for c in range(NCH):
                    for r in range(4):
                        gr = 4 * c + r
                        if gr >= 2:
                            c2, r2 = (gr - 2) // 4, (gr - 2) % 4
                            g.wait_ge(k.pe.h, peC(c2, pe_dg(4 * r2 + 3)))
                        sl = r % 2
                        hb = 16 * r
                        g.dma_start(
                            out=combL[0:3, sl * 2048:(sl + 1) * 2048].rearrange("s (r t) -> s r t", r=16),
                            in_=k.vsp_scr[hb:hb + 16, :].rearrange("r (s ch t) -> s ch r t", s=3, ch=NCH)[:, c],
                        ).then_inc(k.q[6 + gr % 2].h, 16)
                        k.q[6 + gr % 2].n += 16
                        g.dma_start(
                            out=combR[3:6, sl * 2048:(sl + 1) * 2048].rearrange("s (r t) -> s r t", r=16),
                            in_=k.csp_scr[hb:hb + 16, :].rearrange("r (s ch t) -> s ch r t", s=3, ch=NCH)[:, c],
                        ).then_inc(k.q[6 + gr % 2].h, 16)
                        k.q[6 + gr % 2].n += 16
                    if c > 0:
                        g.wait_ge(k.dv.h, dvC(c - 1, dv_hadd(7)))
                    for j in range(8):
                        g.tensor_copy(out=hbf[:, j * 512:(j + 1) * 512],
                                      in_=k.H[:, j * 512:(j + 1) * 512]).then_inc(k.po.h, 1)
                        k.po.n += 1


            @blk.vector
            def _(vector):
                for c in range(NCH):
                    base = dv0 + DV * c

                    def dput(ins_, pos):
                        ins_.then_inc(k.dv.h, 1)
                        k.dv.n += 1
                        assert k.dv.n == base + pos, (c, pos, k.dv.n - base)

                    vector.wait_ge(k.q[c % 2].h, qb0[c % 2] + 16 * (c // 2 + 1))
                    vector.wait_ge(k.ac.h, acC(c, ac_e1))
                    dput(vector.tensor_copy(out=eclT[:], in_=tmpF[:]), dv_eclT)
                    dput(vector.tensor_scalar(out=tmpG[:], in0=k.cl_t[:, c * 128:(c + 1) * 128],
                                              scalar1=-1.0, scalar2=k.t64g[:, c:c + 1],
                                              op0=ALU.mult, op1=ALU.add), dv_tsG)
                    vector.wait_ge(k.dv.h, base + dv_tsG)
                    vector.wait_ge(k.ac.h, acC(c, ac_e3))
                    dput(vector.tensor_tensor(out=tmpF2[:], in0=tmpG2[:],
                                              in1=k.dt_t[:, c * 128:(c + 1) * 128], op=ALU.mult), dv_wmul)
                    vector.wait_ge(k.dv.h, base + dv_wmul)
                    dput(vector.tensor_copy(out=wfT[:], in_=tmpF2[:]), dv_wfT)
                    dput(vector.tensor_scalar(out=gmT[:], in0=k.ones64[:, 0:128],
                                              scalar1=k.t64g[:, 48 + c:49 + c], scalar2=None,
                                              op0=ALU.mult), dv_gmT)
                    vector.wait_ge(k.pe.h, peC(c, pe_cb))
                    dput(vector.tensor_copy(out=CBr[:, 0:128], in_=k.pf[4][:, 0:128]), dv_cbr)
                    vector.wait_ge(k.dv.h, base + dv_cbr)
                    dput(vector.tensor_copy(out=CBr[:, 128:256], in_=CBr[:, 0:128]), dv_cbr + 1)
                    vector.wait_ge(k.dv.h, base + dv_cbr + 1)
                    dput(vector.tensor_copy(out=CBr[:, 256:512], in_=CBr[:, 0:256]), dv_cbr + 2)
                    for i in range(16):
                        if c > 0 or i >= 2:
                            tgt = acC(c, ac_exp(i - 2)) if i >= 2 else acC(c - 1, ac_exp(14 + i))
                            vector.wait_ge(k.ac.h, tgt)
                        vector.wait_ge(k.pe.h, peC(c, pe_dg(i)))
                        pfD = k.pf[i % 2]
                        dput(vector.tensor_tensor(out=estg[:, (i % 2) * 512:(i % 2 + 1) * 512],
                                                  in0=pfD[:], in1=m4b[:], op=ALU.add), dv_em(i))
                        vector.wait_ge(k.ac.h, acC(c, ac_exp(i)))
                        if c > 0:
                            vector.wait_ge(k.pe.h, peC(c - 1, pe_intra(i // 2)))
                        dput(vector.tensor_tensor(out=G[:, i * 512:(i + 1) * 512],
                                                  in0=e2[:, (i % 2) * 512:(i % 2 + 1) * 512],
                                                  in1=CBr[:], op=ALU.mult), dv_g(i))
                    for j in range(8):
                        vector.wait_ge(k.pe.h, peC(c, pe_eclB(j)))
                        pfe = k.pf[4 + (j % 2)]
                        dput(vector.tensor_copy(out=eclb[:, (j % 2) * 512:(j % 2 + 1) * 512],
                                                in_=pfe[:]), dv_eclb(j))
                        dput(vector.tensor_tensor(out=xds[:, (j % 2) * 512:(j % 2 + 1) * 512],
                                                  in0=xtt[c % 2][:, j * 512:(j + 1) * 512],
                                                  in1=drep[:, j * 512:(j + 1) * 512], op=ALU.mult), dv_xd(j))
                        vector.wait_ge(k.dv.h, base + dv_xd(j))
                        vector.wait_ge(k.pe.h, peC(c, pe_raw(j)))
                        pfr = k.pf[5 - (j % 2)]
                        dput(vector.tensor_tensor(out=yst[:], in0=pfr[:],
                                                  in1=eclb[:, (j % 2) * 512:(j % 2 + 1) * 512],
                                                  op=ALU.mult), dv_t1(j))
                        vector.wait_ge(k.dv.h, base + dv_t1(j))
                        jj = 8 * c + j
                        if jj >= 2:
                            vector.wait_ge(k.q[4 + jj % 2].h, qb4[jj % 2] + 16 * (jj // 2))
                        pfy = k.pf[2 + (j % 2)]
                        dput(vector.tensor_add(out=yst2[:, (j % 2) * 512:(j % 2 + 1) * 512],
                                               in0=yst[:], in1=pfy[:]), dv_y(j))
                    for s in range(8):
                        vector.wait_ge(k.pe.h, peC(c, pe_wrep(s)))
                        pfw = k.pf[4 + (s % 2)]
                        dput(vector.tensor_tensor(out=wx[:, (s % 2) * 512:(s % 2 + 1) * 512],
                                                  in0=xtt[c % 2][:, s * 512:(s + 1) * 512],
                                                  in1=pfw[:], op=ALU.mult), dv_wx(s))
                        vector.wait_ge(k.pe.h, peC(c, pe_gmm(s)))
                        pfg = k.pf[2 + (s % 2)]
                        dput(vector.tensor_tensor(out=hmt[:], in0=k.H[:, s * 512:(s + 1) * 512],
                                                  in1=pfg[:], op=ALU.mult), dv_hm(s))
                        vector.wait_ge(k.dv.h, base + dv_hm(s))
                        vector.wait_ge(k.pe.h, peC(c, pe_smm(s)))
                        pfs = k.pf[5 - (s % 2)]
                        dput(vector.tensor_add(out=k.H[:, s * 512:(s + 1) * 512],
                                               in0=hmt[:], in1=pfs[:]), dv_hadd(s))
                        vector.wait_ge(k.dv.h, base + dv_hadd(s))

            @blk.scalar
            def _(scalar):
                for c in range(NCH):
                    if c > 0:
                        scalar.wait_ge(k.dv.h, dvC(c - 1, dv_eclT))
                    scalar.activation(out=tmpF[:], in_=k.cl_t[:, c * 128:(c + 1) * 128],
                                      func=AF.Exp).then_inc(k.ac.h, 1)
                    k.ac.n += 1
                    scalar.wait_ge(k.dv.h, dvC(c, dv_tsG))
                    scalar.activation(out=tmpG2[:], in_=tmpG[:], func=AF.Exp).then_inc(k.ac.h, 1)
                    k.ac.n += 1
                    for i in range(16):
                        scalar.wait_ge(k.dv.h, dvC(c, dv_em(i)))
                        scalar.activation(out=e2[:, (i % 2) * 512:(i % 2 + 1) * 512],
                                          in_=estg[:, (i % 2) * 512:(i % 2 + 1) * 512],
                                          func=AF.Exp).then_inc(k.ac.h, 1)
                        k.ac.n += 1
                    for j in range(8):
                        jj = 8 * c + j
                        scalar.wait_ge(k.dv.h, dvC(c, dv_y(j)))
                        scalar.dma_start(out=k.y_scr[c * 128:(c + 1) * 128, j * 512:(j + 1) * 512],
                                         in_=yst2[:, (j % 2) * 512:(j % 2 + 1) * 512]).then_inc(k.q[4 + jj % 2].h, 16)
                        k.q[4 + jj % 2].n += 16

            @blk.tensor
            def _(tensor):
                bankw = {i: 0 for i in range(6)}

                def bwait(tensor, b, consumer_dv):
                    if bankw[b]:
                        tensor.wait_ge(k.dv.h, bankw[b])
                    bankw[b] = consumer_dv

                for c in range(NCH):
                    base_pe = pe0 + PE_ * c

                    def pput(ins_, pos):
                        ins_.then_inc(k.pe.h, 1)
                        k.pe.n += 1
                        assert k.pe.n == base_pe + pos, (c, pos, k.pe.n - base_pe)

                    bwait(tensor, 4, dvC(c, dv_cbr))
                    pput(tensor.matmul(k.pf[4][:, 0:128], k.bB[:, c * 128:(c + 1) * 128],
                                       k.bC[:, c * 128:(c + 1) * 128], start=True, stop=True), pe_cb)
                    tensor.wait_ge(k.q[c % 2].h, qb0[c % 2] + 16 * (c // 2 + 1))
                    for i in range(16):
                        r = i // 4
                        gr = 4 * c + r
                        tensor.wait_ge(k.q[6 + gr % 2].h, qb2[gr % 2] + 32 * (gr // 2 + 1))
                        bwait(tensor, i % 2, dvC(c, dv_em(i)))
                        pfD = k.pf[i % 2]
                        sl = r % 2
                        last = None
                        for hh in range(4):
                            hl = (i % 4) * 4 + hh
                            last = tensor.matmul(pfD[:, hh * 128:(hh + 1) * 128],
                                                 combL[0:6, sl * 2048 + hl * 128: sl * 2048 + (hl + 1) * 128],
                                                 combR[0:6, sl * 2048 + hl * 128: sl * 2048 + (hl + 1) * 128],
                                                 start=(hh == 0), stop=(hh == 3))
                        pput(last, pe_dg(i))
                    for j in range(8):
                        be = 4 + (j % 2)
                        bwait(tensor, be, dvC(c, dv_eclb(j)))
                        tensor.wait_ge(k.dv.h, dvC(c, dv_eclT))
                        pput(tensor.matmul(k.pf[be][:], eclT[:],
                                           k.r64[:, j * 512:(j + 1) * 512], start=True, stop=True), pe_eclB(j))
                        by = 2 + (j % 2)
                        bwait(tensor, by, dvC(c, dv_y(j)))
                        tensor.wait_ge(k.dv.h, dvC(c, dv_xd(j)))
                        pput(tensor.matmul(k.pf[by][:], k.ident_bf[:],
                                           xds[:, (j % 2) * 512:(j % 2 + 1) * 512],
                                           start=True, stop=False), pe_xd(j))
                        tensor.wait_ge(k.dv.h, dvC(c, dv_g(2 * j + 1)))
                        last = None
                        for jj2 in range(8):
                            head = 8 * j + jj2
                            gcol = head * 128
                            last = tensor.matmul(k.pf[by][:, jj2 * 64:(jj2 + 1) * 64],
                                                 G[:, gcol:gcol + 128],
                                                 xtt[c % 2][:, head * 64:(head + 1) * 64],
                                                 start=False, stop=(jj2 == 7))
                        pput(last, pe_intra(j))
                        br = 5 - (j % 2)
                        bwait(tensor, br, dvC(c, dv_t1(j)))
                        tensor.wait_ge(k.po.h, poC(c, po_hbf(j)))
                        pput(tensor.matmul(k.pf[br][:], k.bC[:, c * 128:(c + 1) * 128],
                                           hbf[:, j * 512:(j + 1) * 512], start=True, stop=True), pe_raw(j))
                    for s in range(8):
                        bw = 4 + (s % 2)
                        bwait(tensor, bw, dvC(c, dv_wx(s)))
                        tensor.wait_ge(k.dv.h, dvC(c, dv_wfT))
                        pput(tensor.matmul(k.pf[bw][:], wfT[:], k.r64[:, s * 512:(s + 1) * 512],
                                           start=True, stop=True), pe_wrep(s))
                        bs = 5 - (s % 2)
                        bwait(tensor, bs, dvC(c, dv_hadd(s)))
                        tensor.wait_ge(k.dv.h, dvC(c, dv_wx(s)))
                        pput(tensor.matmul(k.pf[bs][:], k.bBt[:, c * 128:(c + 1) * 128],
                                           wx[:, (s % 2) * 512:(s % 2 + 1) * 512],
                                           start=True, stop=True), pe_smm(s))
                        bg = 2 + (s % 2)
                        bwait(tensor, bg, dvC(c, dv_hm(s)))
                        tensor.wait_ge(k.dv.h, dvC(c, dv_gmT))
                        pput(tensor.matmul(k.pf[bg][:], gmT[:], k.r64[:, s * 512:(s + 1) * 512],
                                           start=True, stop=True), pe_gmm(s))
    _barrier(k)

    with nc.Block() as blk:
        @blk.scalar
        def _(scalar):
            scalar.dma_start(out=k.cc_in[:, :], in_=k.H[:]).then_inc(k.ls.h, 16)
            k.ls.n += 16

        @blk.gpsimd
        def _(g):
            g.wait_ge(k.ls.h, k.ls.n)
            g.collective_compute("AllGather", ALU.bypass,
                                 replica_groups=[[i, i + 1] for i in range(0, NCORES, 2)],
                                 ins=[k.cc_in.ap().opt()], outs=[k.cc_out.ap().opt()]).then_inc(k.cc.h, 1)
            k.cc.n += 1
            g.wait_ge(k.cc.h, k.cc.n)
    _barrier(k)


def _p5_gate(k):
    nc = k.nc
    with ExitStack() as ctx:
        e = ctx.enter_context
        hbf5 = e(nc.sbuf_tensor("hbf5", [128, DM], BF16))
        ytile = e(nc.sbuf_tensor("ytile", [128, DM], F32))
        ztile = e(nc.sbuf_tensor("ztile", [128, DM], F32))
        mupz5 = e(nc.sbuf_tensor("mupz5", [128, DM], BF16))
        eglb5 = e(nc.sbuf_tensor("eglb5", [128, 512], BF16))
        yst5 = e(nc.sbuf_tensor("yst5", [128, 1024], F32))
        sq5 = e(nc.sbuf_tensor("sq5", [128, 512], F32))
        acc5 = e(nc.sbuf_tensor("acc5", [128, 16], F32))

        with nc.Block() as blk0:
            lg00 = k.lg.n

            @blk0.gpsimd
            def _(g):
                g.wait_ge(k.cc.h, k.cc.n)
                g.dma_start(out=mupz5[:], in_=k.mupz).then_inc(k.lg.h, 16)
                k.lg.n += 16
                for sl in range(8):
                    if sl >= 1:
                        g.wait_ge(k.dv.h, k.dv.n + 2 * sl - 1)
                    g.dma_start(out=sq5[:], in_=k.cc_out[0:128, sl * 512:(sl + 1) * 512]).then_inc(k.lg.h, 16)
                    k.lg.n += 16

            @blk0.vector
            def _(vector):
                for sl in range(8):
                    vector.wait_ge(k.lg.h, lg00 + 16 * (sl + 2))
                    vector.tensor_scalar(out=yst5[:, 0:512], in0=sq5[:], scalar1=k.hm[:, 0:1],
                                         scalar2=None, op0=ALU.mult).then_inc(k.dv.h, 1)
                    k.dv.n += 1
                    vector.wait_ge(k.dv.h, k.dv.n)
                    vector.tensor_copy(out=hbf5[:, sl * 512:(sl + 1) * 512], in_=yst5[:, 0:512]).then_inc(k.dv.h, 1)
                    k.dv.n += 1
        _barrier(k)
        if STOP == "P5p":
            return

        ld0, dv0, pe0, ac0, po0 = k.ld.n, k.dv.n, k.pe.n, k.ac.n, k.po.n
        DV5, PE5, AC5, PO5 = 28, 24, 17, 17
        dv_eglb = lambda j: 1 + 2 * j
        dv_t15 = lambda j: 2 + 2 * j
        dv_red, dv_ts, dv_rcp, dv_yn = 17, 18, 19, 20
        dv_ev = lambda g: 21 + g
        po_ya = lambda j: 1 + j
        po_zm = 9
        po_gm = lambda s: 10 + s
        pe_egl5 = lambda j: 1 + 2 * j
        pe_cor5 = lambda j: 2 + 2 * j
        pe_tr5 = lambda g: 17 + g
        ac_sil = lambda s: 1 + s
        ac_sq = lambda s: 9 + s
        ac_rst = 17

        def dvC(c, p):
            return dv0 + DV5 * c + p

        def peC(c, p):
            return pe0 + PE5 * c + p

        def acC(c, p):
            return ac0 + AC5 * c + p

        def poC(c, p):
            return po0 + PO5 * c + p

        with nc.Block() as blk:
            @blk.sync
            def _(sync):
                for c in range(P5LIM):
                    if c >= 1:
                        sync.wait_ge(k.pe.h, peC(c - 1, pe_tr5(7)))
                        sync.wait_ge(k.po.h, poC(c - 1, po_gm(7)))
                    sync.dma_start(out=ytile[:], in_=k.y_scr[c * 128:(c + 1) * 128, :]).then_inc(k.ld.h, 16)
                    k.ld.n += 16
                    sync.dma_start(out=ztile[:], in_=k.z_scr[c * 128:(c + 1) * 128, :]).then_inc(k.ld.h, 16)
                    k.ld.n += 16

            @blk.tensor
            def _(tensor):
                bankw = {i: 0 for i in range(6)}

                def bwait(b, cons):
                    if bankw[b]:
                        tensor.wait_ge(k.dv.h, bankw[b])
                    bankw[b] = cons

                for c in range(P5LIM):
                    base_pe = pe0 + PE5 * c

                    def pput(ins_, pos):
                        ins_.then_inc(k.pe.h, 1)
                        k.pe.n += 1
                        assert k.pe.n == base_pe + pos, (c, pos, k.pe.n - base_pe)

                    for j in range(8):
                        be = 4 + (j % 2)
                        bwait(be, dvC(c, dv_eglb(j)))
                        pput(tensor.matmul(k.pf[be][:], k.egl_bf[:, c * 128:(c + 1) * 128],
                                           k.r64[:, j * 512:(j + 1) * 512], start=True, stop=True), pe_egl5(j))
                        by = 2 + (j % 2)
                        bwait(by, dvC(c, dv_t15(j)))
                        pput(tensor.matmul(k.pf[by][:], k.bC[:, c * 128:(c + 1) * 128],
                                           hbf5[:, j * 512:(j + 1) * 512], start=True, stop=True), pe_cor5(j))
                    tensor.wait_ge(k.dv.h, dvC(c, dv_yn))
                    for g in range(8):
                        b = g % 2
                        bwait(b, dvC(c, dv_ev(g)))
                        last = None
                        for j in range(4):
                            kk = g * 4 + j
                            last = tensor.transpose(k.pf[b][:, j * 128:(j + 1) * 128],
                                                    ytile[:, kk * 128:(kk + 1) * 128], k.ident_f[:])
                        pput(last, pe_tr5(g))

            @blk.vector
            def _(vector):
                for c in range(P5LIM):
                    base = dv0 + DV5 * c

                    def dput(ins_, pos):
                        ins_.then_inc(k.dv.h, 1)
                        k.dv.n += 1
                        assert k.dv.n == base + pos, (c, pos, k.dv.n - base)

                    vector.wait_ge(k.ld.h, ld0 + 32 * (c + 1))
                    for j in range(8):
                        vector.wait_ge(k.pe.h, peC(c, pe_egl5(j)))
                        dput(vector.tensor_copy(out=eglb5[:], in_=k.pf[4 + (j % 2)][:]), dv_eglb(j))
                        vector.wait_ge(k.dv.h, base + dv_eglb(j))
                        vector.wait_ge(k.pe.h, peC(c, pe_cor5(j)))
                        if 8 * c + j >= 2:
                            jj2 = 8 * c + j - 2
                            vector.wait_ge(k.po.h, poC(jj2 // 8, po_ya(jj2 % 8)))
                        dput(vector.tensor_tensor(out=yst5[:, (j % 2) * 512:(j % 2 + 1) * 512],
                                                  in0=k.pf[2 + (j % 2)][:],
                                                  in1=eglb5[:], op=ALU.mult), dv_t15(j))
                    vector.wait_ge(k.ac.h, acC(c, ac_sq(7)))
                    dput(vector.reduce_sum(out=acc5[:, 8:9], in_=acc5[:, 0:8], axis=AX), dv_red)
                    vector.wait_ge(k.dv.h, base + dv_red)
                    dput(vector.tensor_scalar(out=acc5[:, 9:10], in0=acc5[:, 8:9],
                                              scalar1=1.0 / DM, scalar2=EPS, op0=ALU.mult, op1=ALU.add), dv_ts)
                    vector.wait_ge(k.dv.h, base + dv_ts)
                    dput(vector.reciprocal(out=acc5[:, 10:11], in_=acc5[:, 9:10]), dv_rcp)
                    vector.wait_ge(k.ac.h, acC(c, ac_rst))
                    dput(vector.tensor_scalar(out=ytile[:], in0=ytile[:], scalar1=acc5[:, 11:12],
                                              scalar2=None, op0=ALU.mult), dv_yn)
                    for g in range(8):
                        vector.wait_ge(k.pe.h, peC(c, pe_tr5(g)))
                        last = None
                        for j in range(4):
                            kk = g * 4 + j
                            last = vector.tensor_copy(out=k.yfm(kk)[:, c * 128:(c + 1) * 128],
                                                      in_=k.pf[g % 2][:, j * 128:(j + 1) * 128])
                        dput(last, dv_ev(g))

            @blk.gpsimd
            def _(g_):
                for c in range(P5LIM):
                    for j in range(8):
                        g_.wait_ge(k.dv.h, dvC(c, dv_t15(j)))
                        g_.tensor_add(out=ytile[:, j * 512:(j + 1) * 512],
                                      in0=yst5[:, (j % 2) * 512:(j % 2 + 1) * 512],
                                      in1=ytile[:, j * 512:(j + 1) * 512]).then_inc(k.po.h, 1)
                        k.po.n += 1
                    g_.wait_ge(k.ld.h, ld0 + 32 * (c + 1))
                    g_.tensor_tensor(out=ztile[:], in0=ztile[:], in1=mupz5[:], op=ALU.mult).then_inc(k.po.h, 1)
                    k.po.n += 1
                    for s in range(8):
                        g_.wait_ge(k.ac.h, acC(c, ac_sil(s)))
                        g_.tensor_tensor(out=ytile[:, s * 512:(s + 1) * 512],
                                         in0=ytile[:, s * 512:(s + 1) * 512],
                                         in1=ztile[:, s * 512:(s + 1) * 512],
                                         op=ALU.mult).then_inc(k.po.h, 1)
                        k.po.n += 1

            @blk.scalar
            def _(scalar):
                for c in range(P5LIM):
                    for s in range(8):
                        scalar.wait_ge(k.po.h, poC(c, po_zm))
                        scalar.activation(out=ztile[:, s * 512:(s + 1) * 512],
                                          in_=ztile[:, s * 512:(s + 1) * 512],
                                          func=AF.Silu).then_inc(k.ac.h, 1)
                        k.ac.n += 1
                    for s in range(8):
                        scalar.wait_ge(k.po.h, poC(c, po_gm(s)))
                        scalar.activation(out=sq5[:], in_=ytile[:, s * 512:(s + 1) * 512],
                                          func=AF.Square,
                                          accum_out=acc5[:, s:s + 1]).then_inc(k.ac.h, 1)
                        k.ac.n += 1
                    scalar.wait_ge(k.dv.h, dvC(c, dv_rcp))
                    scalar.activation(out=acc5[:, 11:12], in_=acc5[:, 10:11],
                                      func=AF.Sqrt).then_inc(k.ac.h, 1)
                    k.ac.n += 1
    _barrier(k)


def _p6_outproj(k):
    nc = k.nc
    with ExitStack() as ctx:
        e = ctx.enter_context
        wst = [e(nc.sbuf_tensor(f"wst6{i}", [128, 512], F32)) for i in range(4)]
        wsb = [e(nc.sbuf_tensor(f"wsb6{i}", [128, 32 * 512], BF16)) for i in range(2)]
        ast = [e(nc.sbuf_tensor(f"ast{i}", [128, 512], F32)) for i in range(2)]
        ld0, dv0, pe0, ls0 = k.ld.n, k.dv.n, k.pe.n, k.ls.n
        QM = [0, 1, 10, 11]
        qb0 = [k.q[i].n for i in QM]
        qb2 = [k.q[2].n, k.q[3].n]
        NT = 16

        def cast_dv(g):
            si, kk = g // 32, g % 32
            return dv0 + si * 48 + kk + 1

        def evac_dv(si, j):
            return dv0 + si * 48 + 32 + j + 1

        def tile_pe(si, j):
            return pe0 + si * NT + j + 1

        with nc.Block() as blk:
            @blk.sync
            def _(sync):
                for si in range(8):
                    for kk in range(32):
                        g = si * 32 + kk
                        if g >= 4:
                            sync.wait_ge(k.dv.h, cast_dv(g - 4))
                        sync.dma_start(out=wst[g % 4][:],
                                       in_=k.w_op[kk * 128:(kk + 1) * 128, si * 512:(si + 1) * 512]).then_inc(k.q[QM[g % 4]].h, 16)
                        k.q[QM[g % 4]].n += 16

            @blk.vector
            def _(vector):
                for si in range(8):
                    for kk in range(32):
                        g = si * 32 + kk
                        vector.wait_ge(k.q[QM[g % 4]].h, qb0[g % 4] + 16 * (g // 4 + 1))
                        if si >= 2 and kk == 0:
                            vector.wait_ge(k.pe.h, tile_pe(si - 2, NT - 1))
                        vector.tensor_scalar(out=wsb[si % 2][:, kk * 512:(kk + 1) * 512],
                                             in0=wst[g % 4][:],
                                             scalar1=k.nrmw[:, kk:kk + 1], scalar2=None,
                                             op0=ALU.mult).then_inc(k.dv.h, 1)
                        k.dv.n += 1
                    for j in range(NT):
                        ej = si * NT + j
                        vector.wait_ge(k.pe.h, tile_pe(si, j))
                        if ej >= 2:
                            vector.wait_ge(k.q[2 + ej % 2].h, qb2[ej % 2] + 16 * (ej // 2))
                        vector.tensor_copy(out=ast[j % 2][:], in_=k.pf[j % 2][:]).then_inc(k.dv.h, 1)
                        k.dv.n += 1

            @blk.tensor
            def _(tensor):
                for si in range(8):
                    tensor.wait_ge(k.dv.h, dv0 + si * 48 + 32)
                    for j in range(NT):
                        if si * NT + j >= 2:
                            jj = si * NT + j - 2
                            tensor.wait_ge(k.dv.h, evac_dv(jj // NT, jj % NT))
                        pb = k.pf[j % 2]
                        last = None
                        for kk in range(32):
                            last = tensor.matmul(pb[:], k.yfm(kk)[:, j * 128:(j + 1) * 128],
                                                 wsb[si % 2][:, kk * 512:(kk + 1) * 512],
                                                 start=(kk == 0), stop=(kk == 31))
                        last.then_inc(k.pe.h, 1)
                        k.pe.n += 1

            @blk.scalar
            def _(scalar):
                for si in range(8):
                    for j in range(NT):
                        ej = si * NT + j
                        scalar.wait_ge(k.dv.h, evac_dv(si, j))
                        scalar.dma_start(out=k.attn_scr[j * 128:(j + 1) * 128, si * 512:(si + 1) * 512],
                                         in_=ast[j % 2][:]).then_inc(k.q[2 + ej % 2].h, 16)
                        k.q[2 + ej % 2].n += 16
    _barrier(k)



def _p7_router(k):
    nc = k.nc
    with ExitStack() as ctx:
        e = ctx.enter_context
        atb = e(nc.sbuf_tensor("at7b", [128, DM], BF16))
        wrtb = e(nc.sbuf_tensor("wrtb", [128, 256], BF16))
        w8tb = e(nc.sbuf_tensor("w8tb", [8, TR], BF16))
        lg8 = e(nc.sbuf_tensor("lg8", [128, 48], F32))
        wr_sb = e(nc.sbuf_tensor("wr_sb", [128, 512], F32))
        wst7 = [e(nc.sbuf_tensor(f"wst7{i}", [128, 512], F32)) for i in range(2)]
        ld0, dv0, pe0, ac0 = k.ld.n, k.dv.n, k.pe.n, k.ac.n
        qb0 = [k.q[0].n, k.q[1].n]

        with nc.Block() as blk:
            @blk.gpsimd
            def _(g):
                g.dma_start(out=k.bbb[:], in_=k.bb).then_inc(k.lg.h, 16)
                k.lg.n += 16
                g.wait_ge(k.lg.h, k.lg.n)

            @blk.sync
            def _(sync):
                for kk in range(32):
                    if kk >= 2:
                        sync.wait_ge(k.dv.h, dv0 + kk - 1)
                    sync.dma_start(out=wst7[kk % 2][:, 0:128],
                                   in_=k.aT[kk * 128:(kk + 1) * 128, :]).then_inc(k.q[kk % 2].h, 16)
                    k.q[kk % 2].n += 16
                for kk in range(32):
                    g = 32 + kk
                    sync.wait_ge(k.dv.h, dv0 + g - 1)
                    sync.dma_start(out=wst7[g % 2][:, 0:8],
                                   in_=k.wrT[kk * 128:(kk + 1) * 128, :]).then_inc(k.q[g % 2].h, 16)
                    k.q[g % 2].n += 16
                for sl in range(8):
                    g = 64 + sl
                    sync.wait_ge(k.dv.h, dv0 + g - 1)
                    sync.dma_start(out=wst7[g % 2][:, 0:512],
                                   in_=k.bstk[:, sl * 512:(sl + 1) * 512]).then_inc(k.q[g % 2].h, 16)
                    k.q[g % 2].n += 16


            @blk.vector
            def _(vector):
                def dstep(ins_):
                    ins_.then_inc(k.dv.h, 1)
                    k.dv.n += 1
                    vector.wait_ge(k.dv.h, k.dv.n)
                for kk in range(32):
                    vector.wait_ge(k.q[kk % 2].h, qb0[kk % 2] + 16 * (kk // 2 + 1))
                    dstep(vector.tensor_copy(out=atb[:, kk * 128:(kk + 1) * 128], in_=wst7[kk % 2][:, 0:128]))
                for kk in range(32):
                    g = 32 + kk
                    vector.wait_ge(k.q[g % 2].h, qb0[g % 2] + 16 * (g // 2 + 1))
                    dstep(vector.tensor_copy(out=wrtb[:, kk * 8:(kk + 1) * 8], in_=wst7[g % 2][:, 0:8]))
                for sl in range(8):
                    g = 64 + sl
                    vector.wait_ge(k.q[g % 2].h, qb0[g % 2] + 16 * (g // 2 + 1))
                    dstep(vector.tensor_scalar(out=k.bstb[:, sl * 512:(sl + 1) * 512],
                                               in0=wst7[g % 2][:, 0:512], scalar1=2.0,
                                               scalar2=None, op0=ALU.mult))
                k._p7base = k.dv.n
                k._p7evac, k._p7sub, k._p7fin, k._p7w8, k._p7h = [], [], [], [], []
                for j in range(NCH if P7PART >= 2 else 0):
                    vector.wait_ge(k.pe.h, pe0 + 2 * j + 1)
                    dstep(vector.tensor_copy(out=lg8[:, 0:8], in_=k.pf[4][:, 0:8]))
                    k._p7evac.append(k.dv.n)
                    dstep(vector.reduce_max(out=lg8[:, 40:41], in_=lg8[:, 0:8], axis=AX))
                    dstep(vector.tensor_scalar(out=lg8[:, 8:16], in0=lg8[:, 0:8], scalar1=lg8[:, 40:41],
                                               scalar2=None, op0=ALU.is_equal))
                    dstep(vector.scalar_tensor_tensor(out=lg8[:, 16:24], in0=lg8[:, 8:16], scalar=-1e30,
                                                      in1=lg8[:, 0:8], op0=ALU.mult, op1=ALU.add))
                    dstep(vector.reduce_max(out=lg8[:, 41:42], in_=lg8[:, 16:24], axis=AX))
                    dstep(vector.tensor_sub(out=lg8[:, 42:43], in0=lg8[:, 40:41], in1=lg8[:, 41:42]))
                    k._p7sub.append(k.dv.n)
                    vector.wait_ge(k.ac.h, ac0 + j + 1)
                    dstep(vector.tensor_scalar(out=lg8[:, 44:45], in0=lg8[:, 43:44], scalar1=-1.0,
                                               scalar2=1.0, op0=ALU.mult, op1=ALU.add))
                    dstep(vector.tensor_scalar(out=lg8[:, 24:32], in0=lg8[:, 16:24], scalar1=lg8[:, 41:42],
                                               scalar2=None, op0=ALU.is_equal))
                    dstep(vector.tensor_scalar(out=lg8[:, 32:40], in0=lg8[:, 8:16], scalar1=lg8[:, 43:44],
                                               scalar2=None, op0=ALU.mult))
                    dstep(vector.scalar_tensor_tensor(out=lg8[:, 32:40], in0=lg8[:, 24:32],
                                                      scalar=lg8[:, 44:45], in1=lg8[:, 32:40],
                                                      op0=ALU.mult, op1=ALU.add))
                    k._p7fin.append(k.dv.n)
                    vector.wait_ge(k.pe.h, pe0 + 2 * j + 2)
                    dstep(vector.tensor_copy(out=w8tb[:, j * 128:(j + 1) * 128], in_=k.pf[5][0:8, 0:128]))
                    k._p7w8.append(k.dv.n)
                for t5 in range(4 if P7PART >= 3 else 0):
                    vector.wait_ge(k.pe.h, pe0 + 32 + 2 * t5 + 1)
                    dstep(vector.tensor_copy(out=wr_sb[:], in_=k.pf[3][:]))
                    vector.wait_ge(k.pe.h, pe0 + 32 + 2 * t5 + 2)
                    dstep(vector.tensor_tensor(out=k.hwb[:, t5 * 512:(t5 + 1) * 512],
                                               in0=k.pf[2][:], in1=wr_sb[:], op=ALU.mult))
                    k._p7h.append(k.dv.n)

            @blk.scalar
            def _(scalar):
                for j in range(NCH if P7PART >= 2 else 0):
                    scalar.wait_ge(k.dv.h, k._p7sub[j])
                    scalar.activation(out=lg8[:, 43:44], in_=lg8[:, 42:43],
                                      func=AF.Sigmoid).then_inc(k.ac.h, 1)
                    k.ac.n += 1

            @blk.tensor
            def _(tensor):
                for j in range(NCH if P7PART >= 2 else 0):
                    tensor.wait_ge(k.dv.h, dv0 + 64)
                    if j >= 1:
                        tensor.wait_ge(k.dv.h, k._p7evac[j - 1])
                    last = None
                    for kk in range(32):
                        last = tensor.matmul(k.pf[4][:, 0:8], k.yfm(kk)[:, j * 128:(j + 1) * 128],
                                             wrtb[:, kk * 8:(kk + 1) * 8],
                                             start=(kk == 0), stop=(kk == 31))
                    last.then_inc(k.pe.h, 1)
                    k.pe.n += 1
                    tensor.wait_ge(k.dv.h, k._p7fin[j])
                    if j >= 1:
                        tensor.wait_ge(k.dv.h, k._p7w8[j - 1])
                    tensor.transpose(k.pf[5][0:8, 0:128], lg8[:, 32:40], k.ident_f[:]).then_inc(k.pe.h, 1)
                    k.pe.n += 1
                for t5 in range(4 if P7PART >= 3 else 0):
                    tensor.wait_ge(k.dv.h, dv0 + 32)
                    if t5 >= 1:
                        tensor.wait_ge(k.dv.h, k._p7h[t5 - 1])
                    tensor.wait_ge(k.dv.h, k._p7w8[NCH - 1])
                    tensor.matmul(k.pf[3][:], k.r8[:], w8tb[:, t5 * 512:(t5 + 1) * 512],
                                  start=True, stop=True).then_inc(k.pe.h, 1)
                    k.pe.n += 1
                    last = None
                    for kk in range(32):
                        last = tensor.matmul(k.pf[2][:],
                                             atb[:, kk * 128:(kk + 1) * 128],
                                             k.yfm(kk)[:, t5 * 512:(t5 + 1) * 512],
                                             start=(kk == 0), stop=(kk == 31))
                    last.then_inc(k.pe.h, 1)
                    k.pe.n += 1
    _barrier(k)


def _p8_ffn(k):
    nc = k.nc
    with ExitStack() as ctx:
        e = ctx.enter_context
        CW = 256
        slices = [(s * CW, CW) for s in range(16)]
        wst = [e(nc.sbuf_tensor(f"wst8{i}", [128, CW], F32)) for i in range(4)]
        wsb = [e(nc.sbuf_tensor(f"wsb8{i}", [128, 32 * CW], BF16)) for i in range(2)]
        x1s = [e(nc.sbuf_tensor(f"x1s{i}", [128, CW], F32)) for i in range(2)]
        ost = [e(nc.sbuf_tensor(f"ost{i}", [128, CW], F32)) for i in range(2)]
        ld0, dv0, pe0, ls0, lg0 = k.ld.n, k.dv.n, k.pe.n, k.ls.n, k.lg.n
        QM = [0, 1, 10, 11]
        qb0 = [k.q[i].n for i in QM]
        qb2 = [k.q[8].n, k.q[9].n]
        qb4 = [k.q[4].n, k.q[5].n]
        NT = 16

        def cast_dv(g):
            si, kk = g // 32, g % 32
            return dv0 + si * (32 + NT) + kk + 1

        def evac_dv(si, j):
            return dv0 + si * (32 + NT) + 32 + j + 1

        def tile_pe(si, j):
            return pe0 + si * NT + j + 1

        with nc.Block() as blk:
            @blk.sync
            def _(sync):
                for si, (c0, cw) in enumerate(slices):
                    for kk in range(32):
                        g = si * 32 + kk
                        if g >= 4:
                            sync.wait_ge(k.dv.h, cast_dv(g - 4))
                        sync.dma_start(out=wst[g % 4][:, 0:cw],
                                       in_=k.wbT[kk * 128:(kk + 1) * 128, c0:c0 + cw]).then_inc(k.q[QM[g % 4]].h, 16)
                        k.q[QM[g % 4]].n += 16

            @blk.gpsimd
            def _(g_):
                for si, (c0, cw) in enumerate(slices):
                    for j in range(NT):
                        ej = si * NT + j
                        if ej >= 2:
                            ej2 = ej - 2
                            g_.wait_ge(k.dv.h, evac_dv(ej2 // NT, ej2 % NT))
                        g_.dma_start(out=x1s[j % 2][:, 0:cw],
                                     in_=k.x1_scr[j * 128:(j + 1) * 128, c0:c0 + cw]).then_inc(k.q[8 + ej % 2].h, 16)
                        k.q[8 + ej % 2].n += 16

            @blk.vector
            def _(vector):
                for si, (c0, cw) in enumerate(slices):
                    for kk in range(32):
                        g = si * 32 + kk
                        vector.wait_ge(k.q[QM[g % 4]].h, qb0[g % 4] + 16 * (g // 4 + 1))
                        if si >= 2 and kk == 0:
                            vector.wait_ge(k.pe.h, tile_pe(si - 2, NT - 1))
                        vector.tensor_copy(out=wsb[si % 2][:, kk * CW:kk * CW + cw],
                                           in_=wst[g % 4][:, 0:cw]).then_inc(k.dv.h, 1)
                        k.dv.n += 1
                    for j in range(NT):
                        ej = si * NT + j
                        vector.wait_ge(k.pe.h, tile_pe(si, j))
                        vector.wait_ge(k.q[8 + ej % 2].h, qb2[ej % 2] + 16 * (ej // 2 + 1))
                        if ej >= 2:
                            vector.wait_ge(k.q[4 + ej % 2].h, qb4[ej % 2] + 16 * (ej // 2))
                        vector.tensor_add(out=ost[j % 2][:, 0:cw], in0=x1s[j % 2][:, 0:cw],
                                          in1=k.pf[j % 2][:, 0:cw]).then_inc(k.dv.h, 1)
                        k.dv.n += 1

            @blk.tensor
            def _(tensor):
                for si, (c0, cw) in enumerate(slices):
                    tensor.wait_ge(k.dv.h, dv0 + si * (32 + NT) + 32)
                    for j in range(NT):
                        if si * NT + j >= 2:
                            jj = si * NT + j - 2
                            tensor.wait_ge(k.dv.h, evac_dv(jj // NT, jj % NT))
                        pb = k.pf[j % 2]
                        last = None
                        for kk in range(32):
                            last = tensor.matmul(pb[:, 0:cw], k.yfm(kk)[:, j * 128:(j + 1) * 128],
                                                 wsb[si % 2][:, kk * CW:kk * CW + cw],
                                                 start=(kk == 0), stop=False)
                        tensor.matmul(pb[:, 0:cw], k.hwb[:, j * 128:(j + 1) * 128],
                                      k.bstb[:, c0:c0 + cw], start=False, stop=False)
                        last = tensor.matmul(pb[:, 0:cw], k.ones1[:],
                                             k.bbb[0:1, c0:c0 + cw], start=False, stop=True)
                        last.then_inc(k.pe.h, 1)
                        k.pe.n += 1

            @blk.scalar
            def _(scalar):
                for si, (c0, cw) in enumerate(slices):
                    for j in range(NT):
                        scalar.wait_ge(k.dv.h, evac_dv(si, j))
                        ej = si * NT + j
                        scalar.dma_start(out=k.out_d[j * 128:(j + 1) * 128, c0:c0 + cw],
                                         in_=ost[j % 2][:, 0:cw]).then_inc(k.q[4 + ej % 2].h, 16)
                        k.q[4 + ej % 2].n += 16
    _barrier(k)

def build_nc():
    nc = bass.Bass("TRN2", target_bir_lowering=False, detect_race_conditions=False)
    k = K()
    k.nc = nc

    dr = lambda n, s, d=F32, kind="ExternalInput": nc.dram_tensor(n, s, d, kind=kind).ap()
    k.x0h = dr("x0h", [TH, DM])
    k.w_in = dr("w_in", [DM, PROJ])
    k.w_op = dr("w_op", [DM, DM])
    k.wbT = dr("wbT", [DM, DM])
    k.wrT = dr("wrT", [DM, 8])
    k.aT = dr("aT", [DM, 128])
    k.bstk = dr("bstk", [128, DM])
    k.bb = dr("bb", [1, DM])
    k.conv_fm = dr("conv_fm", [128, 136])
    k.lnw_fm = dr("lnw_fm", [128, 32])
    k.nrmw_fm = dr("nrmw_fm", [128, 32])
    k.mupz = dr("mupz", [128, DM])
    k.mup_fm = dr("mup_fm", [128, 35])
    k.dt_bias = dr("dt_bias", [64, 1])
    k.a_log = dr("a_log", [64, 1])
    k.d_bc = dr("d_bc", [128, 64])
    k.d_row = dr("d_row", [1, DM])
    k.hmask = dr("hmask", [128, 1])
    k.ident_bf_d = dr("ident_bf", [128, 128], BF16)
    k.ident_f_d = dr("ident_f", [128, 128], F32)
    k.ones1_d = dr("ones1", [1, 128], BF16)
    k.r64_d = dr("r64", [64, DM], BF16)
    k.r8_d = dr("r8", [8, 128], BF16)
    k.m4_d = dr("m4", [128, 512], F32)
    k.ones64_d = dr("ones64", [64, 128], F32)
    k.out_d = dr("out", [TR, DM], F32, kind="ExternalOutput")

    k.z_scr = nc.dram_tensor("z_scr", [TR, DM], F32).ap()
    k.xbc_scr = nc.dram_tensor("xbc_scr", [CONVD, TH], F32).ap()
    k.y_scr = nc.dram_tensor("y_scr", [TR, DM], F32).ap()
    k.attn_scr = nc.dram_tensor("attn_scr", [TR, DM], F32).ap()
    k.x1_scr = nc.dram_tensor("x1_scr", [TR, DM], F32).ap()
    k.vsp_scr = nc.dram_tensor("vsp_scr", [64, 3 * TR], BF16).ap()
    k.csp_scr = nc.dram_tensor("csp_scr", [64, 3 * TR], BF16).ap()
    k.dt_scr = nc.dram_tensor("dt_scr", [64, TH], F32).ap()
    k.xtm_scr = nc.dram_tensor("xtm_scr", [TR, DM], BF16).ap()
    k.cc_in = nc.dram_tensor("cc_in", [128, DM], F32)
    k.cc_out = nc.dram_tensor("cc_out", [256, DM], F32)

    dbg_outs = {}
    k.dbgG = None
    k.dbgCB = None
    for name, shape in [("z", [TR, DM]), ("xbc", [CONVD, TH]), ("y", [TR, DM]),
                        ("attn", [TR, DM]), ("x1", [TR, DM]), ("H", [128, DM]),
                        ("dt", [64, TH]), ("xtm", [TR, DM]), ("xn", [128, 69632]), ("G", [128, 2048]), ("CB", [128, 512])]:
        if name in DBG:
            dbg_outs[name] = dr("dbg_" + name, shape, F32, kind="ExternalOutput")
            if name == "G":
                k.dbgG = dbg_outs[name]
            if name == "CB":
                k.dbgCB = dbg_outs[name]

    with ExitStack() as top:
        e = top.enter_context
        k.ld = Sem(top, nc, "ld")
        k.lg = Sem(top, nc, "lg")
        k.ls = Sem(top, nc, "ls")
        k.out = Sem(top, nc, "outs")
        k.pe = Sem(top, nc, "pe")
        k.dv = Sem(top, nc, "dv")
        k.ac = Sem(top, nc, "ac")
        k.po = Sem(top, nc, "po")
        k.cc = Sem(top, nc, "cc")
        k.q = [Sem(top, nc, f"q{i}") for i in range(12)]

        k.pf = [e(nc.psum_tensor(f"pf{i}", [128, 512], F32)) for i in range(6)]
        k.pbf = [e(nc.psum_tensor(f"pbf{i}", [128, 1024], BF16)) for i in range(2)]

        k.ident_bf = e(nc.sbuf_tensor("identbf", [128, 128], BF16))
        k.ident_f = e(nc.sbuf_tensor("identf", [128, 128], F32))
        k.ones1 = e(nc.sbuf_tensor("sones1", [1, 128], BF16))
        k.r8 = e(nc.sbuf_tensor("sr8", [8, 128], BF16))
        k.ones64 = e(nc.sbuf_tensor("sones64", [64, 128], F32))
        k.lnw = e(nc.sbuf_tensor("slnw", [128, 32], F32))
        k.nrmw = e(nc.sbuf_tensor("snrmw", [128, 32], F32))
        k.mupf = e(nc.sbuf_tensor("smupf", [128, 35], F32))
        k.convw = e(nc.sbuf_tensor("sconvw", [128, 136], F32))
        k.dbias = e(nc.sbuf_tensor("sdbias", [64, 1], F32))
        k.aA = e(nc.sbuf_tensor("saA", [64, 1], F32))
        k.dbc = e(nc.sbuf_tensor("sdbc", [128, 64], F32))
        k.hm = e(nc.sbuf_tensor("shm", [128, 1], F32))

        k.xn = lambda kk: k.arena[:, kk * TH:(kk + 1) * TH]

        with nc.Block() as blk:
            @blk.sync
            def _(sync):
                for dst, src in [(k.ident_bf, k.ident_bf_d), (k.ident_f, k.ident_f_d),
                                 (k.ones1, k.ones1_d), (k.r8, k.r8_d),
                                 (k.ones64, k.ones64_d), (k.lnw, k.lnw_fm),
                                 (k.nrmw, k.nrmw_fm), (k.mupf, k.mup_fm), (k.convw, k.conv_fm),
                                 (k.dbias, k.dt_bias), (k.aA, k.a_log), (k.dbc, k.d_bc),
                                 (k.hm, k.hmask)]:
                    sync.dma_start(out=dst[:], in_=src).then_inc(k.ld.h, 16)
                    k.ld.n += 16

            @blk.scalar
            def _(scalar):
                scalar.wait_ge(k.ld.h, k.ld.n)
                scalar.activation(out=k.aA[:], in_=k.aA[:], func=AF.Exp).then_inc(k.ac.h, 1)
                k.ac.n += 1

            @blk.vector
            def _(vector):
                vector.wait_ge(k.ac.h, k.ac.n)
                vector.tensor_scalar(out=k.aA[:], in0=k.aA[:], scalar1=-1.0, scalar2=None,
                                     op0=ALU.mult).then_inc(k.dv.h, 1)
                k.dv.n += 1
        _barrier(k)

        with ExitStack() as scA:
            k.arena = scA.enter_context(nc.sbuf_tensor("arena", [128, 69632], BF16))
            _ln_phase(k, k.x0h, lambda kk, tc: k.xn(kk)[:, tc * 128:(tc + 1) * 128], 17, with_rms=True)
            _barrier(k)
            if "xn" in DBG:
                with nc.Block() as blkx:
                    @blkx.gpsimd
                    def _(g):
                        g.dma_start(out=dbg_outs["xn"], in_=k.arena[:, 0:69632]).then_inc(k.lg.h, 16)
                        k.lg.n += 16
                        g.wait_ge(k.lg.h, k.lg.n)
                _barrier(k)
            if STOP == "P1":
                _dbg_dump(k, dbg_outs)
                return nc, dbg_outs

            _p2_inproj(k)
            _barrier(k)
        if STOP == "P2":
            _dbg_dump(k, dbg_outs)
            return nc, dbg_outs

        with ExitStack() as scP:
            eP = scP.enter_context
            k.bC = eP(nc.sbuf_tensor("bC", [128, TR], BF16))
            k.egl_bf = eP(nc.sbuf_tensor("eglbf", [64, TR], BF16))
            k.r64 = eP(nc.sbuf_tensor("sr64", [64, DM], BF16))
            with nc.Block() as blk0:
                @blk0.sync
                def _(sync):
                    sync.dma_start(out=k.r64[:], in_=k.r64_d).then_inc(k.ld.h, 16)
                    k.ld.n += 16

            with ExitStack() as scB:
                eB = scB.enter_context
                k.dt_t = eB(nc.sbuf_tensor("dtT", [64, TR], F32))
                k.cl_t = eB(nc.sbuf_tensor("clT", [64, TR], F32))
                k.bB = eB(nc.sbuf_tensor("bB", [128, TR], BF16))
                k.bBt = eB(nc.sbuf_tensor("bBt", [128, TR], BF16))
                k.t64g = eB(nc.sbuf_tensor("t64g", [64, 64], F32))
                _barrier(k)
                _p3_dt(k)
                _p3_conv(k)
                if STOP == "P3" or "xtm" in DBG:
                    with nc.Block() as blkx:
                        @blkx.gpsimd
                        def _(g):
                            if "xtm" in DBG:
                                g.dma_start(out=dbg_outs["xtm"], in_=k.xtm_scr).then_inc(k.lg.h, 16)
                                k.lg.n += 16
                            g.wait_ge(k.lg.h, k.lg.n)
                    _barrier(k)
                if STOP == "P3":
                    _dbg_dump(k, dbg_outs)
                    return nc, dbg_outs

                _p4_ssd(k)
                if STOP == "P4" and "H" in DBG:
                    with nc.Block() as blkh:
                        @blkh.gpsimd
                        def _(g):
                            g.dma_start(out=dbg_outs["H"], in_=k.H[:]).then_inc(k.lg.h, 16)
                            k.lg.n += 16
                            g.wait_ge(k.lg.h, k.lg.n)
                    _barrier(k)
                if STOP == "P4":
                    _dbg_dump(k, dbg_outs)
                    return nc, dbg_outs

            k.arena2 = e(nc.sbuf_tensor("arena2", [128, 65536], BF16, side="right"))
            k.yfm = lambda kk: k.arena2[:, kk * TR:(kk + 1) * TR]
            _p5_gate(k)
            if STOP == "P5p":
                _dbg_dump(k, dbg_outs)
                return nc, dbg_outs
            if STOP == "P5":
                _dbg_dump(k, dbg_outs)
                return nc, dbg_outs

        _p6_outproj(k)
        if STOP == "P6":
            _dbg_dump(k, dbg_outs)
            return nc, dbg_outs

        _ln_phase(k, k.x0h[128:TH, :], lambda kk, tc: k.yfm(kk)[:, tc * 128:(tc + 1) * 128],
                  16, with_rms=False, x1_mode=True, tag="b")
        _barrier(k)
        if STOP == "P7a":
            _dbg_dump(k, dbg_outs)
            return nc, dbg_outs

        k.bstb = e(nc.sbuf_tensor("bstb", [128, DM], BF16))
        k.bbb = e(nc.sbuf_tensor("bbb", [1, DM], BF16))
        k.hwb = e(nc.sbuf_tensor("hwb", [128, TR], BF16))
        _p7_router(k)
        if STOP == "P7":
            _dbg_dump(k, dbg_outs)
            return nc, dbg_outs

        _p8_ffn(k)
        _dbg_dump(k, dbg_outs)
    return nc, dbg_outs


def _dbg_dump(k, dbg_outs):
    nc = k.nc
    with nc.Block() as blk:
        @blk.gpsimd
        def _(g):
            for s in (k.ld, k.ls, k.lg, k.dv, k.pe, k.ac):
                if s.n:
                    g.wait_ge(s.h, s.n)
            for name, d in dbg_outs.items():
                src = {"z": k.z_scr, "xbc": k.xbc_scr, "y": k.y_scr,
                       "attn": k.attn_scr, "x1": k.x1_scr}.get(name)
                if src is not None:
                    g.dma_start(out=d, in_=src).then_inc(k.lg.h, 16)
                    k.lg.n += 16
                elif name == "dt":
                    g.dma_start(out=d, in_=k.dt_scr).then_inc(k.lg.h, 16)
                    k.lg.n += 16

            g.wait_ge(k.lg.h, k.lg.n)
            if k.out.n:
                g.wait_ge(k.out.h, k.out.n)


def _host_inputs(inp):
    x0 = np.asarray(inp["L0_attn_hidden_states"])
    ln_w = np.asarray(inp["L0_attn_ln_w"])
    w_in = np.ascontiguousarray(np.asarray(inp["L0_attn_in_proj_w"], np.float32))
    conv_w = np.asarray(inp["L0_attn_conv_w"])
    a_log = np.asarray(inp["L0_attn_A_log"])
    dt_b = np.asarray(inp["L0_attn_dt_bias"])
    Dv = np.asarray(inp["L0_attn_D"])
    norm_w = np.asarray(inp["L0_attn_norm_w"])
    w_op = np.ascontiguousarray(np.asarray(inp["L0_attn_out_proj_w"], np.float32))
    mup = np.asarray(inp["L0_attn_mup_vector"])
    wb = np.asarray(inp["L0_ffn_W_base"])
    bb = np.asarray(inp["L0_ffn_b_base"])
    wr = np.asarray(inp["L0_ffn_W_router"])
    Am = np.asarray(inp["L0_ffn_A"])
    Bm = np.asarray(inp["L0_ffn_B"])

    shared = {
        "w_in": w_in,
        "w_op": w_op,
        "wbT": np.ascontiguousarray(wb.T.astype(np.float32)),
        "wrT": np.ascontiguousarray(wr.T.astype(np.float32)),
        "aT": np.ascontiguousarray(Am.reshape(128, DM).T.astype(np.float32)),
        "bstk": np.ascontiguousarray(Bm.transpose(0, 2, 1).reshape(128, DM).astype(np.float32)),
        "bb": np.ascontiguousarray(bb.reshape(1, DM).astype(np.float32)),
        "conv_fm": np.ascontiguousarray(conv_w.reshape(34, 128, 4).transpose(1, 0, 2).reshape(128, 136).astype(np.float32)),
        "lnw_fm": np.ascontiguousarray(ln_w.reshape(32, 128).T.astype(np.float32)),
        "nrmw_fm": np.ascontiguousarray(norm_w.reshape(32, 128).T.astype(np.float32)),
        "mupz": np.ascontiguousarray(np.broadcast_to(mup[:DM], (128, DM)).astype(np.float32)),
        "mup_fm": np.ascontiguousarray(np.pad(mup[DM:].astype(np.float32), (0, 64)).reshape(35, 128).T),
        "dt_bias": np.ascontiguousarray(dt_b.reshape(64, 1).astype(np.float32)),
        "a_log": np.ascontiguousarray(a_log.reshape(64, 1).astype(np.float32)),
        "d_bc": np.ascontiguousarray(np.broadcast_to(np.asarray(Dv)[None, :], (128, 64)).astype(np.float32)),
        "d_row": np.ascontiguousarray(np.repeat(np.asarray(Dv), 64).reshape(1, DM).astype(np.float32)),
        "ident_bf": np.eye(128, dtype=BF),
        "ident_f": np.eye(128, dtype=np.float32),
        "ones1": np.ones((1, 128), dtype=BF),
        "r64": np.ascontiguousarray(np.repeat(np.eye(64, dtype=np.float32), 64, axis=1).astype(BF)),
        "r8": np.ascontiguousarray(np.repeat(np.eye(8, dtype=np.float32), 16, axis=1).astype(BF)),
        "m4": np.ascontiguousarray(np.tile(np.where(np.arange(128)[:, None] <= np.arange(128)[None, :], 0.0, -1e30).astype(np.float32), (1, 4))),
        "ones64": np.ones((64, 128), dtype=np.float32),
    }
    maps = []
    for c in range(8):
        b, h = c // 2, c % 2
        if h == 0:
            xh = np.concatenate([np.zeros((128, DM), np.float32), np.asarray(x0[b, 0:TR], np.float32)], axis=0)
        else:
            xh = np.ascontiguousarray(np.asarray(x0[b, TR - 128:2 * TR], np.float32))
        m = dict(shared)
        m["x0h"] = xh
        m["hmask"] = np.full((128, 1), float(h), np.float32)
        maps.append(m)
    return maps


def kernel(**inputs):
    maps = _host_inputs(inputs)
    nc, _ = build_nc()
    res = run_bass_kernel_spmd(nc, maps, core_ids=list(range(8)))
    out = np.empty((4, 4096, DM), np.float32)
    for c in range(8):
        b, h = c // 2, c % 2
        out[b, h * TR:(h + 1) * TR] = np.asarray(res.results[c]["out"])
    return out


# version 06
# speedup vs baseline: 1.1209x; 1.0067x over previous
import numpy as np
import ml_dtypes
from contextlib import ExitStack

import concourse.bass as bass
import concourse.mybir as mybir
from concourse.bass_utils import run_bass_kernel_spmd

F32 = mybir.dt.float32
BF16 = mybir.dt.bfloat16
AF = mybir.ActivationFunctionType
ALU = mybir.AluOpType
AX = mybir.AxisListType.X
BF = ml_dtypes.bfloat16

DM = 4096
PROJ = 8512
CONVD = 4352
NH, HD, SS = 64, 64, 128
TR = 2048
TH = 2176
NCH = 16
EPS = 1e-5
P5LIM = 16
P7PART = 3

DBG = set()
STOP = None
NCORES = 8


class Sem:
    def __init__(self, ctx, nc, name):
        self.h = ctx.enter_context(nc.semaphore(name))
        self.n = 0

    def inc(self, instr, k=1):
        instr.then_inc(self.h, k)
        self.n += k
        return self.n


class K:
    pass


def _barrier(k):
    nc = k.nc
    sems = [k.ld, k.lg, k.ls, k.pe, k.dv, k.ac, k.po, k.cc, k.out] + list(k.q)
    vals = [(s.h, s.n) for s in sems if s.n]
    with nc.Block() as blk:
        for eng_name in ("sync", "tensor", "vector", "scalar", "gpsimd"):
            def mk(eng, _v=vals):
                for h, n in _v:
                    eng.wait_ge(h, n)
            getattr(blk, eng_name)(mk)


def _ln_phase(k, src_dram, dst_view, nchunks, with_rms, x1_mode=False, tag=""):
    nc = k.nc
    npc = 15 if x1_mode else 14
    ldpc = 32 if x1_mode else 16
    ld0, dv0, ac0, pe0, ls0 = k.ld.n, k.dv.n, k.ac.n, k.pe.n, k.ls.n
    q0_base = [k.q[0].n, k.q[1].n]
    q2_base = k.q[2].n
    off = 1 if x1_mode else 0

    with ExitStack() as ctx:
        e = ctx.enter_context
        xt = [e(nc.sbuf_tensor(f"xt{tag}{i}", [128, DM], F32)) for i in range(2)]
        xb = [e(nc.sbuf_tensor(f"xb{tag}{i}", [128, DM], BF16)) for i in range(2)]
        at = e(nc.sbuf_tensor(f"at{tag}", [128, DM], F32)) if x1_mode else None
        stats = e(nc.sbuf_tensor(f"lnst{tag}", [128, 48], F32))
        mv = e(nc.sbuf_tensor(f"lnmv{tag}", [128, 8], F32))

        with nc.Block() as blk:
            @blk.sync
            def _(sync):
                for i in range(nchunks):
                    if i >= 2:
                        sync.wait_ge(k.dv.h, dv0 + npc * (i - 2) + off + 10)
                        if x1_mode:
                            sync.wait_ge(k.ls.h, ls0 + 16 * (i - 1))
                    qs = k.q[i % 2]
                    sync.dma_start(out=xt[i % 2][:], in_=src_dram[i * 128:(i + 1) * 128, :]).then_inc(qs.h, 16)
                    qs.n += 16
                    if x1_mode:
                        if i >= 1:
                            sync.wait_ge(k.dv.h, dv0 + npc * (i - 1) + 1)
                        sync.dma_start(out=at[:], in_=k.attn_scr[i * 128:(i + 1) * 128, :]).then_inc(k.q[2].h, 16)
                        k.q[2].n += 16

            @blk.vector
            def _(vector):
                for i in range(nchunks):
                    sl = i % 2
                    base = dv0 + npc * i
                    vector.wait_ge(k.q[i % 2].h, q0_base[i % 2] + 16 * (i // 2 + 1))
                    if x1_mode:
                        vector.wait_ge(k.q[2].h, q2_base + 16 * (i + 1))
                        vector.tensor_add(out=xt[sl][:], in0=xt[sl][:], in1=at[:]).then_inc(k.dv.h, 1)
                        k.dv.n += 1
                    for q in range(8):
                        ins_ = vector.bn_stats(out=stats[:, q * 6:(q + 1) * 6],
                                               in_=xt[sl][:, q * 512:(q + 1) * 512])
                    ins_.then_inc(k.dv.h, 1)
                    k.dv.n += 1
                    vector.wait_ge(k.dv.h, base + off + 1)
                    def _step(ins_):
                        ins_.then_inc(k.dv.h, 1)
                        k.dv.n += 1
                        k._tmpc += 1
                        vector.wait_ge(k.dv.h, k.dv.n)
                    k._tmpc = 0
                    _step(vector.bn_aggr(out=mv[:, 0:2], in_=stats[:, 0:48]))
                    _step(vector.tensor_scalar(out=mv[:, 2:3], in0=mv[:, 1:2], scalar1=EPS, scalar2=None, op0=ALU.add))
                    _step(vector.reciprocal(out=mv[:, 3:4], in_=mv[:, 2:3]))
                    if with_rms:
                        _step(vector.tensor_tensor(out=mv[:, 4:5], in0=mv[:, 1:2], in1=mv[:, 3:4], op=ALU.mult))
                        _step(vector.tensor_scalar(out=mv[:, 4:5], in0=mv[:, 4:5], scalar1=EPS, scalar2=None, op0=ALU.add))
                        _step(vector.reciprocal(out=mv[:, 5:6], in_=mv[:, 4:5]))
                    else:
                        for _ in range(3):
                            _step(vector.engine_nop())
                    vector.engine_nop().then_inc(k.dv.h, 1)
                    k.dv.n += 1
                    vector.wait_ge(k.ac.h, ac0 + (i + 1))
                    if with_rms:
                        vector.tensor_tensor(out=mv[:, 6:7], in0=mv[:, 6:7], in1=mv[:, 7:8], op=ALU.mult).then_inc(k.dv.h, 1)
                        k.dv.n += 1
                        vector.wait_ge(k.dv.h, base + off + 9)
                    else:
                        vector.engine_nop().then_inc(k.dv.h, 1)
                        k.dv.n += 1
                    vector.tensor_scalar(out=xb[sl][:], in0=xt[sl][:], scalar1=mv[:, 0:1], scalar2=mv[:, 6:7],
                                         op0=ALU.subtract, op1=ALU.mult).then_inc(k.dv.h, 1)
                    k.dv.n += 1
                    for g in range(4):
                        vector.wait_ge(k.pe.h, pe0 + 32 * i + 8 * (g + 1))
                        pb = k.pbf[g % 2]
                        last = None
                        for j in range(8):
                            kk = g * 8 + j
                            last = vector.tensor_copy(out=dst_view(kk, i), in_=pb[:, j * 128:(j + 1) * 128])
                        last.then_inc(k.dv.h, 1)
                        k.dv.n += 1

            @blk.scalar
            def _(scalar):
                for i in range(nchunks):
                    base = dv0 + npc * i
                    scalar.wait_ge(k.dv.h, base + off + 8)
                    if with_rms:
                        scalar.activation(out=mv[:, 6:7], in_=mv[:, 3:4], func=AF.Sqrt)
                        scalar.activation(out=mv[:, 7:8], in_=mv[:, 5:6], func=AF.Sqrt).then_inc(k.ac.h, 1)
                    else:
                        scalar.activation(out=mv[:, 6:7], in_=mv[:, 3:4], func=AF.Sqrt).then_inc(k.ac.h, 1)
                    k.ac.n += 1
                    if x1_mode:
                        scalar.wait_ge(k.dv.h, base + 1)
                        scalar.dma_start(out=k.x1_scr[i * 128:(i + 1) * 128, :], in_=xt[i % 2][:]).then_inc(k.ls.h, 16)
                        k.ls.n += 16

            @blk.tensor
            def _(tensor):
                for i in range(nchunks):
                    base = dv0 + npc * i
                    tensor.wait_ge(k.dv.h, base + off + 10)
                    for g in range(4):
                        gg = 4 * i + g
                        if gg >= 2:
                            i2, g2 = (gg - 2) // 4, (gg - 2) % 4
                            tensor.wait_ge(k.dv.h, dv0 + npc * i2 + off + 10 + g2 + 1)
                        pb = k.pbf[g % 2]
                        last = None
                        for j in range(8):
                            kk = g * 8 + j
                            last = tensor.transpose(pb[:, j * 128:(j + 1) * 128],
                                                    xb[i % 2][:, kk * 128:(kk + 1) * 128], k.ident_bf[:])
                        last.then_inc(k.pe.h, 8)
                        k.pe.n += 8


def _p2_inproj(k):
    nc = k.nc
    ZT = list(range(1, 17))
    TWS = [(0, 512), (512, 512), (1024, 512), (1536, 512), (2048, 128)]
    CW = 384
    FM0 = 11
    slices = [(s * CW, CW, "z") for s in range(10)] + [(3840, 256, "z")] + \
             [(4096 + s * CW, CW, "fm") for s in range(11)] + [(8320, 192, "fm")]

    def nevac(sl):
        c0, cw, kind = sl
        return 16 if kind == "z" else ((cw + 127) // 128) * 5

    def ntile(sl):
        return nevac(sl)

    dcum = [0]
    for sl in slices:
        dcum.append(dcum[-1] + 32 + nevac(sl))
    pcum = [0]
    for sl in slices:
        pcum.append(pcum[-1] + ntile(sl))

    def cast_dv(g):
        si, kk = g // 32, g % 32
        return dcum[si] + kk + 1

    def cast_dv2(g2):
        si, kk2 = g2 // 16, g2 % 16
        return cast_dv(si * 32 + kk2 * 2 + 1)

    def evac_dv(si, e):
        return dcum[si] + 32 + e + 1

    def tile_pe(si, j):
        return pcum[si] + j + 1

    ld0, dv0, pe0, ls0 = k.ld.n, k.dv.n, k.pe.n, k.ls.n
    QM = [0, 1, 10, 11]
    qb2 = [k.q[i].n for i in QM]
    qb3 = [k.q[2].n, k.q[3].n]
    qb4 = [k.q[4].n, k.q[5].n]
    k.fm_evac_ctr = 0
    k.fm_store_ctr = 0

    with ExitStack() as ctx:
        e = ctx.enter_context
        wst = [e(nc.sbuf_tensor(f"wst{i}", [128, 768], F32)) for i in range(4)]
        wsb = [e(nc.sbuf_tensor(f"wsb{i}", [128, 32 * 384], BF16)) for i in range(2)]
        zst = [e(nc.sbuf_tensor(f"zst{i}", [128, 384], F32)) for i in range(2)]
        fst = [e(nc.sbuf_tensor(f"fst{i}", [128, 512], F32)) for i in range(2)]

        with nc.Block() as blk:
            @blk.sync
            def _(sync):
                for si, (c0, cw, kind) in enumerate(slices):
                    for kk2 in range(16):
                        g = si * 16 + kk2
                        if g >= 4:
                            sync.wait_ge(k.dv.h, dv0 + cast_dv2(g - 4))
                        sync.dma_start(out=wst[g % 4][:, 0:2 * cw].rearrange("p (a c) -> p a c", a=2),
                                       in_=k.w_in[kk2 * 256:(kk2 + 1) * 256, c0:c0 + cw].rearrange("(a p) c -> p a c", p=128)).then_inc(k.q[QM[g % 4]].h, 16)
                        k.q[QM[g % 4]].n += 16

            @blk.vector
            def _(vector):
                for si, (c0, cw, kind) in enumerate(slices):
                    for kk in range(32):
                        g2 = si * 16 + kk // 2
                        vector.wait_ge(k.q[QM[g2 % 4]].h, qb2[g2 % 4] + 16 * (g2 // 4 + 1))
                        if si >= 2 and kk == 0:
                            vector.wait_ge(k.pe.h, pe0 + pcum[si - 1])
                        vector.tensor_scalar(out=wsb[si % 2][:, kk * CW:kk * CW + cw],
                                             in0=wst[g2 % 4][:, (kk % 2) * cw:(kk % 2) * cw + cw],
                                             scalar1=k.lnw[:, kk:kk + 1], scalar2=None,
                                             op0=ALU.mult).then_inc(k.dv.h, 1)
                        k.dv.n += 1
                    if kind == "z":
                        for j in range(16):
                            ze = si * 16 + j
                            vector.wait_ge(k.pe.h, pe0 + tile_pe(si, j))
                            if ze >= 2:
                                vector.wait_ge(k.q[2 + ze % 2].h, qb3[ze % 2] + 16 * (ze // 2))
                            pb = k.pf[j % 2]
                            vector.tensor_copy(out=zst[j % 2][:, 0:cw],
                                                 in_=pb[:, 0:cw]).then_inc(k.dv.h, 1)
                            k.dv.n += 1
                    else:
                        nmc = (cw + 127) // 128
                        for m in range(nmc):
                            mw = min(128, cw - m * 128)
                            ch = (c0 + m * 128 - 4096) // 128
                            for j, (t0, tw) in enumerate(TWS):
                                idx = m * 5 + j
                                fe = k.fm_evac_ctr; k.fm_evac_ctr += 1
                                vector.wait_ge(k.pe.h, pe0 + tile_pe(si, idx))
                                if fe >= 2:
                                    vector.wait_ge(k.q[4 + fe % 2].h, qb4[fe % 2] + 16 * (fe // 2))
                                pb = k.pf[2 + idx % 2]
                                vector.tensor_scalar(out=fst[idx % 2][0:mw, 0:tw], in0=pb[0:mw, 0:tw],
                                                     scalar1=k.mupf[0:mw, ch:ch + 1],
                                                     scalar2=None, op0=ALU.mult).then_inc(k.dv.h, 1)
                                k.dv.n += 1

            @blk.tensor
            def _(tensor):
                for si, (c0, cw, kind) in enumerate(slices):
                    tensor.wait_ge(k.dv.h, dv0 + dcum[si] + 32)
                    if kind == "z":
                        for j, tc in enumerate(ZT):
                            if si * 16 + j >= 2:
                                jj = si * 16 + j - 2
                                si2, j2 = jj // 16, jj % 16
                                tensor.wait_ge(k.dv.h, dv0 + evac_dv(si2, j2))
                            pb = k.pf[j % 2]
                            last = None
                            for kk in range(32):
                                last = tensor.matmul(pb[:, 0:cw], k.xn(kk)[:, tc * 128:(tc + 1) * 128],
                                                     wsb[si % 2][:, kk * CW:kk * CW + cw],
                                                     start=(kk == 0), stop=(kk == 31))
                            last.then_inc(k.pe.h, 1)
                            k.pe.n += 1
                    else:
                        nmc = (cw + 127) // 128
                        for m in range(nmc):
                            mw = min(128, cw - m * 128)
                            for j, (t0, tw) in enumerate(TWS):
                                idx = m * 5 + j
                                gidx = sum(ntile(slices[s2]) for s2 in range(FM0, si)) + idx
                                if gidx >= 2:
                                    pj = gidx - 2
                                    acc = 0
                                    for s2 in range(FM0, len(slices)):
                                        nt = ntile(slices[s2])
                                        if pj < acc + nt:
                                            tensor.wait_ge(k.dv.h, dv0 + evac_dv(s2, pj - acc))
                                            break
                                        acc += nt
                                pb = k.pf[2 + idx % 2]
                                last = None
                                for kk in range(32):
                                    last = tensor.matmul(pb[0:mw, 0:tw],
                                                         wsb[si % 2][:, kk * CW + m * 128: kk * CW + m * 128 + mw],
                                                         k.xn(kk)[:, t0:t0 + tw],
                                                         start=(kk == 0), stop=(kk == 31))
                                last.then_inc(k.pe.h, 1)
                                k.pe.n += 1

            @blk.scalar
            def _(scalar):
                for si, (c0, cw, kind) in enumerate(slices):
                    if kind == "z":
                        for j, tc in enumerate(ZT):
                            scalar.wait_ge(k.dv.h, dv0 + evac_dv(si, j))
                            ze2 = si * 16 + j
                            scalar.dma_start(out=k.z_scr[(tc - 1) * 128:tc * 128, c0:c0 + cw],
                                             in_=zst[j % 2][:, 0:cw]).then_inc(k.q[2 + ze2 % 2].h, 16)
                            k.q[2 + ze2 % 2].n += 16
                    else:
                        nmc = (cw + 127) // 128
                        for m in range(nmc):
                            mw = min(128, cw - m * 128)
                            ch0 = c0 + m * 128 - 4096
                            for j, (t0, tw) in enumerate(TWS):
                                idx = m * 5 + j
                                scalar.wait_ge(k.dv.h, dv0 + evac_dv(si, idx))
                                fe2 = k.fm_store_ctr; k.fm_store_ctr += 1
                                if ch0 < CONVD:
                                    scalar.dma_start(out=k.xbc_scr[ch0:ch0 + mw, t0:t0 + tw],
                                                     in_=fst[idx % 2][0:mw, 0:tw]).then_inc(k.q[4 + fe2 % 2].h, 16)
                                else:
                                    scalar.dma_start(out=k.dt_scr[0:64, t0:t0 + tw],
                                                     in_=fst[idx % 2][0:64, 0:tw]).then_inc(k.q[4 + fe2 % 2].h, 16)
                                k.q[4 + fe2 % 2].n += 16




def _p3_dt(k):
    nc = k.nc
    with ExitStack() as ctx:
        e = ctx.enter_context
        dtf = e(nc.sbuf_tensor("dtf", [64, TH], F32))
        tA = e(nc.sbuf_tensor("tA", [64, TR], F32))
        tB = e(nc.sbuf_tensor("tB", [64, TR], F32))
        sp = e(nc.sbuf_tensor("sp", [64, TR], BF16))
        ldt = dtf[:, 0:TR]
        spf = dtf[:, 0:TR]
        ld0, dv0, ac0, lg0 = k.ld.n, k.dv.n, k.ac.n, k.lg.n
        t64 = k.t64g
        k._sp_pos = []

        with nc.Block() as blk:
            @blk.sync
            def _(sync):
                sync.dma_start(out=dtf[:], in_=k.dt_scr).then_inc(k.ld.h, 16)
                k.ld.n += 16

            @blk.vector
            def _(vector):
                def dstep(ins_):
                    ins_.then_inc(k.dv.h, 1)
                    k.dv.n += 1
                    vector.wait_ge(k.dv.h, k.dv.n)
                vector.wait_ge(k.ld.h, ld0 + 16)
                dstep(vector.tensor_scalar(out=tA[:], in0=dtf[:, 128:TH], scalar1=k.dbias[:, 0:1],
                                           scalar2=None, op0=ALU.add))
                dstep(vector.tensor_scalar(out=tB[:], in0=tA[:], scalar1=0.0, scalar2=None, op0=ALU.max))
                dstep(vector.engine_nop())
                vector.wait_ge(k.ac.h, ac0 + 3)
                dstep(vector.tensor_add(out=k.dt_t[:], in0=tB[:], in1=ldt[:]))
                dstep(vector.tensor_scalar(out=tB[:], in0=k.dt_t[:], scalar1=1e-30, scalar2=None, op0=ALU.max))
                dstep(vector.tensor_scalar(out=tA[:], in0=k.dt_t[:], scalar1=k.aA[:, 0:1],
                                           scalar2=None, op0=ALU.mult))
                for c in range(NCH):
                    dstep(vector.tensor_tensor_scan(out=k.cl_t[:, c * 128:(c + 1) * 128],
                                                    data0=k.ones64[:, 0:128],
                                                    data1=tA[:, c * 128:(c + 1) * 128],
                                                    initial=0.0, op0=ALU.mult, op1=ALU.add))
                dstep(vector.memset(t64[:, 16:17], 0.0))
                for c in range(NCH):
                    dstep(vector.tensor_copy(out=t64[:, c:c + 1], in_=k.cl_t[:, c * 128 + 127:c * 128 + 128]))
                for c in range(1, NCH):
                    dstep(vector.tensor_add(out=t64[:, 16 + c:17 + c], in0=t64[:, 15 + c:16 + c],
                                            in1=t64[:, c - 1:c]))
                vector.wait_ge(k.ac.h, ac0 + 4)
                for c in range(NCH):
                    dstep(vector.tensor_scalar(out=tB[:, c * 128:(c + 1) * 128],
                                               in0=k.cl_t[:, c * 128:(c + 1) * 128],
                                               scalar1=t64[:, 16 + c:17 + c], scalar2=None, op0=ALU.add))
                vector.wait_ge(k.ac.h, ac0 + 5)
                dstep(vector.tensor_copy(out=k.egl_bf[:], in_=tA[:]))
                dstep(vector.tensor_sub(out=tB[:], in0=ldt[:], in1=k.cl_t[:]))
                nd = 0
                for which, val in enumerate((tB, k.cl_t)):
                    cur = val
                    for part in range(3):
                        if nd:
                            vector.wait_ge(k.lg.h, lg0 + 16 * nd)
                        dstep(vector.tensor_copy(out=sp[:], in_=cur[:]))
                        k._sp_pos.append(k.dv.n)
                        nd += 1
                        if part < 2:
                            dstep(vector.tensor_copy(out=spf[:], in_=sp[:]))
                            dstep(vector.tensor_sub(out=tA[:], in0=cur[:], in1=spf[:]))
                            cur = tA
                dstep(vector.tensor_copy(out=t64[:, 32:48], in_=t64[:, 0:16]))
                vector.wait_ge(k.ac.h, ac0 + 6)
                vector.wait_ge(k.lg.h, lg0 + 16 * nd)

            @blk.scalar
            def _(scalar):
                def astep(ins_):
                    ins_.then_inc(k.ac.h, 1)
                    k.ac.n += 1
                    scalar.wait_ge(k.ac.h, k.ac.n)
                scalar.wait_ge(k.dv.h, dv0 + 3)
                astep(scalar.activation(out=ldt[:], in_=tA[:], func=AF.Abs))
                astep(scalar.activation(out=tA[:], in_=ldt[:], func=AF.Exp, scale=-1.0))
                astep(scalar.activation(out=ldt[:], in_=tA[:], func=AF.Ln, bias=1.0))
                scalar.wait_ge(k.dv.h, dv0 + 5)
                astep(scalar.activation(out=ldt[:], in_=tB[:], func=AF.Ln))
                scalar.wait_ge(k.dv.h, dv0 + 70)
                astep(scalar.activation(out=tA[:], in_=tB[:], func=AF.Exp))
                scalar.wait_ge(k.dv.h, dv0 + 87)
                astep(scalar.activation(out=t64[:, 48:64], in_=t64[:, 32:48], func=AF.Exp))

            @blk.gpsimd
            def _(g):
                for di, dvpos in enumerate(k._sp_pos):
                    which, part = di // 3, di % 3
                    scr = k.vsp_scr if which == 0 else k.csp_scr
                    g.wait_ge(k.dv.h, dvpos)
                    g.dma_start(out=scr[:, part * TR:(part + 1) * TR], in_=sp[:]).then_inc(k.lg.h, 16)
                    k.lg.n += 16
                g.wait_ge(k.lg.h, k.lg.n)
    _barrier(k)


def _p3_conv(k):
    nc = k.nc
    with ExitStack() as ctx:
        e = ctx.enter_context
        cbuf = [e(nc.sbuf_tensor(f"cbuf{i}", [128, TH - 124], F32)) for i in range(2)]
        co = [e(nc.sbuf_tensor(f"co{i}", [128, TR], F32)) for i in range(2)]
        cs = [e(nc.sbuf_tensor(f"cs{i}", [128, TR], F32)) for i in range(2)]
        xsb = [e(nc.sbuf_tensor(f"xsb{i}", [128, TR], BF16)) for i in range(2)]
        xts = [e(nc.sbuf_tensor(f"xts{i}", [128, TR], BF16)) for i in range(2)]
        qb0 = [k.q[0].n, k.q[1].n]
        qb2 = [k.q[2].n, k.q[3].n]
        dv0, pe0, ac0, po0 = k.dv.n, k.pe.n, k.ac.n, k.po.n
        tap4 = {}
        castp = {}
        evacg = {}
        ndv = dv0
        npo = po0
        for q in range(34):
            ndv += 4
            tap4[q] = ndv
            npo += 1
            castp[q] = npo
            qq = q - 1
            if 0 <= qq <= 32:
                for g in range(2):
                    ndv += 1
                    evacg[(qq, g)] = ndv

        with nc.Block() as blk:
            @blk.sync
            def _(sync):
                for q in range(34):
                    if q >= 2:
                        sync.wait_ge(k.dv.h, tap4[q - 2])
                    sync.dma_start(out=cbuf[q % 2][:], in_=k.xbc_scr[q * 128:(q + 1) * 128, 124:TH]).then_inc(k.q[q % 2].h, 16)
                    k.q[q % 2].n += 16

            @blk.vector
            def _(vector):
                for q in range(34):
                    cb = cbuf[q % 2]
                    c0_, c1_ = co[q % 2], cs[q % 2]
                    vector.wait_ge(k.q[q % 2].h, qb0[q % 2] + 16 * (q // 2 + 1))
                    if q >= 2:
                        vector.wait_ge(k.po.h, castp[q - 2])
                    vector.tensor_scalar(out=c0_[:], in0=cb[:, 1:1 + TR], scalar1=k.convw[:, 4 * q:4 * q + 1],
                                         scalar2=None, op0=ALU.mult).then_inc(k.dv.h, 1)
                    k.dv.n += 1
                    vector.scalar_tensor_tensor(out=c1_[:], in0=cb[:, 2:2 + TR], scalar=k.convw[:, 4 * q + 1:4 * q + 2],
                                                in1=c0_[:], op0=ALU.mult, op1=ALU.add).then_inc(k.dv.h, 1)
                    k.dv.n += 1
                    vector.scalar_tensor_tensor(out=c0_[:], in0=cb[:, 3:3 + TR], scalar=k.convw[:, 4 * q + 2:4 * q + 3],
                                                in1=c1_[:], op0=ALU.mult, op1=ALU.add).then_inc(k.dv.h, 1)
                    k.dv.n += 1
                    vector.scalar_tensor_tensor(out=c1_[:], in0=cb[:, 4:4 + TR], scalar=k.convw[:, 4 * q + 3:4 * q + 4],
                                                in1=c0_[:], op0=ALU.mult, op1=ALU.add).then_inc(k.dv.h, 1)
                    k.dv.n += 1
                    qq = q - 1
                    if 0 <= qq <= 32:
                        for g in range(2):
                            vector.wait_ge(k.pe.h, pe0 + 16 * qq + 8 * (g + 1))
                            if qq >= 2:
                                vector.wait_ge(k.q[2 + qq % 2].h, qb2[qq % 2] + 256 * (qq // 2))
                            pb = k.pbf[g % 2]
                            last = None
                            for j in range(8):
                                tc = g * 8 + j
                                dstv = (xts[qq % 2][:, tc * 128:(tc + 1) * 128] if qq < 32
                                        else k.bBt[:, tc * 128:(tc + 1) * 128])
                                last = vector.tensor_copy(out=dstv, in_=pb[:, j * 128:(j + 1) * 128])
                            last.then_inc(k.dv.h, 1)
                            k.dv.n += 1

            @blk.gpsimd
            def _(g_):
                for q in range(34):
                    g_.wait_ge(k.ac.h, ac0 + q + 1)
                    dst = xsb[q % 2] if q < 32 else (k.bB if q == 32 else k.bC)
                    g_.tensor_copy(out=dst[:], in_=co[q % 2][:]).then_inc(k.po.h, 1)
                    k.po.n += 1

            @blk.scalar
            def _(scalar):
                for q in range(34):
                    scalar.wait_ge(k.dv.h, tap4[q])
                    scalar.activation(out=co[q % 2][:], in_=cs[q % 2][:], func=AF.Silu).then_inc(k.ac.h, 1)
                    k.ac.n += 1
                    qq = q - 1
                    if 0 <= qq < 32:
                        for g in range(2):
                            scalar.wait_ge(k.dv.h, evacg[(qq, g)])
                            for j in range(8):
                                tc = g * 8 + j
                                scalar.dma_start(out=k.xtm_scr[tc * 128:(tc + 1) * 128, qq * 128:(qq + 1) * 128],
                                                 in_=xts[qq % 2][:, tc * 128:(tc + 1) * 128]).then_inc(k.q[2 + qq % 2].h, 16)
                                k.q[2 + qq % 2].n += 16

            @blk.tensor
            def _(tensor):
                for q in range(33):
                    tensor.wait_ge(k.po.h, castp[q])
                    src2 = (xsb[q % 2] if q < 32 else k.bB)
                    for g in range(2):
                        gg = 2 * q + g
                        if gg >= 2:
                            q2, g2 = (gg - 2) // 2, (gg - 2) % 2
                            tensor.wait_ge(k.dv.h, evacg[(q2, g2)])
                        pb = k.pbf[g % 2]
                        last = None
                        for j in range(8):
                            tc = g * 8 + j
                            last = tensor.transpose(pb[:, j * 128:(j + 1) * 128],
                                                    src2[:, tc * 128:(tc + 1) * 128], k.ident_bf[:])
                        last.then_inc(k.pe.h, 8)
                        k.pe.n += 8
    _barrier(k)


def _p4_ssd(k):
    nc = k.nc
    with ExitStack() as ctx:
        e = ctx.enter_context
        xtt = [e(nc.sbuf_tensor(f"xtt{i}", [128, DM], BF16)) for i in range(2)]
        m4b = e(nc.sbuf_tensor("m4b", [128, 512], BF16))
        combL = e(nc.sbuf_tensor("combL", [6, 4096], BF16))
        combR = e(nc.sbuf_tensor("combR", [6, 4096], BF16))
        G = e(nc.sbuf_tensor("G", [128, 8192], BF16))
        estg = e(nc.sbuf_tensor("estg", [128, 1024], F32))
        e2 = e(nc.sbuf_tensor("e2", [128, 1024], F32))
        CBr = e(nc.sbuf_tensor("CBr", [128, 512], F32))
        eclT = e(nc.sbuf_tensor("eclT", [64, 128], BF16))
        wfT = e(nc.sbuf_tensor("wfT", [64, 128], BF16))
        gmT = e(nc.sbuf_tensor("gmT", [64, 128], BF16))
        tmpF = e(nc.sbuf_tensor("tmpF", [64, 128], F32))
        tmpG = e(nc.sbuf_tensor("tmpG", [64, 128], F32))
        tmpG2 = e(nc.sbuf_tensor("tmpG2", [64, 128], F32))
        tmpF2 = e(nc.sbuf_tensor("tmpF2", [64, 128], F32))
        eclb = e(nc.sbuf_tensor("eclb", [128, 1024], BF16))
        hbf = e(nc.sbuf_tensor("hbf", [128, 4096], BF16))
        wx = e(nc.sbuf_tensor("wx", [128, 1024], BF16))
        xds = e(nc.sbuf_tensor("xds", [128, 1024], BF16))
        drep = e(nc.sbuf_tensor("drep", [128, 4096], BF16))
        drow = e(nc.sbuf_tensor("drow", [1, 4096], BF16))
        yst = e(nc.sbuf_tensor("yst", [128, 512], F32))
        yst2 = e(nc.sbuf_tensor("yst2", [128, 1024], F32))
        hmt = e(nc.sbuf_tensor("hmt", [128, 512], F32))
        k.H = e(nc.sbuf_tensor("H", [128, DM], F32))

        with nc.Block() as blk0:
            ac00, pe00 = k.ac.n, k.pe.n

            @blk0.vector
            def _(vector):
                vector.memset(k.H[:], 0.0)
                vector.memset(combL[:], 1.0)
                vector.memset(combR[:], 1.0).then_inc(k.dv.h, 1)
                k.dv.n += 1

            @blk0.gpsimd
            def _(g):
                g.dma_start(out=m4b[:], in_=k.m4_d).then_inc(k.lg.h, 16)
                k.lg.n += 16
                g.dma_start(out=drow[:], in_=k.d_row).then_inc(k.lg.h, 16)
                k.lg.n += 16

            @blk0.tensor
            def _(tensor):
                tensor.wait_ge(k.lg.h, k.lg.n)
                for sl8 in range(8):
                    if sl8 >= 2:
                        tensor.wait_ge(k.ac.h, ac00 + sl8 - 1)
                    tensor.matmul(k.pf[sl8 % 2][:], k.ones1[:], drow[0:1, sl8 * 512:(sl8 + 1) * 512],
                                  start=True, stop=True).then_inc(k.pe.h, 1)
                    k.pe.n += 1

            @blk0.scalar
            def _(scalar):
                for sl8 in range(8):
                    scalar.wait_ge(k.pe.h, pe00 + sl8 + 1)
                    scalar.copy(out=drep[:, sl8 * 512:(sl8 + 1) * 512],
                                in_=k.pf[sl8 % 2][:]).then_inc(k.ac.h, 1)
                    k.ac.n += 1
        _barrier(k)

        ld0, lg0, ls0, dv0, pe0, ac0, po0 = k.ld.n, k.lg.n, k.ls.n, k.dv.n, k.pe.n, k.ac.n, k.po.n
        qb0 = [k.q[0].n, k.q[1].n]
        qb2 = [k.q[6].n, k.q[7].n]
        qb4 = [k.q[4].n, k.q[5].n]
        DV, PE_, AC, PO = 96, 73, 18, 8

        def dvC(c, p):
            return dv0 + DV * c + p

        def peC(c, p):
            return pe0 + PE_ * c + p

        def acC(c, p):
            return ac0 + AC * c + p

        def poC(c, p):
            return po0 + PO * c + p

        pe_cb = 1
        pe_dg = lambda i: 2 + i
        pe_eclB = lambda j: 18 + 4 * j
        pe_xd = lambda j: 19 + 4 * j
        pe_intra = lambda j: 20 + 4 * j
        pe_raw = lambda j: 21 + 4 * j
        pe_wrep = lambda s: 50 + 3 * s
        pe_smm = lambda s: 51 + 3 * s
        pe_gmm = lambda s: 52 + 3 * s
        dv_eclT, dv_tsG, dv_wmul, dv_wfT, dv_gmT = 1, 2, 3, 4, 5
        dv_cbr = 6
        dv_em = lambda i: 9 + 2 * i
        dv_g = lambda i: 10 + 2 * i
        dv_eclb = lambda j: 41 + 4 * j
        dv_xd = lambda j: 42 + 4 * j
        dv_t1 = lambda j: 43 + 4 * j
        dv_y = lambda j: 44 + 4 * j
        dv_wx = lambda s: 73 + 3 * s
        dv_hm = lambda s: 74 + 3 * s
        dv_hadd = lambda s: 75 + 3 * s
        po_hbf = lambda j: 1 + j
        ac_e1, ac_e3 = 1, 2
        ac_exp = lambda i: 3 + i

        with nc.Block() as blk:
            @blk.sync
            def _(sync):
                for c in range(NCH):
                    if c >= 2:
                        sync.wait_ge(k.dv.h, dvC(c - 2, dv_wx(7)))
                        sync.wait_ge(k.pe.h, peC(c - 2, pe_raw(7)))
                    sync.dma_start(out=xtt[c % 2][:], in_=k.xtm_scr[c * 128:(c + 1) * 128, :]).then_inc(k.q[c % 2].h, 16)
                    k.q[c % 2].n += 16

            @blk.gpsimd
            def _(g):
                for c in range(NCH):
                    for r in range(4):
                        gr = 4 * c + r
                        if gr >= 2:
                            c2, r2 = (gr - 2) // 4, (gr - 2) % 4
                            g.wait_ge(k.pe.h, peC(c2, pe_dg(4 * r2 + 3)))
                        sl = r % 2
                        hb = 16 * r
                        g.dma_start(
                            out=combL[0:3, sl * 2048:(sl + 1) * 2048].rearrange("s (r t) -> s r t", r=16),
                            in_=k.vsp_scr[hb:hb + 16, :].rearrange("r (s ch t) -> s ch r t", s=3, ch=NCH)[:, c],
                        ).then_inc(k.q[6 + gr % 2].h, 16)
                        k.q[6 + gr % 2].n += 16
                        g.dma_start(
                            out=combR[3:6, sl * 2048:(sl + 1) * 2048].rearrange("s (r t) -> s r t", r=16),
                            in_=k.csp_scr[hb:hb + 16, :].rearrange("r (s ch t) -> s ch r t", s=3, ch=NCH)[:, c],
                        ).then_inc(k.q[6 + gr % 2].h, 16)
                        k.q[6 + gr % 2].n += 16
                    if c > 0:
                        g.wait_ge(k.dv.h, dvC(c - 1, dv_hadd(7)))
                    for j in range(8):
                        g.tensor_copy(out=hbf[:, j * 512:(j + 1) * 512],
                                      in_=k.H[:, j * 512:(j + 1) * 512]).then_inc(k.po.h, 1)
                        k.po.n += 1


            @blk.vector
            def _(vector):
                for c in range(NCH):
                    base = dv0 + DV * c

                    def dput(ins_, pos):
                        ins_.then_inc(k.dv.h, 1)
                        k.dv.n += 1
                        assert k.dv.n == base + pos, (c, pos, k.dv.n - base)

                    vector.wait_ge(k.q[c % 2].h, qb0[c % 2] + 16 * (c // 2 + 1))
                    vector.wait_ge(k.ac.h, acC(c, ac_e1))
                    dput(vector.tensor_copy(out=eclT[:], in_=tmpF[:]), dv_eclT)
                    dput(vector.tensor_scalar(out=tmpG[:], in0=k.cl_t[:, c * 128:(c + 1) * 128],
                                              scalar1=-1.0, scalar2=k.t64g[:, c:c + 1],
                                              op0=ALU.mult, op1=ALU.add), dv_tsG)
                    vector.wait_ge(k.dv.h, base + dv_tsG)
                    vector.wait_ge(k.ac.h, acC(c, ac_e3))
                    dput(vector.tensor_tensor(out=tmpF2[:], in0=tmpG2[:],
                                              in1=k.dt_t[:, c * 128:(c + 1) * 128], op=ALU.mult), dv_wmul)
                    vector.wait_ge(k.dv.h, base + dv_wmul)
                    dput(vector.tensor_copy(out=wfT[:], in_=tmpF2[:]), dv_wfT)
                    dput(vector.tensor_scalar(out=gmT[:], in0=k.ones64[:, 0:128],
                                              scalar1=k.t64g[:, 48 + c:49 + c], scalar2=None,
                                              op0=ALU.mult), dv_gmT)
                    vector.wait_ge(k.pe.h, peC(c, pe_cb))
                    dput(vector.tensor_copy(out=CBr[:, 0:128], in_=k.pf[4][:, 0:128]), dv_cbr)
                    vector.wait_ge(k.dv.h, base + dv_cbr)
                    dput(vector.tensor_copy(out=CBr[:, 128:256], in_=CBr[:, 0:128]), dv_cbr + 1)
                    vector.wait_ge(k.dv.h, base + dv_cbr + 1)
                    dput(vector.tensor_copy(out=CBr[:, 256:512], in_=CBr[:, 0:256]), dv_cbr + 2)
                    for i in range(16):
                        if c > 0 or i >= 2:
                            tgt = acC(c, ac_exp(i - 2)) if i >= 2 else acC(c - 1, ac_exp(14 + i))
                            vector.wait_ge(k.ac.h, tgt)
                        vector.wait_ge(k.pe.h, peC(c, pe_dg(i)))
                        pfD = k.pf[i % 2]
                        dput(vector.tensor_tensor(out=estg[:, (i % 2) * 512:(i % 2 + 1) * 512],
                                                  in0=pfD[:], in1=m4b[:], op=ALU.add), dv_em(i))
                        vector.wait_ge(k.ac.h, acC(c, ac_exp(i)))
                        if c > 0:
                            vector.wait_ge(k.pe.h, peC(c - 1, pe_intra(i // 2)))
                        dput(vector.tensor_tensor(out=G[:, i * 512:(i + 1) * 512],
                                                  in0=e2[:, (i % 2) * 512:(i % 2 + 1) * 512],
                                                  in1=CBr[:], op=ALU.mult), dv_g(i))
                    for j in range(8):
                        vector.wait_ge(k.pe.h, peC(c, pe_eclB(j)))
                        pfe = k.pf[4 + (j % 2)]
                        dput(vector.tensor_copy(out=eclb[:, (j % 2) * 512:(j % 2 + 1) * 512],
                                                in_=pfe[:]), dv_eclb(j))
                        dput(vector.tensor_tensor(out=xds[:, (j % 2) * 512:(j % 2 + 1) * 512],
                                                  in0=xtt[c % 2][:, j * 512:(j + 1) * 512],
                                                  in1=drep[:, j * 512:(j + 1) * 512], op=ALU.mult), dv_xd(j))
                        vector.wait_ge(k.dv.h, base + dv_xd(j))
                        vector.wait_ge(k.pe.h, peC(c, pe_raw(j)))
                        pfr = k.pf[5 - (j % 2)]
                        dput(vector.tensor_tensor(out=yst[:], in0=pfr[:],
                                                  in1=eclb[:, (j % 2) * 512:(j % 2 + 1) * 512],
                                                  op=ALU.mult), dv_t1(j))
                        vector.wait_ge(k.dv.h, base + dv_t1(j))
                        jj = 8 * c + j
                        if jj >= 2:
                            vector.wait_ge(k.q[4 + jj % 2].h, qb4[jj % 2] + 16 * (jj // 2))
                        pfy = k.pf[2 + (j % 2)]
                        dput(vector.tensor_add(out=yst2[:, (j % 2) * 512:(j % 2 + 1) * 512],
                                               in0=yst[:], in1=pfy[:]), dv_y(j))
                    for s in range(8):
                        vector.wait_ge(k.pe.h, peC(c, pe_wrep(s)))
                        pfw = k.pf[4 + (s % 2)]
                        dput(vector.tensor_tensor(out=wx[:, (s % 2) * 512:(s % 2 + 1) * 512],
                                                  in0=xtt[c % 2][:, s * 512:(s + 1) * 512],
                                                  in1=pfw[:], op=ALU.mult), dv_wx(s))
                        vector.wait_ge(k.pe.h, peC(c, pe_gmm(s)))
                        pfg = k.pf[2 + (s % 2)]
                        dput(vector.tensor_tensor(out=hmt[:], in0=k.H[:, s * 512:(s + 1) * 512],
                                                  in1=pfg[:], op=ALU.mult), dv_hm(s))
                        vector.wait_ge(k.dv.h, base + dv_hm(s))
                        vector.wait_ge(k.pe.h, peC(c, pe_smm(s)))
                        pfs = k.pf[5 - (s % 2)]
                        dput(vector.tensor_add(out=k.H[:, s * 512:(s + 1) * 512],
                                               in0=hmt[:], in1=pfs[:]), dv_hadd(s))
                        vector.wait_ge(k.dv.h, base + dv_hadd(s))

            @blk.scalar
            def _(scalar):
                for c in range(NCH):
                    if c > 0:
                        scalar.wait_ge(k.dv.h, dvC(c - 1, dv_eclT))
                    scalar.activation(out=tmpF[:], in_=k.cl_t[:, c * 128:(c + 1) * 128],
                                      func=AF.Exp).then_inc(k.ac.h, 1)
                    k.ac.n += 1
                    scalar.wait_ge(k.dv.h, dvC(c, dv_tsG))
                    scalar.activation(out=tmpG2[:], in_=tmpG[:], func=AF.Exp).then_inc(k.ac.h, 1)
                    k.ac.n += 1
                    for i in range(16):
                        scalar.wait_ge(k.dv.h, dvC(c, dv_em(i)))
                        scalar.activation(out=e2[:, (i % 2) * 512:(i % 2 + 1) * 512],
                                          in_=estg[:, (i % 2) * 512:(i % 2 + 1) * 512],
                                          func=AF.Exp).then_inc(k.ac.h, 1)
                        k.ac.n += 1
                    for j in range(8):
                        jj = 8 * c + j
                        scalar.wait_ge(k.dv.h, dvC(c, dv_y(j)))
                        scalar.dma_start(out=k.y_scr[c * 128:(c + 1) * 128, j * 512:(j + 1) * 512],
                                         in_=yst2[:, (j % 2) * 512:(j % 2 + 1) * 512]).then_inc(k.q[4 + jj % 2].h, 16)
                        k.q[4 + jj % 2].n += 16

            @blk.tensor
            def _(tensor):
                bankw = {i: 0 for i in range(6)}

                def bwait(tensor, b, consumer_dv):
                    if bankw[b]:
                        tensor.wait_ge(k.dv.h, bankw[b])
                    bankw[b] = consumer_dv

                for c in range(NCH):
                    base_pe = pe0 + PE_ * c

                    def pput(ins_, pos):
                        ins_.then_inc(k.pe.h, 1)
                        k.pe.n += 1
                        assert k.pe.n == base_pe + pos, (c, pos, k.pe.n - base_pe)

                    bwait(tensor, 4, dvC(c, dv_cbr))
                    pput(tensor.matmul(k.pf[4][:, 0:128], k.bB[:, c * 128:(c + 1) * 128],
                                       k.bC[:, c * 128:(c + 1) * 128], start=True, stop=True), pe_cb)
                    tensor.wait_ge(k.q[c % 2].h, qb0[c % 2] + 16 * (c // 2 + 1))
                    for i in range(16):
                        r = i // 4
                        gr = 4 * c + r
                        tensor.wait_ge(k.q[6 + gr % 2].h, qb2[gr % 2] + 32 * (gr // 2 + 1))
                        bwait(tensor, i % 2, dvC(c, dv_em(i)))
                        pfD = k.pf[i % 2]
                        sl = r % 2
                        last = None
                        for hh in range(4):
                            hl = (i % 4) * 4 + hh
                            last = tensor.matmul(pfD[:, hh * 128:(hh + 1) * 128],
                                                 combL[0:6, sl * 2048 + hl * 128: sl * 2048 + (hl + 1) * 128],
                                                 combR[0:6, sl * 2048 + hl * 128: sl * 2048 + (hl + 1) * 128],
                                                 start=(hh == 0), stop=(hh == 3))
                        pput(last, pe_dg(i))
                    for j in range(8):
                        be = 4 + (j % 2)
                        bwait(tensor, be, dvC(c, dv_eclb(j)))
                        tensor.wait_ge(k.dv.h, dvC(c, dv_eclT))
                        pput(tensor.matmul(k.pf[be][:], eclT[:],
                                           k.r64[:, j * 512:(j + 1) * 512], start=True, stop=True), pe_eclB(j))
                        by = 2 + (j % 2)
                        bwait(tensor, by, dvC(c, dv_y(j)))
                        tensor.wait_ge(k.dv.h, dvC(c, dv_xd(j)))
                        pput(tensor.matmul(k.pf[by][:], k.ident_bf[:],
                                           xds[:, (j % 2) * 512:(j % 2 + 1) * 512],
                                           start=True, stop=False), pe_xd(j))
                        tensor.wait_ge(k.dv.h, dvC(c, dv_g(2 * j + 1)))
                        last = None
                        for jj2 in range(8):
                            head = 8 * j + jj2
                            gcol = head * 128
                            last = tensor.matmul(k.pf[by][:, jj2 * 64:(jj2 + 1) * 64],
                                                 G[:, gcol:gcol + 128],
                                                 xtt[c % 2][:, head * 64:(head + 1) * 64],
                                                 start=False, stop=(jj2 == 7))
                        pput(last, pe_intra(j))
                        br = 5 - (j % 2)
                        bwait(tensor, br, dvC(c, dv_t1(j)))
                        tensor.wait_ge(k.po.h, poC(c, po_hbf(j)))
                        pput(tensor.matmul(k.pf[br][:], k.bC[:, c * 128:(c + 1) * 128],
                                           hbf[:, j * 512:(j + 1) * 512], start=True, stop=True), pe_raw(j))
                    for s in range(8):
                        bw = 4 + (s % 2)
                        bwait(tensor, bw, dvC(c, dv_wx(s)))
                        tensor.wait_ge(k.dv.h, dvC(c, dv_wfT))
                        pput(tensor.matmul(k.pf[bw][:], wfT[:], k.r64[:, s * 512:(s + 1) * 512],
                                           start=True, stop=True), pe_wrep(s))
                        bs = 5 - (s % 2)
                        bwait(tensor, bs, dvC(c, dv_hadd(s)))
                        tensor.wait_ge(k.dv.h, dvC(c, dv_wx(s)))
                        pput(tensor.matmul(k.pf[bs][:], k.bBt[:, c * 128:(c + 1) * 128],
                                           wx[:, (s % 2) * 512:(s % 2 + 1) * 512],
                                           start=True, stop=True), pe_smm(s))
                        bg = 2 + (s % 2)
                        bwait(tensor, bg, dvC(c, dv_hm(s)))
                        tensor.wait_ge(k.dv.h, dvC(c, dv_gmT))
                        pput(tensor.matmul(k.pf[bg][:], gmT[:], k.r64[:, s * 512:(s + 1) * 512],
                                           start=True, stop=True), pe_gmm(s))
    _barrier(k)

    with nc.Block() as blk:
        @blk.scalar
        def _(scalar):
            scalar.dma_start(out=k.cc_in[:, :], in_=k.H[:]).then_inc(k.ls.h, 16)
            k.ls.n += 16

        @blk.gpsimd
        def _(g):
            g.wait_ge(k.ls.h, k.ls.n)
            g.collective_compute("AllGather", ALU.bypass,
                                 replica_groups=[[i, i + 1] for i in range(0, NCORES, 2)],
                                 ins=[k.cc_in.ap().opt()], outs=[k.cc_out.ap().opt()]).then_inc(k.cc.h, 1)
            k.cc.n += 1
            g.wait_ge(k.cc.h, k.cc.n)
    _barrier(k)


def _p5_gate(k):
    nc = k.nc
    with ExitStack() as ctx:
        e = ctx.enter_context
        hbf5 = e(nc.sbuf_tensor("hbf5", [128, DM], BF16))
        ytile = e(nc.sbuf_tensor("ytile", [128, DM], F32))
        ztile = e(nc.sbuf_tensor("ztile", [128, DM], F32))
        mupz5 = e(nc.sbuf_tensor("mupz5", [128, DM], BF16))
        eglb5 = e(nc.sbuf_tensor("eglb5", [128, 512], BF16))
        yst5 = e(nc.sbuf_tensor("yst5", [128, 512], F32))
        sq5 = e(nc.sbuf_tensor("sq5", [128, 512], F32))
        acc5 = e(nc.sbuf_tensor("acc5", [128, 16], F32))

        with nc.Block() as blk0:
            lg00 = k.lg.n

            @blk0.gpsimd
            def _(g):
                g.wait_ge(k.cc.h, k.cc.n)
                g.dma_start(out=mupz5[:], in_=k.mupz).then_inc(k.lg.h, 16)
                k.lg.n += 16
                for sl in range(8):
                    if sl >= 1:
                        g.wait_ge(k.dv.h, k.dv.n + 2 * sl - 1)
                    g.dma_start(out=sq5[:], in_=k.cc_out[0:128, sl * 512:(sl + 1) * 512]).then_inc(k.lg.h, 16)
                    k.lg.n += 16

            @blk0.vector
            def _(vector):
                for sl in range(8):
                    vector.wait_ge(k.lg.h, lg00 + 16 * (sl + 2))
                    vector.tensor_scalar(out=yst5[:], in0=sq5[:], scalar1=k.hm[:, 0:1],
                                         scalar2=None, op0=ALU.mult).then_inc(k.dv.h, 1)
                    k.dv.n += 1
                    vector.wait_ge(k.dv.h, k.dv.n)
                    vector.tensor_copy(out=hbf5[:, sl * 512:(sl + 1) * 512], in_=yst5[:]).then_inc(k.dv.h, 1)
                    k.dv.n += 1
        _barrier(k)
        if STOP == "P5p":
            return

        ld0, dv0, pe0, ac0 = k.ld.n, k.dv.n, k.pe.n, k.ac.n
        DV5, PE5, AC5 = 45, 24, 17
        dv_eglb = lambda j: 1 + 3 * j
        dv_t15 = lambda j: 2 + 3 * j
        dv_ya = lambda j: 3 + 3 * j
        dv_zm = 25
        dv_gm = lambda s: 26 + s
        dv_red, dv_ts, dv_rcp, dv_yn = 34, 35, 36, 37
        dv_ev = lambda g: 38 + g
        pe_egl5 = lambda j: 1 + 2 * j
        pe_cor5 = lambda j: 2 + 2 * j
        pe_tr5 = lambda g: 17 + g
        ac_sil = lambda s: 1 + s
        ac_sq = lambda s: 9 + s
        ac_rst = 17

        def dvC(c, p):
            return dv0 + DV5 * c + p

        def peC(c, p):
            return pe0 + PE5 * c + p

        def acC(c, p):
            return ac0 + AC5 * c + p

        with nc.Block() as blk:
            @blk.sync
            def _(sync):
                for c in range(P5LIM):
                    if c >= 1:
                        sync.wait_ge(k.pe.h, peC(c - 1, pe_tr5(7)))
                        sync.wait_ge(k.dv.h, dvC(c - 1, dv_gm(7)))
                    sync.dma_start(out=ytile[:], in_=k.y_scr[c * 128:(c + 1) * 128, :]).then_inc(k.ld.h, 16)
                    k.ld.n += 16
                    sync.dma_start(out=ztile[:], in_=k.z_scr[c * 128:(c + 1) * 128, :]).then_inc(k.ld.h, 16)
                    k.ld.n += 16

            @blk.tensor
            def _(tensor):
                bankw = {i: 0 for i in range(6)}

                def bwait(b, cons):
                    if bankw[b]:
                        tensor.wait_ge(k.dv.h, bankw[b])
                    bankw[b] = cons

                for c in range(P5LIM):
                    base_pe = pe0 + PE5 * c

                    def pput(ins_, pos):
                        ins_.then_inc(k.pe.h, 1)
                        k.pe.n += 1
                        assert k.pe.n == base_pe + pos, (c, pos, k.pe.n - base_pe)

                    for j in range(8):
                        be = 4 + (j % 2)
                        bwait(be, dvC(c, dv_eglb(j)))
                        pput(tensor.matmul(k.pf[be][:], k.egl_bf[:, c * 128:(c + 1) * 128],
                                           k.r64[:, j * 512:(j + 1) * 512], start=True, stop=True), pe_egl5(j))
                        by = 2 + (j % 2)
                        bwait(by, dvC(c, dv_t15(j)))
                        pput(tensor.matmul(k.pf[by][:], k.bC[:, c * 128:(c + 1) * 128],
                                           hbf5[:, j * 512:(j + 1) * 512], start=True, stop=True), pe_cor5(j))
                    tensor.wait_ge(k.dv.h, dvC(c, dv_yn))
                    for g in range(8):
                        b = g % 2
                        bwait(b, dvC(c, dv_ev(g)))
                        last = None
                        for j in range(4):
                            kk = g * 4 + j
                            last = tensor.transpose(k.pf[b][:, j * 128:(j + 1) * 128],
                                                    ytile[:, kk * 128:(kk + 1) * 128], k.ident_f[:])
                        pput(last, pe_tr5(g))

            @blk.vector
            def _(vector):
                for c in range(P5LIM):
                    base = dv0 + DV5 * c

                    def dput(ins_, pos):
                        ins_.then_inc(k.dv.h, 1)
                        k.dv.n += 1
                        assert k.dv.n == base + pos, (c, pos, k.dv.n - base)

                    vector.wait_ge(k.ld.h, ld0 + 32 * (c + 1))
                    for j in range(8):
                        vector.wait_ge(k.pe.h, peC(c, pe_egl5(j)))
                        dput(vector.tensor_copy(out=eglb5[:], in_=k.pf[4 + (j % 2)][:]), dv_eglb(j))
                        vector.wait_ge(k.dv.h, base + dv_eglb(j))
                        vector.wait_ge(k.pe.h, peC(c, pe_cor5(j)))
                        dput(vector.tensor_tensor(out=yst5[:], in0=k.pf[2 + (j % 2)][:],
                                                  in1=eglb5[:], op=ALU.mult), dv_t15(j))
                        vector.wait_ge(k.dv.h, base + dv_t15(j))
                        dput(vector.tensor_add(out=ytile[:, j * 512:(j + 1) * 512],
                                               in0=yst5[:], in1=ytile[:, j * 512:(j + 1) * 512]), dv_ya(j))
                    dput(vector.tensor_tensor(out=ztile[:], in0=ztile[:], in1=mupz5[:], op=ALU.mult), dv_zm)
                    vector.wait_ge(k.dv.h, base + dv_zm)
                    for s in range(8):
                        vector.wait_ge(k.ac.h, acC(c, ac_sil(s)))
                        dput(vector.tensor_tensor(out=ytile[:, s * 512:(s + 1) * 512],
                                                  in0=ytile[:, s * 512:(s + 1) * 512],
                                                  in1=sq5[:] if False else ztile[:, s * 512:(s + 1) * 512],
                                                  op=ALU.mult), dv_gm(s))
                        vector.wait_ge(k.dv.h, base + dv_gm(s))
                    vector.wait_ge(k.ac.h, acC(c, ac_sq(7)))
                    dput(vector.reduce_sum(out=acc5[:, 8:9], in_=acc5[:, 0:8], axis=AX), dv_red)
                    vector.wait_ge(k.dv.h, base + dv_red)
                    dput(vector.tensor_scalar(out=acc5[:, 9:10], in0=acc5[:, 8:9],
                                              scalar1=1.0 / DM, scalar2=EPS, op0=ALU.mult, op1=ALU.add), dv_ts)
                    vector.wait_ge(k.dv.h, base + dv_ts)
                    dput(vector.reciprocal(out=acc5[:, 10:11], in_=acc5[:, 9:10]), dv_rcp)
                    vector.wait_ge(k.ac.h, acC(c, ac_rst))
                    dput(vector.tensor_scalar(out=ytile[:], in0=ytile[:], scalar1=acc5[:, 11:12],
                                              scalar2=None, op0=ALU.mult), dv_yn)
                    for g in range(8):
                        vector.wait_ge(k.pe.h, peC(c, pe_tr5(g)))
                        last = None
                        for j in range(4):
                            kk = g * 4 + j
                            last = vector.tensor_copy(out=k.yfm(kk)[:, c * 128:(c + 1) * 128],
                                                      in_=k.pf[g % 2][:, j * 128:(j + 1) * 128])
                        dput(last, dv_ev(g))

            @blk.scalar
            def _(scalar):
                for c in range(P5LIM):
                    for s in range(8):
                        scalar.wait_ge(k.dv.h, dvC(c, dv_zm))
                        scalar.activation(out=ztile[:, s * 512:(s + 1) * 512],
                                          in_=ztile[:, s * 512:(s + 1) * 512],
                                          func=AF.Silu).then_inc(k.ac.h, 1)
                        k.ac.n += 1
                    for s in range(8):
                        scalar.wait_ge(k.dv.h, dvC(c, dv_gm(s)))
                        scalar.activation(out=sq5[:], in_=ytile[:, s * 512:(s + 1) * 512],
                                          func=AF.Square,
                                          accum_out=acc5[:, s:s + 1]).then_inc(k.ac.h, 1)
                        k.ac.n += 1
                    scalar.wait_ge(k.dv.h, dvC(c, dv_rcp))
                    scalar.activation(out=acc5[:, 11:12], in_=acc5[:, 10:11],
                                      func=AF.Sqrt).then_inc(k.ac.h, 1)
                    k.ac.n += 1
    _barrier(k)


def _p6_outproj(k):
    nc = k.nc
    with ExitStack() as ctx:
        e = ctx.enter_context
        wst = [e(nc.sbuf_tensor(f"wst6{i}", [128, 1024], F32)) for i in range(2)]
        wsb = [e(nc.sbuf_tensor(f"wsb6{i}", [128, 32 * 512], BF16)) for i in range(2)]
        ast = [e(nc.sbuf_tensor(f"ast{i}", [128, 512], F32)) for i in range(2)]
        ld0, dv0, pe0, ls0 = k.ld.n, k.dv.n, k.pe.n, k.ls.n
        QM = [0, 1, 10, 11]
        qb0 = [k.q[i].n for i in QM]
        qb2 = [k.q[2].n, k.q[3].n]
        NT = 16

        def cast_dv(g):
            si, kk = g // 32, g % 32
            return dv0 + si * 48 + kk + 1

        def cast_dv2(g2):
            si, kk2 = g2 // 16, g2 % 16
            return cast_dv(si * 32 + kk2 * 2 + 1)

        def evac_dv(si, j):
            return dv0 + si * 48 + 32 + j + 1

        def tile_pe(si, j):
            return pe0 + si * NT + j + 1

        with nc.Block() as blk:
            @blk.sync
            def _(sync):
                for si in range(8):
                    for kk2 in range(16):
                        g = si * 16 + kk2
                        if g >= 2:
                            sync.wait_ge(k.dv.h, cast_dv2(g - 2))
                        sync.dma_start(out=wst[g % 2][:].rearrange("p (a c) -> p a c", a=2),
                                       in_=k.w_op[kk2 * 256:(kk2 + 1) * 256, si * 512:(si + 1) * 512].rearrange("(a p) c -> p a c", p=128)).then_inc(k.q[QM[g % 2]].h, 16)
                        k.q[QM[g % 2]].n += 16

            @blk.vector
            def _(vector):
                for si in range(8):
                    for kk in range(32):
                        g2 = si * 16 + kk // 2
                        vector.wait_ge(k.q[QM[g2 % 2]].h, qb0[g2 % 2] + 16 * (g2 // 2 + 1))
                        if si >= 2 and kk == 0:
                            vector.wait_ge(k.pe.h, tile_pe(si - 2, NT - 1))
                        vector.tensor_scalar(out=wsb[si % 2][:, kk * 512:(kk + 1) * 512],
                                             in0=wst[g2 % 2][:, (kk % 2) * 512:(kk % 2 + 1) * 512],
                                             scalar1=k.nrmw[:, kk:kk + 1], scalar2=None,
                                             op0=ALU.mult).then_inc(k.dv.h, 1)
                        k.dv.n += 1
                    for j in range(NT):
                        ej = si * NT + j
                        vector.wait_ge(k.pe.h, tile_pe(si, j))
                        if ej >= 2:
                            vector.wait_ge(k.q[2 + ej % 2].h, qb2[ej % 2] + 16 * (ej // 2))
                        vector.tensor_copy(out=ast[j % 2][:], in_=k.pf[j % 2][:]).then_inc(k.dv.h, 1)
                        k.dv.n += 1

            @blk.tensor
            def _(tensor):
                for si in range(8):
                    tensor.wait_ge(k.dv.h, dv0 + si * 48 + 32)
                    for j in range(NT):
                        if si * NT + j >= 2:
                            jj = si * NT + j - 2
                            tensor.wait_ge(k.dv.h, evac_dv(jj // NT, jj % NT))
                        pb = k.pf[j % 2]
                        last = None
                        for kk in range(32):
                            last = tensor.matmul(pb[:], k.yfm(kk)[:, j * 128:(j + 1) * 128],
                                                 wsb[si % 2][:, kk * 512:(kk + 1) * 512],
                                                 start=(kk == 0), stop=(kk == 31))
                        last.then_inc(k.pe.h, 1)
                        k.pe.n += 1

            @blk.scalar
            def _(scalar):
                for si in range(8):
                    for j in range(NT):
                        ej = si * NT + j
                        scalar.wait_ge(k.dv.h, evac_dv(si, j))
                        scalar.dma_start(out=k.attn_scr[j * 128:(j + 1) * 128, si * 512:(si + 1) * 512],
                                         in_=ast[j % 2][:]).then_inc(k.q[2 + ej % 2].h, 16)
                        k.q[2 + ej % 2].n += 16
    _barrier(k)



def _p7_router(k):
    nc = k.nc
    with ExitStack() as ctx:
        e = ctx.enter_context
        atb = e(nc.sbuf_tensor("at7b", [128, DM], BF16))
        wrtb = e(nc.sbuf_tensor("wrtb", [128, 256], BF16))
        w8tb = e(nc.sbuf_tensor("w8tb", [8, TR], BF16))
        lg8 = e(nc.sbuf_tensor("lg8", [128, 48], F32))
        wr_sb = e(nc.sbuf_tensor("wr_sb", [128, 512], F32))
        wst7 = [e(nc.sbuf_tensor(f"wst7{i}", [128, 512], F32)) for i in range(2)]
        ld0, dv0, pe0, ac0 = k.ld.n, k.dv.n, k.pe.n, k.ac.n
        qb0 = [k.q[0].n, k.q[1].n]

        with nc.Block() as blk:
            @blk.gpsimd
            def _(g):
                g.dma_start(out=k.bbb[:], in_=k.bb).then_inc(k.lg.h, 16)
                k.lg.n += 16
                g.wait_ge(k.lg.h, k.lg.n)

            @blk.sync
            def _(sync):
                for kk in range(32):
                    if kk >= 2:
                        sync.wait_ge(k.dv.h, dv0 + kk - 1)
                    sync.dma_start(out=wst7[kk % 2][:, 0:128],
                                   in_=k.aT[kk * 128:(kk + 1) * 128, :]).then_inc(k.q[kk % 2].h, 16)
                    k.q[kk % 2].n += 16
                for kk in range(32):
                    g = 32 + kk
                    sync.wait_ge(k.dv.h, dv0 + g - 1)
                    sync.dma_start(out=wst7[g % 2][:, 0:8],
                                   in_=k.wrT[kk * 128:(kk + 1) * 128, :]).then_inc(k.q[g % 2].h, 16)
                    k.q[g % 2].n += 16
                for sl in range(8):
                    g = 64 + sl
                    sync.wait_ge(k.dv.h, dv0 + g - 1)
                    sync.dma_start(out=wst7[g % 2][:, 0:512],
                                   in_=k.bstk[:, sl * 512:(sl + 1) * 512]).then_inc(k.q[g % 2].h, 16)
                    k.q[g % 2].n += 16


            @blk.vector
            def _(vector):
                def dstep(ins_):
                    ins_.then_inc(k.dv.h, 1)
                    k.dv.n += 1
                    vector.wait_ge(k.dv.h, k.dv.n)
                for kk in range(32):
                    vector.wait_ge(k.q[kk % 2].h, qb0[kk % 2] + 16 * (kk // 2 + 1))
                    dstep(vector.tensor_copy(out=atb[:, kk * 128:(kk + 1) * 128], in_=wst7[kk % 2][:, 0:128]))
                for kk in range(32):
                    g = 32 + kk
                    vector.wait_ge(k.q[g % 2].h, qb0[g % 2] + 16 * (g // 2 + 1))
                    dstep(vector.tensor_copy(out=wrtb[:, kk * 8:(kk + 1) * 8], in_=wst7[g % 2][:, 0:8]))
                for sl in range(8):
                    g = 64 + sl
                    vector.wait_ge(k.q[g % 2].h, qb0[g % 2] + 16 * (g // 2 + 1))
                    dstep(vector.tensor_scalar(out=k.bstb[:, sl * 512:(sl + 1) * 512],
                                               in0=wst7[g % 2][:, 0:512], scalar1=2.0,
                                               scalar2=None, op0=ALU.mult))
                k._p7base = k.dv.n
                k._p7evac, k._p7sub, k._p7fin, k._p7w8, k._p7h = [], [], [], [], []
                for j in range(NCH if P7PART >= 2 else 0):
                    vector.wait_ge(k.pe.h, pe0 + 2 * j + 1)
                    dstep(vector.tensor_copy(out=lg8[:, 0:8], in_=k.pf[4][:, 0:8]))
                    k._p7evac.append(k.dv.n)
                    dstep(vector.reduce_max(out=lg8[:, 40:41], in_=lg8[:, 0:8], axis=AX))
                    dstep(vector.tensor_scalar(out=lg8[:, 8:16], in0=lg8[:, 0:8], scalar1=lg8[:, 40:41],
                                               scalar2=None, op0=ALU.is_equal))
                    dstep(vector.scalar_tensor_tensor(out=lg8[:, 16:24], in0=lg8[:, 8:16], scalar=-1e30,
                                                      in1=lg8[:, 0:8], op0=ALU.mult, op1=ALU.add))
                    dstep(vector.reduce_max(out=lg8[:, 41:42], in_=lg8[:, 16:24], axis=AX))
                    dstep(vector.tensor_sub(out=lg8[:, 42:43], in0=lg8[:, 40:41], in1=lg8[:, 41:42]))
                    k._p7sub.append(k.dv.n)
                    vector.wait_ge(k.ac.h, ac0 + j + 1)
                    dstep(vector.tensor_scalar(out=lg8[:, 44:45], in0=lg8[:, 43:44], scalar1=-1.0,
                                               scalar2=1.0, op0=ALU.mult, op1=ALU.add))
                    dstep(vector.tensor_scalar(out=lg8[:, 24:32], in0=lg8[:, 16:24], scalar1=lg8[:, 41:42],
                                               scalar2=None, op0=ALU.is_equal))
                    dstep(vector.tensor_scalar(out=lg8[:, 32:40], in0=lg8[:, 8:16], scalar1=lg8[:, 43:44],
                                               scalar2=None, op0=ALU.mult))
                    dstep(vector.scalar_tensor_tensor(out=lg8[:, 32:40], in0=lg8[:, 24:32],
                                                      scalar=lg8[:, 44:45], in1=lg8[:, 32:40],
                                                      op0=ALU.mult, op1=ALU.add))
                    k._p7fin.append(k.dv.n)
                    vector.wait_ge(k.pe.h, pe0 + 2 * j + 2)
                    dstep(vector.tensor_copy(out=w8tb[:, j * 128:(j + 1) * 128], in_=k.pf[5][0:8, 0:128]))
                    k._p7w8.append(k.dv.n)
                for t5 in range(4 if P7PART >= 3 else 0):
                    vector.wait_ge(k.pe.h, pe0 + 32 + 2 * t5 + 1)
                    dstep(vector.tensor_copy(out=wr_sb[:], in_=k.pf[3][:]))
                    vector.wait_ge(k.pe.h, pe0 + 32 + 2 * t5 + 2)
                    dstep(vector.tensor_tensor(out=k.hwb[:, t5 * 512:(t5 + 1) * 512],
                                               in0=k.pf[2][:], in1=wr_sb[:], op=ALU.mult))
                    k._p7h.append(k.dv.n)

            @blk.scalar
            def _(scalar):
                for j in range(NCH if P7PART >= 2 else 0):
                    scalar.wait_ge(k.dv.h, k._p7sub[j])
                    scalar.activation(out=lg8[:, 43:44], in_=lg8[:, 42:43],
                                      func=AF.Sigmoid).then_inc(k.ac.h, 1)
                    k.ac.n += 1

            @blk.tensor
            def _(tensor):
                for j in range(NCH if P7PART >= 2 else 0):
                    tensor.wait_ge(k.dv.h, dv0 + 64)
                    if j >= 1:
                        tensor.wait_ge(k.dv.h, k._p7evac[j - 1])
                    last = None
                    for kk in range(32):
                        last = tensor.matmul(k.pf[4][:, 0:8], k.yfm(kk)[:, j * 128:(j + 1) * 128],
                                             wrtb[:, kk * 8:(kk + 1) * 8],
                                             start=(kk == 0), stop=(kk == 31))
                    last.then_inc(k.pe.h, 1)
                    k.pe.n += 1
                    tensor.wait_ge(k.dv.h, k._p7fin[j])
                    if j >= 1:
                        tensor.wait_ge(k.dv.h, k._p7w8[j - 1])
                    tensor.transpose(k.pf[5][0:8, 0:128], lg8[:, 32:40], k.ident_f[:]).then_inc(k.pe.h, 1)
                    k.pe.n += 1
                for t5 in range(4 if P7PART >= 3 else 0):
                    tensor.wait_ge(k.dv.h, dv0 + 32)
                    if t5 >= 1:
                        tensor.wait_ge(k.dv.h, k._p7h[t5 - 1])
                    tensor.wait_ge(k.dv.h, k._p7w8[NCH - 1])
                    tensor.matmul(k.pf[3][:], k.r8[:], w8tb[:, t5 * 512:(t5 + 1) * 512],
                                  start=True, stop=True).then_inc(k.pe.h, 1)
                    k.pe.n += 1
                    last = None
                    for kk in range(32):
                        last = tensor.matmul(k.pf[2][:],
                                             atb[:, kk * 128:(kk + 1) * 128],
                                             k.yfm(kk)[:, t5 * 512:(t5 + 1) * 512],
                                             start=(kk == 0), stop=(kk == 31))
                    last.then_inc(k.pe.h, 1)
                    k.pe.n += 1
    _barrier(k)


def _p8_ffn(k):
    nc = k.nc
    with ExitStack() as ctx:
        e = ctx.enter_context
        CW = 256
        slices = [(s * CW, CW) for s in range(16)]
        wst = [e(nc.sbuf_tensor(f"wst8{i}", [128, 2 * CW], F32)) for i in range(4)]
        wsb = [e(nc.sbuf_tensor(f"wsb8{i}", [128, 32 * CW], BF16)) for i in range(2)]
        x1s = [e(nc.sbuf_tensor(f"x1s{i}", [128, CW], F32)) for i in range(2)]
        ost = [e(nc.sbuf_tensor(f"ost{i}", [128, CW], F32)) for i in range(2)]
        ld0, dv0, pe0, ls0, lg0 = k.ld.n, k.dv.n, k.pe.n, k.ls.n, k.lg.n
        QM = [0, 1, 10, 11]
        qb0 = [k.q[i].n for i in QM]
        qb2 = [k.q[8].n, k.q[9].n]
        qb4 = [k.q[4].n, k.q[5].n]
        NT = 16

        def cast_dv(g):
            si, kk = g // 32, g % 32
            return dv0 + si * (32 + NT) + kk + 1

        def cast_dv2(g2):
            si, kk2 = g2 // 16, g2 % 16
            return cast_dv(si * 32 + kk2 * 2 + 1)

        def evac_dv(si, j):
            return dv0 + si * (32 + NT) + 32 + j + 1

        def tile_pe(si, j):
            return pe0 + si * NT + j + 1

        with nc.Block() as blk:
            @blk.sync
            def _(sync):
                for si, (c0, cw) in enumerate(slices):
                    for kk2 in range(16):
                        g = si * 16 + kk2
                        if g >= 4:
                            sync.wait_ge(k.dv.h, cast_dv2(g - 4))
                        sync.dma_start(out=wst[g % 4][:, 0:2 * cw].rearrange("p (a c) -> p a c", a=2),
                                       in_=k.wbT[kk2 * 256:(kk2 + 1) * 256, c0:c0 + cw].rearrange("(a p) c -> p a c", p=128)).then_inc(k.q[QM[g % 4]].h, 16)
                        k.q[QM[g % 4]].n += 16

            @blk.gpsimd
            def _(g_):
                for si, (c0, cw) in enumerate(slices):
                    for j in range(NT):
                        ej = si * NT + j
                        if ej >= 2:
                            ej2 = ej - 2
                            g_.wait_ge(k.dv.h, evac_dv(ej2 // NT, ej2 % NT))
                        g_.dma_start(out=x1s[j % 2][:, 0:cw],
                                     in_=k.x1_scr[j * 128:(j + 1) * 128, c0:c0 + cw]).then_inc(k.q[8 + ej % 2].h, 16)
                        k.q[8 + ej % 2].n += 16

            @blk.vector
            def _(vector):
                for si, (c0, cw) in enumerate(slices):
                    for kk in range(32):
                        g2 = si * 16 + kk // 2
                        vector.wait_ge(k.q[QM[g2 % 4]].h, qb0[g2 % 4] + 16 * (g2 // 4 + 1))
                        if si >= 2 and kk == 0:
                            vector.wait_ge(k.pe.h, tile_pe(si - 2, NT - 1))
                        vector.tensor_copy(out=wsb[si % 2][:, kk * CW:kk * CW + cw],
                                           in_=wst[g2 % 4][:, (kk % 2) * cw:(kk % 2) * cw + cw]).then_inc(k.dv.h, 1)
                        k.dv.n += 1
                    for j in range(NT):
                        ej = si * NT + j
                        vector.wait_ge(k.pe.h, tile_pe(si, j))
                        vector.wait_ge(k.q[8 + ej % 2].h, qb2[ej % 2] + 16 * (ej // 2 + 1))
                        if ej >= 2:
                            vector.wait_ge(k.q[4 + ej % 2].h, qb4[ej % 2] + 16 * (ej // 2))
                        vector.tensor_add(out=ost[j % 2][:, 0:cw], in0=x1s[j % 2][:, 0:cw],
                                          in1=k.pf[j % 2][:, 0:cw]).then_inc(k.dv.h, 1)
                        k.dv.n += 1

            @blk.tensor
            def _(tensor):
                for si, (c0, cw) in enumerate(slices):
                    tensor.wait_ge(k.dv.h, dv0 + si * (32 + NT) + 32)
                    for j in range(NT):
                        if si * NT + j >= 2:
                            jj = si * NT + j - 2
                            tensor.wait_ge(k.dv.h, evac_dv(jj // NT, jj % NT))
                        pb = k.pf[j % 2]
                        last = None
                        for kk in range(32):
                            last = tensor.matmul(pb[:, 0:cw], k.yfm(kk)[:, j * 128:(j + 1) * 128],
                                                 wsb[si % 2][:, kk * CW:kk * CW + cw],
                                                 start=(kk == 0), stop=False)
                        tensor.matmul(pb[:, 0:cw], k.hwb[:, j * 128:(j + 1) * 128],
                                      k.bstb[:, c0:c0 + cw], start=False, stop=False)
                        last = tensor.matmul(pb[:, 0:cw], k.ones1[:],
                                             k.bbb[0:1, c0:c0 + cw], start=False, stop=True)
                        last.then_inc(k.pe.h, 1)
                        k.pe.n += 1

            @blk.scalar
            def _(scalar):
                for si, (c0, cw) in enumerate(slices):
                    for j in range(NT):
                        scalar.wait_ge(k.dv.h, evac_dv(si, j))
                        ej = si * NT + j
                        scalar.dma_start(out=k.out_d[j * 128:(j + 1) * 128, c0:c0 + cw],
                                         in_=ost[j % 2][:, 0:cw]).then_inc(k.q[4 + ej % 2].h, 16)
                        k.q[4 + ej % 2].n += 16
    _barrier(k)

def build_nc():
    nc = bass.Bass("TRN2", target_bir_lowering=False, detect_race_conditions=False)
    k = K()
    k.nc = nc

    dr = lambda n, s, d=F32, kind="ExternalInput": nc.dram_tensor(n, s, d, kind=kind).ap()
    k.x0h = dr("x0h", [TH, DM])
    k.w_in = dr("w_in", [DM, PROJ])
    k.w_op = dr("w_op", [DM, DM])
    k.wbT = dr("wbT", [DM, DM])
    k.wrT = dr("wrT", [DM, 8])
    k.aT = dr("aT", [DM, 128])
    k.bstk = dr("bstk", [128, DM])
    k.bb = dr("bb", [1, DM])
    k.conv_fm = dr("conv_fm", [128, 136])
    k.lnw_fm = dr("lnw_fm", [128, 32])
    k.nrmw_fm = dr("nrmw_fm", [128, 32])
    k.mupz = dr("mupz", [128, DM])
    k.mup_fm = dr("mup_fm", [128, 35])
    k.dt_bias = dr("dt_bias", [64, 1])
    k.a_log = dr("a_log", [64, 1])
    k.d_bc = dr("d_bc", [128, 64])
    k.d_row = dr("d_row", [1, DM])
    k.hmask = dr("hmask", [128, 1])
    k.ident_bf_d = dr("ident_bf", [128, 128], BF16)
    k.ident_f_d = dr("ident_f", [128, 128], F32)
    k.ones1_d = dr("ones1", [1, 128], BF16)
    k.r64_d = dr("r64", [64, DM], BF16)
    k.r8_d = dr("r8", [8, 128], BF16)
    k.m4_d = dr("m4", [128, 512], F32)
    k.ones64_d = dr("ones64", [64, 128], F32)
    k.out_d = dr("out", [TR, DM], F32, kind="ExternalOutput")

    k.z_scr = nc.dram_tensor("z_scr", [TR, DM], F32).ap()
    k.xbc_scr = nc.dram_tensor("xbc_scr", [CONVD, TH], F32).ap()
    k.y_scr = nc.dram_tensor("y_scr", [TR, DM], F32).ap()
    k.attn_scr = nc.dram_tensor("attn_scr", [TR, DM], F32).ap()
    k.x1_scr = nc.dram_tensor("x1_scr", [TR, DM], F32).ap()
    k.vsp_scr = nc.dram_tensor("vsp_scr", [64, 3 * TR], BF16).ap()
    k.csp_scr = nc.dram_tensor("csp_scr", [64, 3 * TR], BF16).ap()
    k.dt_scr = nc.dram_tensor("dt_scr", [64, TH], F32).ap()
    k.xtm_scr = nc.dram_tensor("xtm_scr", [TR, DM], BF16).ap()
    k.cc_in = nc.dram_tensor("cc_in", [128, DM], F32)
    k.cc_out = nc.dram_tensor("cc_out", [256, DM], F32)

    dbg_outs = {}
    k.dbgG = None
    k.dbgCB = None
    for name, shape in [("z", [TR, DM]), ("xbc", [CONVD, TH]), ("y", [TR, DM]),
                        ("attn", [TR, DM]), ("x1", [TR, DM]), ("H", [128, DM]),
                        ("dt", [64, TH]), ("xtm", [TR, DM]), ("xn", [128, 69632]), ("G", [128, 2048]), ("CB", [128, 512])]:
        if name in DBG:
            dbg_outs[name] = dr("dbg_" + name, shape, F32, kind="ExternalOutput")
            if name == "G":
                k.dbgG = dbg_outs[name]
            if name == "CB":
                k.dbgCB = dbg_outs[name]

    with ExitStack() as top:
        e = top.enter_context
        k.ld = Sem(top, nc, "ld")
        k.lg = Sem(top, nc, "lg")
        k.ls = Sem(top, nc, "ls")
        k.out = Sem(top, nc, "outs")
        k.pe = Sem(top, nc, "pe")
        k.dv = Sem(top, nc, "dv")
        k.ac = Sem(top, nc, "ac")
        k.po = Sem(top, nc, "po")
        k.cc = Sem(top, nc, "cc")
        k.q = [Sem(top, nc, f"q{i}") for i in range(12)]

        k.pf = [e(nc.psum_tensor(f"pf{i}", [128, 512], F32)) for i in range(6)]
        k.pbf = [e(nc.psum_tensor(f"pbf{i}", [128, 1024], BF16)) for i in range(2)]

        k.ident_bf = e(nc.sbuf_tensor("identbf", [128, 128], BF16))
        k.ident_f = e(nc.sbuf_tensor("identf", [128, 128], F32))
        k.ones1 = e(nc.sbuf_tensor("sones1", [1, 128], BF16))
        k.r8 = e(nc.sbuf_tensor("sr8", [8, 128], BF16))
        k.ones64 = e(nc.sbuf_tensor("sones64", [64, 128], F32))
        k.lnw = e(nc.sbuf_tensor("slnw", [128, 32], F32))
        k.nrmw = e(nc.sbuf_tensor("snrmw", [128, 32], F32))
        k.mupf = e(nc.sbuf_tensor("smupf", [128, 35], F32))
        k.convw = e(nc.sbuf_tensor("sconvw", [128, 136], F32))
        k.dbias = e(nc.sbuf_tensor("sdbias", [64, 1], F32))
        k.aA = e(nc.sbuf_tensor("saA", [64, 1], F32))
        k.dbc = e(nc.sbuf_tensor("sdbc", [128, 64], F32))
        k.hm = e(nc.sbuf_tensor("shm", [128, 1], F32))

        k.xn = lambda kk: k.arena[:, kk * TH:(kk + 1) * TH]

        with nc.Block() as blk:
            @blk.sync
            def _(sync):
                for dst, src in [(k.ident_bf, k.ident_bf_d), (k.ident_f, k.ident_f_d),
                                 (k.ones1, k.ones1_d), (k.r8, k.r8_d),
                                 (k.ones64, k.ones64_d), (k.lnw, k.lnw_fm),
                                 (k.nrmw, k.nrmw_fm), (k.mupf, k.mup_fm), (k.convw, k.conv_fm),
                                 (k.dbias, k.dt_bias), (k.aA, k.a_log), (k.dbc, k.d_bc),
                                 (k.hm, k.hmask)]:
                    sync.dma_start(out=dst[:], in_=src).then_inc(k.ld.h, 16)
                    k.ld.n += 16

            @blk.scalar
            def _(scalar):
                scalar.wait_ge(k.ld.h, k.ld.n)
                scalar.activation(out=k.aA[:], in_=k.aA[:], func=AF.Exp).then_inc(k.ac.h, 1)
                k.ac.n += 1

            @blk.vector
            def _(vector):
                vector.wait_ge(k.ac.h, k.ac.n)
                vector.tensor_scalar(out=k.aA[:], in0=k.aA[:], scalar1=-1.0, scalar2=None,
                                     op0=ALU.mult).then_inc(k.dv.h, 1)
                k.dv.n += 1
        _barrier(k)

        with ExitStack() as scA:
            k.arena = scA.enter_context(nc.sbuf_tensor("arena", [128, 69632], BF16))
            _ln_phase(k, k.x0h, lambda kk, tc: k.xn(kk)[:, tc * 128:(tc + 1) * 128], 17, with_rms=True)
            _barrier(k)
            if "xn" in DBG:
                with nc.Block() as blkx:
                    @blkx.gpsimd
                    def _(g):
                        g.dma_start(out=dbg_outs["xn"], in_=k.arena[:, 0:69632]).then_inc(k.lg.h, 16)
                        k.lg.n += 16
                        g.wait_ge(k.lg.h, k.lg.n)
                _barrier(k)
            if STOP == "P1":
                _dbg_dump(k, dbg_outs)
                return nc, dbg_outs

            _p2_inproj(k)
            _barrier(k)
        if STOP == "P2":
            _dbg_dump(k, dbg_outs)
            return nc, dbg_outs

        with ExitStack() as scP:
            eP = scP.enter_context
            k.bC = eP(nc.sbuf_tensor("bC", [128, TR], BF16))
            k.egl_bf = eP(nc.sbuf_tensor("eglbf", [64, TR], BF16))
            k.r64 = eP(nc.sbuf_tensor("sr64", [64, DM], BF16))
            with nc.Block() as blk0:
                @blk0.sync
                def _(sync):
                    sync.dma_start(out=k.r64[:], in_=k.r64_d).then_inc(k.ld.h, 16)
                    k.ld.n += 16

            with ExitStack() as scB:
                eB = scB.enter_context
                k.dt_t = eB(nc.sbuf_tensor("dtT", [64, TR], F32))
                k.cl_t = eB(nc.sbuf_tensor("clT", [64, TR], F32))
                k.bB = eB(nc.sbuf_tensor("bB", [128, TR], BF16))
                k.bBt = eB(nc.sbuf_tensor("bBt", [128, TR], BF16))
                k.t64g = eB(nc.sbuf_tensor("t64g", [64, 64], F32))
                _barrier(k)
                _p3_dt(k)
                _p3_conv(k)
                if STOP == "P3" or "xtm" in DBG:
                    with nc.Block() as blkx:
                        @blkx.gpsimd
                        def _(g):
                            if "xtm" in DBG:
                                g.dma_start(out=dbg_outs["xtm"], in_=k.xtm_scr).then_inc(k.lg.h, 16)
                                k.lg.n += 16
                            g.wait_ge(k.lg.h, k.lg.n)
                    _barrier(k)
                if STOP == "P3":
                    _dbg_dump(k, dbg_outs)
                    return nc, dbg_outs

                _p4_ssd(k)
                if STOP == "P4" and "H" in DBG:
                    with nc.Block() as blkh:
                        @blkh.gpsimd
                        def _(g):
                            g.dma_start(out=dbg_outs["H"], in_=k.H[:]).then_inc(k.lg.h, 16)
                            k.lg.n += 16
                            g.wait_ge(k.lg.h, k.lg.n)
                    _barrier(k)
                if STOP == "P4":
                    _dbg_dump(k, dbg_outs)
                    return nc, dbg_outs

            k.arena2 = e(nc.sbuf_tensor("arena2", [128, 65536], BF16, side="right"))
            k.yfm = lambda kk: k.arena2[:, kk * TR:(kk + 1) * TR]
            _p5_gate(k)
            if STOP == "P5p":
                _dbg_dump(k, dbg_outs)
                return nc, dbg_outs
            if STOP == "P5":
                _dbg_dump(k, dbg_outs)
                return nc, dbg_outs

        _p6_outproj(k)
        if STOP == "P6":
            _dbg_dump(k, dbg_outs)
            return nc, dbg_outs

        _ln_phase(k, k.x0h[128:TH, :], lambda kk, tc: k.yfm(kk)[:, tc * 128:(tc + 1) * 128],
                  16, with_rms=False, x1_mode=True, tag="b")
        _barrier(k)
        if STOP == "P7a":
            _dbg_dump(k, dbg_outs)
            return nc, dbg_outs

        k.bstb = e(nc.sbuf_tensor("bstb", [128, DM], BF16))
        k.bbb = e(nc.sbuf_tensor("bbb", [1, DM], BF16))
        k.hwb = e(nc.sbuf_tensor("hwb", [128, TR], BF16))
        _p7_router(k)
        if STOP == "P7":
            _dbg_dump(k, dbg_outs)
            return nc, dbg_outs

        _p8_ffn(k)
        _dbg_dump(k, dbg_outs)
    return nc, dbg_outs


def _dbg_dump(k, dbg_outs):
    nc = k.nc
    with nc.Block() as blk:
        @blk.gpsimd
        def _(g):
            for s in (k.ld, k.ls, k.lg, k.dv, k.pe, k.ac):
                if s.n:
                    g.wait_ge(s.h, s.n)
            for name, d in dbg_outs.items():
                src = {"z": k.z_scr, "xbc": k.xbc_scr, "y": k.y_scr,
                       "attn": k.attn_scr, "x1": k.x1_scr}.get(name)
                if src is not None:
                    g.dma_start(out=d, in_=src).then_inc(k.lg.h, 16)
                    k.lg.n += 16
                elif name == "dt":
                    g.dma_start(out=d, in_=k.dt_scr).then_inc(k.lg.h, 16)
                    k.lg.n += 16

            g.wait_ge(k.lg.h, k.lg.n)
            if k.out.n:
                g.wait_ge(k.out.h, k.out.n)


def _host_inputs(inp):
    x0 = np.asarray(inp["L0_attn_hidden_states"])
    ln_w = np.asarray(inp["L0_attn_ln_w"])
    w_in = np.ascontiguousarray(np.asarray(inp["L0_attn_in_proj_w"], np.float32))
    conv_w = np.asarray(inp["L0_attn_conv_w"])
    a_log = np.asarray(inp["L0_attn_A_log"])
    dt_b = np.asarray(inp["L0_attn_dt_bias"])
    Dv = np.asarray(inp["L0_attn_D"])
    norm_w = np.asarray(inp["L0_attn_norm_w"])
    w_op = np.ascontiguousarray(np.asarray(inp["L0_attn_out_proj_w"], np.float32))
    mup = np.asarray(inp["L0_attn_mup_vector"])
    wb = np.asarray(inp["L0_ffn_W_base"])
    bb = np.asarray(inp["L0_ffn_b_base"])
    wr = np.asarray(inp["L0_ffn_W_router"])
    Am = np.asarray(inp["L0_ffn_A"])
    Bm = np.asarray(inp["L0_ffn_B"])

    shared = {
        "w_in": w_in,
        "w_op": w_op,
        "wbT": np.ascontiguousarray(wb.T.astype(np.float32)),
        "wrT": np.ascontiguousarray(wr.T.astype(np.float32)),
        "aT": np.ascontiguousarray(Am.reshape(128, DM).T.astype(np.float32)),
        "bstk": np.ascontiguousarray(Bm.transpose(0, 2, 1).reshape(128, DM).astype(np.float32)),
        "bb": np.ascontiguousarray(bb.reshape(1, DM).astype(np.float32)),
        "conv_fm": np.ascontiguousarray(conv_w.reshape(34, 128, 4).transpose(1, 0, 2).reshape(128, 136).astype(np.float32)),
        "lnw_fm": np.ascontiguousarray(ln_w.reshape(32, 128).T.astype(np.float32)),
        "nrmw_fm": np.ascontiguousarray(norm_w.reshape(32, 128).T.astype(np.float32)),
        "mupz": np.ascontiguousarray(np.broadcast_to(mup[:DM], (128, DM)).astype(np.float32)),
        "mup_fm": np.ascontiguousarray(np.pad(mup[DM:].astype(np.float32), (0, 64)).reshape(35, 128).T),
        "dt_bias": np.ascontiguousarray(dt_b.reshape(64, 1).astype(np.float32)),
        "a_log": np.ascontiguousarray(a_log.reshape(64, 1).astype(np.float32)),
        "d_bc": np.ascontiguousarray(np.broadcast_to(np.asarray(Dv)[None, :], (128, 64)).astype(np.float32)),
        "d_row": np.ascontiguousarray(np.repeat(np.asarray(Dv), 64).reshape(1, DM).astype(np.float32)),
        "ident_bf": np.eye(128, dtype=BF),
        "ident_f": np.eye(128, dtype=np.float32),
        "ones1": np.ones((1, 128), dtype=BF),
        "r64": np.ascontiguousarray(np.repeat(np.eye(64, dtype=np.float32), 64, axis=1).astype(BF)),
        "r8": np.ascontiguousarray(np.repeat(np.eye(8, dtype=np.float32), 16, axis=1).astype(BF)),
        "m4": np.ascontiguousarray(np.tile(np.where(np.arange(128)[:, None] <= np.arange(128)[None, :], 0.0, -1e30).astype(np.float32), (1, 4))),
        "ones64": np.ones((64, 128), dtype=np.float32),
    }
    maps = []
    for c in range(8):
        b, h = c // 2, c % 2
        if h == 0:
            xh = np.concatenate([np.zeros((128, DM), np.float32), np.asarray(x0[b, 0:TR], np.float32)], axis=0)
        else:
            xh = np.ascontiguousarray(np.asarray(x0[b, TR - 128:2 * TR], np.float32))
        m = dict(shared)
        m["x0h"] = xh
        m["hmask"] = np.full((128, 1), float(h), np.float32)
        maps.append(m)
    return maps


def kernel(**inputs):
    maps = _host_inputs(inputs)
    nc, _ = build_nc()
    res = run_bass_kernel_spmd(nc, maps, core_ids=list(range(8)))
    out = np.empty((4, 4096, DM), np.float32)
    for c in range(8):
        b, h = c // 2, c % 2
        out[b, h * TR:(h + 1) * TR] = np.asarray(res.results[c]["out"])
    return out
